# version 01 (final)
# baseline (speedup 1.0000x reference)
import math
import os
import numpy as np
import concourse.bass as bass
import concourse.mybir as mybir
from concourse.bass_utils import run_bass_kernel_spmd
from concourse.bass import IndirectOffsetOnAxis

F32 = mybir.dt.float32
BF16 = mybir.dt.bfloat16
U32 = mybir.dt.uint32
AF = mybir.ActivationFunctionType
ALU = mybir.AluOpType
AX = mybir.AxisListType

NCORE = 8
T = 16384
TLOC = T // NCORE
D = 1024
DCH = D // 128
TS = 512
NSL = TLOC // TS
NTT = TLOC // 128
RG8 = [list(range(8))]
NEG = -30000.0


class Sem:
    def __init__(self, nc, name):
        self.h = nc.alloc_semaphore(name=name)
        self.n = 0

    def inc(self, instr, k=1):
        instr.then_inc(self.h, k)
        self.n += k
        return self.n

    def wait(self, eng, n=None):
        eng.wait_ge(self.h, self.n if n is None else n)


class VS:
    _PASS = {"wait_ge"}

    def __init__(self, b, eng, sem):
        self._b = b
        self._eng = eng
        self._sem = sem

    def __getattr__(self, name):
        fn = getattr(self._eng, name)
        if name in VS._PASS or not callable(fn):
            return fn
        def wrap(*a, **k):
            i = fn(*a, **k)
            self._sem.inc(i)
            self._eng.wait_ge(self._sem.h, self._sem.n)
            return i
        return wrap


class B:
    def __init__(self, tag):
        self.nc = bass.Bass("TRN2", target_bir_lowering=False, debug=False)
        nc = self.nc
        self.te, self.ve, self.se, self.ge, self.ye = (
            nc.tensor, nc.vector, nc.scalar, nc.gpsimd, nc.sync)
        self.tag = tag
        self.providers = {}
        self._namecnt = 0
        self.sLD = self.sem("sLD")
        self.sGD = self.sem("sGD")
        self.sT = self.sem("sT")
        self.sV = self.sem("sV")
        self.sS = self.sem("sS")
        self.sG = self.sem("sG")
        self.PS = [nc.alloc_psum_tensor(f"psb{i}", [128, 512], F32) for i in range(8)]
        self.XN = self.sb("XN", [128, DCH * TLOC], BF16)
        self.P1 = self.sb("P1", [128, 3 * DCH * TLOC], BF16)
        self.WW = self.sb("WW", [128, DCH * 1024], BF16)
        self.GF = self.sb("GF", [128, 4096], F32)
        self.SMALL = self.sb("SMALL", [128, 2048], F32)
        self._cvals = []
        self.CC = self.sb("CC", [128, 16], F32)
        self.vs = VS(self, self.ve, self.sV)
        self.ss = VS(self, self.se, self.sS)
        self.DUMMY = self.sb("DUMMY", [1, 2], F32)

    def phase_barrier(self):
        for eng in (self.ge, self.ye, self.te, self.ve, self.se):
            self.sT.wait(eng)
            self.sV.wait(eng)
            self.sS.wait(eng)
            self.sG.wait(eng)

    def act_accum(self, out, in_, func, bias, accum_out):
        self.se.activation(out, in_, func, bias=bias, accum_out=accum_out)
        self.ss.copy(self.DUMMY[:, 0:1], self.DUMMY[:, 1:2])

    def sem(self, name=None):
        self._namecnt += 1
        return Sem(self.nc, name or f"sem{self._namecnt}")

    def name(self, p):
        self._namecnt += 1
        return f"{p}_{self._namecnt}"

    def inp(self, name, shape, dtype, provider):
        h = self.nc.dram_tensor(name, list(shape), dtype, kind="ExternalInput")
        self.providers[name] = provider
        return h.ap()

    def weight(self, key, shape, transform=None, name=None):
        def prov(c, inputs, state, key=key, transform=transform):
            w = np.asarray(inputs[key])
            if transform is not None:
                w = transform(w)
            return np.ascontiguousarray(w, dtype=np.float32)
        return self.inp(name or ("w_" + key), shape, F32, prov)

    def const(self, name, arr_fn):
        a0 = np.asarray(arr_fn(0))
        dt = {np.dtype(np.float32): F32, np.dtype(np.uint32): U32}[a0.dtype]
        return self.inp(name, list(a0.shape), dt,
                        lambda c, inputs, state, f=arr_fn: np.ascontiguousarray(f(c)))

    def xin(self, name="XIN"):
        return self.inp(name, [D, TLOC], F32, lambda c, inputs, state: state[c])

    def out(self, name, shape, dtype):
        return self.nc.dram_tensor(name, list(shape), dtype, kind="ExternalOutput").ap()

    def dram(self, name, shape, dtype, shared=False):
        return self.nc.dram_tensor(self.name(name), list(shape), dtype, kind="Internal",
                                   addr_space="Shared" if shared else "Local").ap()

    def sb(self, name, shape, dtype):
        return self.nc.alloc_sbuf_tensor(self.name(name), list(shape), dtype)

    def ccol(self, v):
        v = float(v)
        if v not in self._cvals:
            assert len(self._cvals) < 16
            self._cvals.append(v)
        return self.CC[:, self._cvals.index(v):self._cvals.index(v) + 1]

    def allgather(self, in_ap, out_ap):
        i = self.ge.collective_compute("AllGather", ALU.bypass, replica_groups=RG8,
                                       ins=[in_ap], outs=[out_ap])
        return self.sG.inc(i, 1)

    def allreduce(self, in_ap, out_ap, op=ALU.add):
        i = self.ge.collective_compute("AllReduce", op, replica_groups=RG8,
                                       ins=[in_ap], outs=[out_ap])
        return self.sG.inc(i, 1)


def std_consts(b):
    C = {}
    C["ones128"] = b.sb("c_ones", [128, 128], F32)
    C["ident"] = b.sb("c_ident", [128, 128], F32)
    ONES_IN = b.const("c_ones_in", lambda c: np.ones((128, 128), np.float32))
    ID_IN = b.const("c_ident_in", lambda c: np.eye(128, dtype=np.float32))
    C["identb"] = b.sb("c_identb", [128, 128], BF16)
    C["onesb"] = b.sb("c_onesb", [128, 128], BF16)
    def ccprov(c, inputs, state):
        a = np.zeros((128, 16), np.float32)
        for j, v in enumerate(b._cvals):
            a[:, j] = v
        return a
    CC_IN = b.inp("c_cc_in", [128, 16], F32, ccprov)
    i = b.ge.dma_start(out=C["ones128"][:, :], in_=ONES_IN[:, :]); b.sGD.inc(i, 16)
    i = b.ge.dma_start(out=C["ident"][:, :], in_=ID_IN[:, :]); b.sGD.inc(i, 16)
    i = b.ge.dma_start(out=C["identb"][:, :], in_=ID_IN[:, :]); b.sGD.inc(i, 16)
    i = b.ge.dma_start(out=C["onesb"][:, :], in_=ONES_IN[:, :]); b.sGD.inc(i, 16)
    i = b.ge.dma_start(out=b.CC[:, :], in_=CC_IN[:, :]); b.sGD.inc(i, 16)
    b.sGD.wait(b.te); b.sGD.wait(b.ve); b.sGD.wait(b.se)
    return C



def emit_ln(b, XIN, XN, C):
    v, s, t, y = b.vs, b.ss, b.te, b.ye
    eps = 1e-5
    ones = C["ones128"]
    st_ = b.P1[:, 12288:28672].bitcast(F32)
    sumx = st_[0:1, 0:TLOC]
    sumsq = st_[0:1, TLOC:2 * TLOC]
    mr = st_[0:1, 2 * TLOC:3 * TLOC]
    rstd = st_[0:1, 3 * TLOC:4 * TLOC]
    rstdb = b.GF[:, 0:TLOC]
    mrb = b.GF[:, TLOC:2 * TLOC]
    xch = [b.P1[:, 0:4096].bitcast(F32), b.P1[:, 4096:8192].bitcast(F32)]
    sq = b.P1[:, 8192:12288].bitcast(F32)

    y.wait_ge(b.sT.h, b.sT.n)
    y.wait_ge(b.sV.h, b.sV.n)
    y.wait_ge(b.sS.h, b.sS.n)
    for k in range(DCH):
        buf = xch[k % 2]
        if k >= 2:
            y.wait_ge(b.sT.h, b.sT.n)
        i = y.dma_start(out=buf[:, :], in_=XIN[k * 128:(k + 1) * 128, :])
        b.sLD.inc(i, 16)
        t.wait_ge(b.sLD.h, b.sLD.n)
        s.wait_ge(b.sLD.h, b.sLD.n)
        for si in range(NSL):
            i = t.matmul(b.PS[si][0:1, 0:TS],
                         ones[:, 0:1], buf[:, si * TS:(si + 1) * TS],
                         start=(k == 0), stop=(k == DCH - 1))
        if k >= 1:
            s.wait_ge(b.sT.h, b.sT.n)
        i = s.activation(sq[:, :], buf[:, :], AF.Square)
        t.wait_ge(b.sS.h, b.sS.n)
        for si in range(NSL):
            i = t.matmul(b.PS[4 + si][0:1, 0:TS],
                         ones[:, 0:1], sq[:, si * TS:(si + 1) * TS],
                         start=(k == 0), stop=(k == DCH - 1))
        b.sT.inc(i)
    v.wait_ge(b.sT.h, b.sT.n)
    for si in range(NSL):
        v.tensor_copy(sumx[:, si * TS:(si + 1) * TS], b.PS[si][0:1, 0:TS])
        i = v.tensor_copy(sumsq[:, si * TS:(si + 1) * TS], b.PS[4 + si][0:1, 0:TS])
    v.tensor_scalar_mul(sumx, sumx, 1.0 / D)
    v.tensor_scalar_mul(sumsq, sumsq, 1.0 / D)
    v.tensor_copy(mr, sumx)
    v.tensor_tensor(rstd, sumx, sumx, ALU.mult)
    i = v.tensor_sub(rstd, sumsq, rstd)
    s.wait_ge(b.sV.h, b.sV.n)
    i = s.activation(rstd, rstd, AF.Sqrt, bias=b.ccol(eps)[0:1, 0:1])
    v.wait_ge(b.sS.h, b.sS.n)
    i = v.reciprocal(rstd, rstd)
    i = v.tensor_mul(mr, mr, rstd)

    t.wait_ge(b.sV.h, b.sV.n)
    for si in range(NSL):
        t.matmul(b.PS[4][:, 0:TS], ones[0:1, 0:128], rstd[:, si * TS:(si + 1) * TS],
                 start=True, stop=True)
        i = t.matmul(b.PS[5][:, 0:TS], ones[0:1, 0:128], mr[:, si * TS:(si + 1) * TS],
                     start=True, stop=True)
        b.sT.inc(i)
        v.wait_ge(b.sT.h, b.sT.n)
        v.tensor_copy(rstdb[:, si * TS:(si + 1) * TS], b.PS[4][:, 0:TS])
        i = v.tensor_copy(mrb[:, si * TS:(si + 1) * TS], b.PS[5][:, 0:TS])
        t.wait_ge(b.sV.h, b.sV.n)

    if os.environ.get("KDEV"):
        STATS = b.out("STATS", [1, 4 * TLOC], F32)
        b.ye.wait_ge(b.sV.h, b.sV.n)
        i = b.ye.dma_start(out=STATS[:, :], in_=st_[0:1, 0:4 * TLOC])
        b.sLD.inc(i, 16)
    y.wait_ge(b.sT.h, b.sT.n)
    y.wait_ge(b.sS.h, b.sS.n)
    for k in range(DCH):
        buf = xch[k % 2]
        if k >= 2:
            y.wait_ge(b.sV.h, b.sV.n)
        i = y.dma_start(out=buf[:, :], in_=XIN[k * 128:(k + 1) * 128, :])
        b.sLD.inc(i, 16)
        v.wait_ge(b.sLD.h, b.sLD.n)
        osl = XN[:, k * TLOC:(k + 1) * TLOC]
        i = v.scalar_tensor_tensor(osl, buf[:, :], 1.0, rstdb, ALU.mult, ALU.mult)
        i = v.tensor_sub(osl, osl, mrb)


def load_weight_slab(b, dst_sb, W_hbm, nch, width, row0=0, col0=None):
    for k in range(nch):
        if col0 is None:
            src = W_hbm[row0 + k * 128:row0 + (k + 1) * 128, :]
        else:
            src = W_hbm[row0 + k * 128:row0 + (k + 1) * 128, col0:col0 + width]
        i = b.ge.dma_start(out=dst_sb[:, k * width:(k + 1) * width], in_=src)
        b.sGD.inc(i, 16)


def proj_fm(b, OUT, out_off, XN, wslab, F, bias_col=None, act=None,
            psA=6, psB=7, nk=DCH, wwidth=None, out_stride=TLOC, scale=None):
    t, s = b.te, b.se
    wwidth = wwidth or F
    nf = F // 128
    for f in range(nf):
        for si in range(NSL):
            ps = b.PS[psA if (f * NSL + si) % 2 == 0 else psB][:, 0:TS]
            t.wait_ge(b.sS.h, b.sS.n)
            for k in range(nk):
                i = t.matmul(ps, wslab[:, k * wwidth + f * 128:k * wwidth + f * 128 + 128],
                             XN[:, k * TLOC + si * TS:k * TLOC + (si + 1) * TS],
                             start=(k == 0), stop=(k == nk - 1))
            b.sT.inc(i)
            s.wait_ge(b.sT.h, b.sT.n)
            dst = OUT[:, out_off + f * out_stride + si * TS: out_off + f * out_stride + (si + 1) * TS]
            kw = {}
            if bias_col is not None:
                kw["bias"] = bias_col[:, f:f + 1]
            if scale is not None:
                kw["scale"] = scale
            i = s.activation(dst, ps, (act or (AF.Identity if kw else AF.Copy)), **kw)
            b.sS.inc(i)


def emit_final_combine(b, XIN, XOUT, branches):
    v, y = b.vs, b.ye
    a = b.GF[:, 0:2048]
    bb = b.SMALL[:, 0:2048]
    for k in range(DCH):
        if k >= 1:
            y.wait_ge(b.sV.h, b.sV.n)
        i = y.dma_start(out=a[:, :], in_=XIN[k * 128:(k + 1) * 128, :])
        b.sLD.inc(i, 16)
        for br in branches:
            i = y.dma_start(out=bb[:, :], in_=br[k * 128:(k + 1) * 128, :])
            b.sLD.inc(i, 16)
            v.wait_ge(b.sLD.h, b.sLD.n)
            i = v.tensor_add(a[:, :], a[:, :], bb[:, :])
            y.wait_ge(b.sV.h, b.sV.n)
        i = y.dma_start(out=XOUT[k * 128:(k + 1) * 128, :], in_=a[:, :])
        b.sLD.inc(i, 16)
    b.sLD.wait(b.ye)



def l1_masks(core):
    m = np.full((3, 128, 384), NEG, np.float32)
    for vi in range(3):
        m[vi, 0:64, 64:256] = 0.0
        m[vi, 64:128, 128:320] = 0.0
    if core == 0:
        pass
    if core == 7:
        m[2, 64:128, 256:320] = NEG
    m = np.transpose(m, (1, 0, 2)).reshape(128, 3 * 384)
    return np.ascontiguousarray(m)


def emit_l1(b, C):
    v, s, t, g, y = b.vs, b.ss, b.te, b.ge, b.ye
    XIN = b.xin()
    XOUT = b.out("XOUT", [D, TLOC], F32)
    DEV = bool(os.environ.get("KDEV"))
    if DEV:
        ATTH = b.out("ATTH", [D, TLOC], F32)
        FFNH = b.out("FFNH", [D, TLOC], F32)
        XNO = b.out("XNO", [128, DCH * TLOC], F32)
    else:
        ATTH = b.dram("l1att", [D, TLOC], F32)
        FFNH = b.dram("l1ffn", [D, TLOC], F32)

    Wq = b.weight("L1_attn_Wq", [D, D])
    Wk = b.weight("L1_attn_Wk", [D, D])
    Wv = b.weight("L1_attn_Wv", [D, D])
    Wo = b.weight("L1_attn_Wo", [D, D])
    bq = b.weight("L1_attn_bq", [128, DCH], transform=lambda w: w.reshape(DCH, 128).T)
    bk = b.weight("L1_attn_bk", [128, DCH], transform=lambda w: w.reshape(DCH, 128).T)
    bo = b.weight("L1_attn_bo", [128, DCH], transform=lambda w: w.reshape(DCH, 128).T)
    bv_row = b.weight("L1_attn_bv", [1, D], transform=lambda w: w.reshape(1, D))
    MASKS = b.const("l1masks", l1_masks)
    IDXP = b.const("l1idxp", lambda c: (128 * ((c - 1) % 8) + np.arange(128, dtype=np.uint32)).reshape(128, 1))
    IDXN = b.const("l1idxn", lambda c: (128 * ((c + 1) % 8) + np.arange(128, dtype=np.uint32)).reshape(128, 1))
    IDXP64 = b.const("l1idxp64", lambda c: (128 * ((c - 1) % 8) + 64 + np.arange(64, dtype=np.uint32)).reshape(64, 1))
    IDXN64 = b.const("l1idxn64", lambda c: (128 * ((c + 1) % 8) + np.arange(64, dtype=np.uint32)).reshape(64, 1))
    M0 = b.const("l1m0", lambda c: np.full((128, 2), [1.0 if c == 0 else 0.0, 0.0 if c == 0 else 1.0], np.float32))

    QT = b.P1[:, 0:DCH * TLOC]
    KT = b.P1[:, DCH * TLOC:2 * DCH * TLOC]
    VV = b.P1[:, 2 * DCH * TLOC:3 * DCH * TLOC]
    KT3 = KT.rearrange("p (c t) -> p c t", c=DCH)
    QT3 = QT.rearrange("p (c t) -> p c t", c=DCH)

    emit_ln(b, XIN, b.XN, C)
    if DEV:
        xnf = b.GF[:, 0:TLOC]
        for k in range(DCH):
            b.vs.wait_ge(b.sLD.h, b.sLD.n)
            i = b.vs.tensor_copy(xnf, b.XN[:, k * TLOC:(k + 1) * TLOC])
            b.ye.wait_ge(b.sV.h, b.sV.n)
            i = b.ye.dma_start(out=XNO[:, k * TLOC:(k + 1) * TLOC], in_=xnf)
            b.sLD.inc(i, 16)
        b.sLD.wait(b.ye)

    masksb = b.sb("l1masksb", [128, 3 * 384], F32)
    i = g.dma_start(out=masksb[:, :], in_=MASKS[:, :]); b.sGD.inc(i, 16)
    bqs = b.sb("l1bq", [128, 3 * DCH], F32)
    i = g.dma_start(out=bqs[:, 0:DCH], in_=bq[:, :]); b.sGD.inc(i, 16)
    i = g.dma_start(out=bqs[:, DCH:2 * DCH], in_=bk[:, :]); b.sGD.inc(i, 16)
    i = g.dma_start(out=bqs[:, 2 * DCH:3 * DCH], in_=bo[:, :]); b.sGD.inc(i, 16)
    bvr = b.SMALL[0:1, 0:D]
    i = g.dma_start(out=bvr, in_=bv_row[:, :]); b.sGD.inc(i, 16)
    idxs = b.sb("l1idx", [128, 4], U32)
    i = g.dma_start(out=idxs[:, 0:1], in_=IDXP[:, :]); b.sGD.inc(i, 16)
    i = g.dma_start(out=idxs[:, 1:2], in_=IDXN[:, :]); b.sGD.inc(i, 16)
    i = g.dma_start(out=idxs[0:64, 2:3], in_=IDXP64[:, :]); b.sGD.inc(i, 16)
    i = g.dma_start(out=idxs[0:64, 3:4], in_=IDXN64[:, :]); b.sGD.inc(i, 16)
    m0s = b.sb("l1m0s", [128, 2], F32)
    i = g.dma_start(out=m0s[:, :], in_=M0[:, :]); b.sGD.inc(i, 16)

    v.wait_ge(b.sGD.h, b.sGD.n)
    v.tensor_scalar_mul(bqs[:, 0:DCH], bqs[:, 0:DCH], 0.125)
    b.sV.wait(t)
    b.sV.wait(s)
    load_weight_slab(b, b.WW, Wq, DCH, D)
    b.sGD.wait(t)
    proj_fm(b, QT, 0, b.XN, b.WW, D, bias_col=bqs[:, 0:DCH], scale=0.125)
    b.sT.wait(g); b.sS.wait(g)
    load_weight_slab(b, b.WW, Wk, DCH, D)
    b.sGD.wait(t)
    proj_fm(b, KT, 0, b.XN, b.WW, D, bias_col=bqs[:, DCH:2 * DCH])
    b.sT.wait(g); b.sS.wait(g)
    load_weight_slab(b, b.WW, Wv, DCH, D)
    b.sGD.wait(t)
    for tt in range(NTT):
        for vc in range(2):
            ps = b.PS[6 if vc == 0 else 7][:, 0:TS]
            t.wait_ge(b.sS.h, b.sS.n)
            for k in range(DCH):
                t.matmul(ps, b.XN[:, k * TLOC + tt * 128:k * TLOC + tt * 128 + 128],
                         b.WW[:, k * D + vc * TS:k * D + (vc + 1) * TS],
                         start=(k == 0), stop=False)
            i = t.matmul(ps, C["ones128"][0:1, 0:128], bvr[:, vc * TS:(vc + 1) * TS],
                         start=False, stop=True)
            b.sT.inc(i)
            s.wait_ge(b.sT.h, b.sT.n)
            i = s.activation(VV[:, tt * D + vc * TS:tt * D + (vc + 1) * TS], ps, AF.Copy)

    if DEV:
        QTO = b.out("QTO", [128, DCH * TLOC], BF16)
        KTO = b.out("KTO", [128, DCH * TLOC], BF16)
        VVO = b.out("VVO", [128, DCH * TLOC], BF16)
        b.ye.wait_ge(b.sS.h, b.sS.n)
        i = b.ye.dma_start(out=QTO[:, :], in_=QT); b.sLD.inc(i, 16)
        i = b.ye.dma_start(out=KTO[:, :], in_=KT); b.sLD.inc(i, 16)
        i = b.ye.dma_start(out=VVO[:, :], in_=VV); b.sLD.inc(i, 16)
        b.sLD.wait(b.ye)

    PUBQK = b.dram("l1pubqk", [128, 24 * 64], BF16)
    PUBV = b.dram("l1pubv", [128, D], BF16)
    AGQK = b.dram("l1agqk", [NCORE * 128, 24 * 64], BF16, shared=True)
    AGV = b.dram("l1agv", [NCORE * 128, D], BF16, shared=True)
    g.wait_ge(b.sS.h, b.sS.n)
    i = g.dma_start(out=PUBQK[:, 0:512], in_=KT3[:, :, 0:64]); b.sGD.inc(i, 16)
    i = g.dma_start(out=PUBQK[:, 512:1024], in_=KT3[:, :, TLOC - 64:TLOC]); b.sGD.inc(i, 16)
    i = g.dma_start(out=PUBQK[:, 1024:1536], in_=QT3[:, :, 0:64]); b.sGD.inc(i, 16)
    i = g.dma_start(out=PUBV[0:64, :], in_=VV[0:64, 0:D]); b.sGD.inc(i, 16)
    i = g.dma_start(out=PUBV[64:128, :], in_=VV[64:128, (NTT - 1) * D:NTT * D]); b.sGD.inc(i, 16)
    g.wait_ge(b.sGD.h, b.sGD.n)
    b.allgather(PUBQK[:, :], AGQK[:, :])
    b.allgather(PUBV[:, :], AGV[:, :])
    g.wait_ge(b.sG.h, b.sG.n)

    KBF = b.sb("l1kbf", [128, DCH * 384], BF16)
    KBL = b.sb("l1kbl", [128, DCH * 384], BF16)
    VB0 = b.sb("l1vb0", [128, D], BF16)
    VBL = b.sb("l1vbl", [128, D], BF16)
    QT0 = b.sb("l1qt0", [128, 512], BF16)
    g.memset(KBF[:, :], 0.0)
    g.memset(KBL[:, :], 0.0)
    g.memset(VB0[:, :], 0.0)
    i = g.memset(VBL[:, :], 0.0)
    b.sG.inc(i)
    g.wait_ge(b.sG.h, b.sG.n)
    KBF3 = KBF.rearrange("p (c t) -> p c t", c=DCH)
    KBL3 = KBL.rearrange("p (c t) -> p c t", c=DCH)
    for ch in range(DCH):
        i = g.indirect_dma_start(KBF3[:, ch, 64:128], None, AGQK[:, :],
                                 IndirectOffsetOnAxis(ap=idxs[:, 0:1], axis=0),
                                 element_offset=512 + ch * 64)
        b.sGD.inc(i, 16)
        i = g.indirect_dma_start(KBL3[:, ch, 256:320], None, AGQK[:, :],
                                 IndirectOffsetOnAxis(ap=idxs[:, 1:2], axis=0),
                                 element_offset=0 + ch * 64)
        b.sGD.inc(i, 16)
    i = g.indirect_dma_start(VB0[64:128, :], None, AGV[:, :],
                             IndirectOffsetOnAxis(ap=idxs[0:64, 2:3], axis=0),
                             element_offset=0)
    b.sGD.inc(i, 16)
    i = g.indirect_dma_start(VBL[0:64, :], None, AGV[:, :],
                             IndirectOffsetOnAxis(ap=idxs[0:64, 3:4], axis=0),
                             element_offset=0)
    b.sGD.inc(i, 16)
    i = g.dma_start(out=QT0[:, :], in_=AGQK[0:128, 1024:1536])
    b.sGD.inc(i, 16)
    v.wait_ge(b.sGD.h, b.sGD.n)
    i = v.tensor_copy(KBF3[:, :, 128:384], KT3[:, :, 0:256])
    i = v.tensor_copy(KBL3[:, :, 0:256], KT3[:, :, TLOC - 256:TLOC])

    PTB = b.sb("l1ptb", [128, 384], BF16)
    stt_ = b.sb("l1stats", [128, 16], F32)
    scrE = b.sb("l1scre", [128, 384], BF16)
    nmrow = b.sb("l1nmrow", [1, 128], F32)
    ones = C["ones128"]
    ident = C["ident"]
    v.wait_ge(b.sV.h, b.sV.n)
    s.wait_ge(b.sV.h, b.sV.n)
    t.wait_ge(b.sV.h, b.sV.n)

    for qt in range(NTT):
        if qt == 0:
            kparts = [(KBF3, 0), (KT3, 0), (KT3, 128)]
            vparts = [VB0, VV[:, 0:D], VV[:, D:2 * D]]
            mvar = 0
        elif qt == NTT - 1:
            kparts = [(KT3, (qt - 1) * 128), (KT3, qt * 128), (KBL3, 256)]
            vparts = [VV[:, (qt - 1) * D:qt * D], VV[:, qt * D:(qt + 1) * D], VBL]
            mvar = 2
        else:
            kparts = [(KT3, (qt - 1) * 128), (KT3, qt * 128), (KT3, (qt + 1) * 128)]
            vparts = [VV[:, (qt - 1) * D:qt * D], VV[:, qt * D:(qt + 1) * D],
                      VV[:, (qt + 1) * D:(qt + 2) * D]]
            mvar = 1
        msk = masksb[:, mvar * 384:(mvar + 1) * 384]
        for h in range(16):
            ch, po = h // 2, (h % 2) * 64
            qsl = QT3[po:po + 64, ch, qt * 128:(qt + 1) * 128]
            pS = b.PS[0][:, 0:384]
            t.wait_ge(b.sV.h, b.sV.n)
            t.wait_ge(b.sS.h, b.sS.n)
            for j in range(3):
                kt3, koff = kparts[j]
                i = t.matmul(pS[:, j * 128:(j + 1) * 128], qsl,
                             kt3[po:po + 64, ch, koff:koff + 128],
                             start=True, stop=True)
            b.sT.inc(i)
            v.wait_ge(b.sT.h, b.sT.n)
            i = v.tensor_tensor(pS, pS, msk, ALU.add)
            mcol = stt_[:, 0:1]
            i = v.tensor_reduce(mcol, pS, axis=AX.X, op=ALU.max)
            i = v.tensor_scalar_mul(stt_[:, 1:2], mcol, -1.0)
            s.wait_ge(b.sV.h, b.sV.n)
            b.act_accum(scrE[:, 0:384], pS, AF.Exp, bias=stt_[:, 1:2],
                        accum_out=stt_[:, 2:3])
            v.wait_ge(b.sS.h, b.sS.n)
            i = v.tensor_scalar_max(stt_[:, 2:3], stt_[:, 2:3], 1e-30)
            s.wait_ge(b.sV.h, b.sV.n)
            i = s.activation(stt_[:, 3:4], stt_[:, 2:3], AF.Ln)
            v.wait_ge(b.sS.h, b.sS.n)
            i = v.tensor_sub(stt_[:, 4:5], stt_[:, 1:2], stt_[:, 3:4])
            t.wait_ge(b.sV.h, b.sV.n)
            pR = b.PS[1][0:1, 0:128]
            i = t.matmul(pR, stt_[:, 4:5], ident[:, :], start=True, stop=True)
            b.sT.inc(i)
            v.wait_ge(b.sT.h, b.sT.n)
            i = v.tensor_copy(nmrow[:, :], pR)
            t.wait_ge(b.sV.h, b.sV.n)
            pO = b.PS[5][0:64, 0:128]
            for j in range(3):
                kt3, koff = kparts[j]
                pStj = b.PS[2 + j][:, 0:128]
                t.wait_ge(b.sS.h, b.sS.n)
                t.matmul(pStj, kt3[po:po + 64, ch, koff:koff + 128], qsl,
                         start=True, stop=False)
                t.matmul(pStj, ones[0:1, 0:128], nmrow[:, :], start=False, stop=False)
                i = t.matmul(pStj, msk[:, j * 128:(j + 1) * 128], ident[:, :],
                             start=False, stop=True)
                b.sT.inc(i)
                s.wait_ge(b.sT.h, b.sT.n)
                i = s.activation(PTB[:, j * 128:(j + 1) * 128], pStj, AF.Exp)
            t.wait_ge(b.sS.h, b.sS.n)
            for j in range(3):
                i = t.matmul(pO, vparts[j][:, 64 * h:64 * h + 64],
                             PTB[:, j * 128:(j + 1) * 128],
                             start=(j == 0), stop=(j == 2))
            b.sT.inc(i)
            v.wait_ge(b.sT.h, b.sT.n)
            i = v.tensor_copy(QT3[po:po + 64, ch, qt * 128:(qt + 1) * 128], pO)

    PACK0 = b.dram("l1pack0", [64, 16 * 66], F32)
    AG0 = b.dram("l1ag0", [NCORE * 64, 16 * 66], F32, shared=True)
    pk = b.GF[0:64, 0:16 * 66]
    b0s = b.sb("l1b0s", [64, 8], F32)
    scr64 = b.sb("l1scr64", [64, 512], BF16)
    nmrow0 = b.sb("l1nmrow0", [1, 64], F32)
    PT0 = b.sb("l1pt0", [128, 64], BF16)
    for h in range(16):
        ch, po = h // 2, (h % 2) * 64
        q0 = QT0.rearrange("p (c t) -> p c t", c=DCH)[po:po + 64, ch, :]
        for kc in range(NSL):
            pS = b.PS[0][0:64, 0:TS]
            t.wait_ge(b.sS.h, b.sS.n)
            t.wait_ge(b.sV.h, b.sV.n)
            i = t.matmul(pS, q0, KT3[po:po + 64, ch, kc * TS:(kc + 1) * TS],
                         start=True, stop=True)
            b.sT.inc(i)
            v.wait_ge(b.sT.h, b.sT.n)
            mc = b0s[:, 2:3]
            i = v.tensor_reduce(mc, pS, axis=AX.X, op=ALU.max)
            if kc == 0:
                i = v.tensor_copy(b0s[:, 0:1], mc)
                i = v.memset(b0s[:, 1:2], 0.0)
            else:
                i = v.tensor_copy(b0s[:, 3:4], b0s[:, 0:1])
                i = v.tensor_max(b0s[:, 0:1], b0s[:, 0:1], mc)
            i = v.tensor_scalar_mul(b0s[:, 5:6], b0s[:, 0:1], -1.0)
            if kc > 0:
                s.wait_ge(b.sV.h, b.sV.n)
                i = s.activation(b0s[:, 4:5], b0s[:, 3:4], AF.Exp, bias=b0s[:, 5:6])
                v.wait_ge(b.sS.h, b.sS.n)
                i = v.tensor_mul(b0s[:, 1:2], b0s[:, 1:2], b0s[:, 4:5])
            s.wait_ge(b.sV.h, b.sV.n)
            b.act_accum(scr64[:, 0:TS], pS, AF.Exp, bias=b0s[:, 5:6],
                        accum_out=b0s[:, 6:7])
            v.wait_ge(b.sS.h, b.sS.n)
            i = v.tensor_add(b0s[:, 1:2], b0s[:, 1:2], b0s[:, 6:7])
        t.wait_ge(b.sV.h, b.sV.n)
        pR = b.PS[1][0:1, 0:64]
        i = t.matmul(pR, b0s[:, 5:6], ident[0:64, 0:64], start=True, stop=True)
        b.sT.inc(i)
        v.wait_ge(b.sT.h, b.sT.n)
        i = v.tensor_copy(nmrow0[:, :], pR)
        t.wait_ge(b.sV.h, b.sV.n)
        pO = b.PS[5][0:64, 0:64]
        for kc in range(NTT):
            pStj = b.PS[2 + (kc % 2)][:, 0:64]
            t.wait_ge(b.sS.h, b.sS.n)
            t.matmul(pStj, KT3[po:po + 64, ch, kc * 128:(kc + 1) * 128], q0,
                     start=True, stop=False)
            i = t.matmul(pStj, ones[0:1, 0:128], nmrow0[:, :], start=False, stop=True)
            b.sT.inc(i)
            s.wait_ge(b.sT.h, b.sT.n)
            i = s.activation(PT0[:, :], pStj, AF.Exp)
            t.wait_ge(b.sS.h, b.sS.n)
            i = t.matmul(pO, PT0[:, :], VV[:, kc * D + 64 * h:kc * D + 64 * h + 64],
                         start=(kc == 0), stop=(kc == NTT - 1))
            b.sT.inc(i)
        v.wait_ge(b.sT.h, b.sT.n)
        i = v.tensor_copy(pk[:, h * 66:h * 66 + 64], pO)
        i = v.tensor_copy(pk[:, h * 66 + 64:h * 66 + 65], b0s[:, 0:1])
        i = v.tensor_copy(pk[:, h * 66 + 65:h * 66 + 66], b0s[:, 1:2])
    g.wait_ge(b.sV.h, b.sV.n)
    i = g.dma_start(out=PACK0[:, :], in_=pk[:, :])
    b.sGD.inc(i, 16)
    g.wait_ge(b.sGD.h, b.sGD.n)
    b.allgather(PACK0[:, :], AG0[:, :])
    g.wait_ge(b.sG.h, b.sG.n)

    if DEV:
        AG0O = b.out("AG0O", [NCORE * 64, 16 * 66], F32)
        g.wait_ge(b.sG.h, b.sG.n)
        i = g.dma_start(out=AG0O[:, :], in_=AG0[:, :]); b.sGD.inc(i, 16)
    cmb = b.GF[0:64, 1056:1056 + 8 * 66]
    acc0 = b.GF[0:64, 1600:1600 + 72]
    acc0b = b.GF[:, 1700:1764]
    y.wait_ge(b.sG.h, b.sG.n)
    for h in range(16):
        y.wait_ge(b.sV.h, b.sV.n)
        for r in range(NCORE):
            i = y.dma_start(out=cmb[:, r * 66:(r + 1) * 66],
                            in_=AG0[64 * r:64 * (r + 1), h * 66:(h + 1) * 66])
            b.sLD.inc(i, 16)
        v.wait_ge(b.sLD.h, b.sLD.n)
        M = acc0[:, 64:65]
        i = v.tensor_copy(M, cmb[:, 64:65])
        for r in range(1, NCORE):
            i = v.tensor_max(M, M, cmb[:, r * 66 + 64:r * 66 + 65])
        i = v.tensor_scalar_mul(acc0[:, 65:66], M, -1.0)
        i = v.memset(acc0[:, 0:64], 0.0)
        i = v.memset(acc0[:, 66:67], 0.0)
        for r in range(NCORE):
            s.wait_ge(b.sV.h, b.sV.n)
            i = s.activation(acc0[:, 67:68], cmb[:, r * 66 + 64:r * 66 + 65],
                             AF.Exp, bias=acc0[:, 65:66])
            v.wait_ge(b.sS.h, b.sS.n)
            i = v.scalar_tensor_tensor(acc0[:, 0:64], cmb[:, r * 66:r * 66 + 64],
                                       acc0[:, 67:68], acc0[:, 0:64], ALU.mult, ALU.add)
            i = v.scalar_tensor_tensor(acc0[:, 66:67], cmb[:, r * 66 + 65:r * 66 + 66],
                                       acc0[:, 67:68], acc0[:, 66:67], ALU.mult, ALU.add)
        i = v.tensor_scalar_max(acc0[:, 66:67], acc0[:, 66:67], 1e-30)
        i = v.reciprocal(acc0[:, 66:67], acc0[:, 66:67])
        i = v.tensor_scalar_mul(acc0[:, 0:64], acc0[:, 0:64], acc0[:, 66:67])
        t.wait_ge(b.sV.h, b.sV.n)
        ch, po = h // 2, (h % 2) * 64
        pTr = b.PS[6][po:po + 64, 0:64]
        if po == 0:
            i = t.matmul(pTr, acc0[:, 0:64], ident[0:64, 0:64], start=True, stop=True)
        else:
            i = t.matmul(pTr, acc0[:, 0:64], ident[0:64, 0:64], start=True, stop=True,
                         tile_position=(0, 64))
        b.sT.inc(i)
        v.wait_ge(b.sT.h, b.sT.n)
        dst = QT3[po:po + 64, ch, 0:64]
        tmp0 = acc0b[po:po + 64, 0:64]
        i = v.tensor_scalar(tmp0, pTr, m0s[po:po + 64, 0:1], None, ALU.mult)
        i = v.tensor_scalar_mul(dst, dst, m0s[po:po + 64, 1:2])
        i = v.tensor_add(dst, dst, tmp0)

    if DEV:
        ATTOO = b.out("ATTOO", [128, DCH * TLOC], BF16)
        b.ye.wait_ge(b.sV.h, b.sV.n)
        i = b.ye.dma_start(out=ATTOO[:, :], in_=QT); b.sLD.inc(i, 16)
        b.sLD.wait(b.ye)
    t.wait_ge(b.sV.h, b.sV.n)
    b.sT.wait(g); b.sS.wait(g); b.sV.wait(g)
    load_weight_slab(b, b.WW, Wo, DCH, D)
    b.sGD.wait(t)
    oevac = b.SMALL[:, 1024:1024 + TS]
    for f in range(DCH):
        for si in range(NSL):
            ps = b.PS[6 if (f * NSL + si) % 2 == 0 else 7][:, 0:TS]
            t.wait_ge(b.sS.h, b.sS.n)
            for k in range(DCH):
                i = t.matmul(ps, b.WW[:, k * D + f * 128:k * D + f * 128 + 128],
                             QT[:, k * TLOC + si * TS:k * TLOC + (si + 1) * TS],
                             start=(k == 0), stop=(k == DCH - 1))
            b.sT.inc(i)
            s.wait_ge(b.sT.h, b.sT.n)
            s.wait_ge(b.sLD.h, b.sLD.n)
            i = s.activation(oevac, ps, AF.Identity, bias=bqs[:, 2 * DCH + f:2 * DCH + f + 1])
            y.wait_ge(b.sS.h, b.sS.n)
            i = y.dma_start(out=ATTH[f * 128:(f + 1) * 128, si * TS:(si + 1) * TS],
                            in_=oevac)
            b.sLD.inc(i, 16)

    emit_l1_ffn(b, C, FFNH)

    b.sLD.wait(b.ye)
    b.sS.wait(b.ye)
    emit_final_combine(b, XIN, XOUT, [ATTH, FFNH])


def emit_l1_ffn(b, C, FFNH):
    v, s, t, g, y = b.vs, b.ss, b.te, b.ge, b.ye
    FF = 4096
    EPS = 1e-5
    DEV = bool(os.environ.get("KDEV"))
    ones = C["ones128"]

    b.phase_barrier()
    W1T = b.weight("L1_ffn_W1", [D, FF], transform=lambda w: w.T)
    B1c = b.weight("L1_ffn_b1", [128, FF // 128], transform=lambda w: w.reshape(FF // 128, 128).T)
    W2T = b.weight("L1_ffn_W2", [FF, D], transform=lambda w: w.T)
    B2c = b.weight("L1_ffn_b2", [128, DCH], transform=lambda w: w.reshape(DCH, 128).T)
    WQ1 = b.dram("l1wq1", [(FF // 128) * DCH * 128, 128], BF16)
    WQ2 = b.dram("l1wq2", [FF, D], BF16)
    WQ1r = WQ1.rearrange("(f k p) c -> k p f c", k=DCH, p=128)

    wf32a = b.P1[:, 0:4096].bitcast(F32)
    wf32b = b.P1[:, 4096:8192].bitcast(F32)
    wqa = b.P1[:, 8192:12288]
    qneg = b.P1[:, 12288:16384]
    HTQ = b.P1[:, 16384:16384 + 32 * TS]
    w1s = b.P1[:, 32768:32768 + 2 * 4096]
    w2s = b.P1[:, 40960:40960 + 2 * 1024]
    absacc = b.SMALL[:, 0:4]
    red = b.SMALL[:, 4:5]
    gam = b.SMALL[0:1, 8:16]
    thb = b.SMALL[:, 16:18]
    b1sb = b.sb("l1fb1", [128, FF // 128], F32)
    b2sb = b.sb("l1fb2", [128, DCH], F32)
    i = g.dma_start(out=b1sb[:, :], in_=B1c[:, :]); b.sGD.inc(i, 16)
    i = g.dma_start(out=b2sb[:, :], in_=B2c[:, :]); b.sGD.inc(i, 16)

    parts48 = b.sb("l1parts", [128, 48], F32)
    for wi, (WT, nch, width) in enumerate(((W1T, DCH, FF), (W2T, FF // 128, D))):
        for k in range(nch):
            buf = wf32a if k % 2 == 0 else wf32b
            if not (wi == 0 and k == 0):
                g.wait_ge(b.sV.h, b.sV.n)
            if width > 2048:
                i = g.dma_start(out=buf[:, 0:2048], in_=WT[k * 128:(k + 1) * 128, 0:2048])
                b.sGD.inc(i, 16)
                i = g.dma_start(out=(wf32b if k % 2 == 0 else wf32a)[:, 0:2048],
                                in_=WT[k * 128:(k + 1) * 128, 2048:4096])
                b.sGD.inc(i, 16)
                v.wait_ge(b.sGD.h, b.sGD.n)
                i = v.tensor_reduce(parts48[:, 2 * k:2 * k + 1], buf[:, 0:2048],
                                    axis=AX.X, op=ALU.add, apply_absolute_value=True)
                i = v.tensor_reduce(parts48[:, 2 * k + 1:2 * k + 2],
                                    (wf32b if k % 2 == 0 else wf32a)[:, 0:2048],
                                    axis=AX.X, op=ALU.add, apply_absolute_value=True)
            else:
                bw = buf[:, 0:width]
                i = g.dma_start(out=bw, in_=WT[k * 128:(k + 1) * 128, :])
                b.sGD.inc(i, 16)
                v.wait_ge(b.sGD.h, b.sGD.n)
                i = v.tensor_reduce(parts48[:, 16 + k:17 + k], bw,
                                    axis=AX.X, op=ALU.add, apply_absolute_value=True)
    t.wait_ge(b.sV.h, b.sV.n)
    pg = b.PS[0][0:1, 0:48]
    i = t.matmul(pg, ones[:, 0:1], parts48[:, 0:48], start=True, stop=True)
    b.sT.inc(i)
    v.wait_ge(b.sT.h, b.sT.n)
    i = v.tensor_reduce(gam[:, 0:1], pg[0:1, 0:16], axis=AX.X, op=ALU.add)
    i = v.tensor_reduce(gam[:, 1:2], pg[0:1, 16:48], axis=AX.X, op=ALU.add)
    i = v.tensor_scalar(gam[:, 0:2], gam[:, 0:2], 1.0 / (FF * D), None, ALU.mult)
    i = v.tensor_scalar(gam[:, 2:4], gam[:, 0:2], 0.5, 0.5 * EPS, ALU.mult, ALU.add)
    t.wait_ge(b.sV.h, b.sV.n)
    pthb = b.PS[0][:, 0:2]
    i = t.matmul(pthb, ones[0:1, 0:128], gam[:, 2:4], start=True, stop=True)
    b.sT.inc(i)
    v.wait_ge(b.sT.h, b.sT.n)
    i = v.tensor_copy(thb[:, :], pthb)

    if DEV:
        GAMO = b.out("GAMO", [128, 24], F32)
        PARTSO = b.out("PARTSO", [128, 48], F32)
        y.wait_ge(b.sV.h, b.sV.n)
        i = y.dma_start(out=GAMO[:, :], in_=b.SMALL[:, 0:24])
        b.sLD.inc(i, 16)
        i = y.dma_start(out=PARTSO[:, :], in_=parts48[:, :])
        b.sLD.inc(i, 16)
    for wi, (WT, nch, width, th) in enumerate(((W1T, DCH, FF, thb[:, 0:1]),
                                               (W2T, FF // 128, D, thb[:, 1:2]))):
        for k in range(nch):
            if width > 2048:
                parts = [(wf32a, 0), (wf32b, 2048)]
            else:
                parts = [(wf32a if k % 2 == 0 else wf32b, 0)]
            g.wait_ge(b.sV.h, b.sV.n)
            for (buf, off) in parts:
                i = g.dma_start(out=buf[:, 0:min(width, 2048)],
                                in_=WT[k * 128:(k + 1) * 128, off:off + min(width, 2048)])
                b.sGD.inc(i, 16)
            v.wait_ge(b.sGD.h, b.sGD.n)
            qb = wqa[:, 0:width]
            for (buf, off) in parts:
                w = min(width, 2048)
                i = v.tensor_scalar(qb[:, off:off + w], buf[:, 0:w], th, None, ALU.is_gt)
                i = v.tensor_scalar_mul(buf[:, 0:w], buf[:, 0:w], -1.0)
                i = v.tensor_scalar(qneg[:, off:off + w], buf[:, 0:w], th, None, ALU.is_gt)
                i = v.tensor_tensor(qb[:, off:off + w], qb[:, off:off + w],
                                    qneg[:, off:off + w], ALU.subtract)
            g.wait_ge(b.sV.h, b.sV.n)
            if wi == 0:
                i = g.dma_start(out=WQ1r[k], in_=qb.rearrange("p (f c) -> p f c", c=128))
            else:
                i = g.dma_start(out=WQ2[k * 128:(k + 1) * 128, :], in_=qb)
            b.sGD.inc(i, 16)

    if DEV:
        WQ2O = b.out("WQ2O", [FF, D], BF16)
        WQ1O = b.out("WQ1O", [(FF // 128) * DCH * 128, 128], BF16)
        g.wait_ge(b.sGD.h, b.sGD.n)
        i = g.dma_start(out=WQ2O[:, :], in_=WQ2[:, :]); b.sGD.inc(i, 16)
        i = g.dma_start(out=WQ1O[:, :], in_=WQ1[:, :]); b.sGD.inc(i, 16)
        HT0 = b.out("HT0", [128, 32 * TS], BF16)
    oev = b.SMALL[:, 1536:1536 + TS]
    for q in range(NSL):
        for f in range(FF // 128):
            wbuf = w1s[:, (f % 2) * 4096:(f % 2) * 4096 + 1024]
            if not (q == 0 and f < 2):
                g.wait_ge(b.sT.h, b.sT.n)
            i = g.dma_start(out=wbuf.rearrange("p (k c) -> p k c", c=128),
                            in_=WQ1[(f * DCH) * 128:(f * DCH + DCH) * 128, :].rearrange(
                                "(k p) c -> p k c", p=128))
            b.sGD.inc(i, 16)
            ps = b.PS[0 if f % 2 == 0 else 1][:, 0:TS]
            t.wait_ge(b.sGD.h, b.sGD.n)
            t.wait_ge(b.sS.h, b.sS.n)
            for k in range(DCH):
                i = t.matmul(ps, wbuf[:, k * 128:(k + 1) * 128],
                             b.XN[:, k * TLOC + q * TS:k * TLOC + (q + 1) * TS],
                             start=(k == 0), stop=(k == DCH - 1))
            b.sT.inc(i)
            s.wait_ge(b.sT.h, b.sT.n)
            i = s.activation(HTQ[:, f * TS:(f + 1) * TS], ps, AF.Relu,
                             bias=b1sb[:, f:f + 1])
        if DEV and q == 0:
            g.wait_ge(b.sS.h, b.sS.n)
            i = g.dma_start(out=HT0[:, :], in_=HTQ)
            b.sGD.inc(i, 16)
        t.wait_ge(b.sS.h, b.sS.n)
        for f in range(FF // 128):
            wbuf = w2s[:, (f % 2) * 1024:(f % 2) * 1024 + 1024]
            if f >= 2:
                g.wait_ge(b.sT.h, b.sT.n)
            i = g.dma_start(out=wbuf, in_=WQ2[f * 128:(f + 1) * 128, :])
            b.sGD.inc(i, 16)
            t.wait_ge(b.sGD.h, b.sGD.n)
            for dm in range(DCH):
                i = t.matmul(b.PS[dm][:, 0:TS], wbuf[:, dm * 128:(dm + 1) * 128],
                             HTQ[:, f * TS:(f + 1) * TS],
                             start=(f == 0), stop=(f == FF // 128 - 1))
            b.sT.inc(i)
        s.wait_ge(b.sT.h, b.sT.n)
        for dm in range(DCH):
            if dm > 0:
                s.wait_ge(b.sLD.h, b.sLD.n)
            i = s.activation(oev, b.PS[dm][:, 0:TS], AF.Identity, bias=b2sb[:, dm:dm + 1])
            y.wait_ge(b.sS.h, b.sS.n)
            i = y.dma_start(out=FFNH[dm * 128:(dm + 1) * 128, q * TS:(q + 1) * TS],
                            in_=oev)
            b.sLD.inc(i, 16)
        t.wait_ge(b.sLD.h, b.sLD.n)
    b.sLD.wait(b.ye)



def emit_l0(b, C):
    v, s, t, g, y = b.vs, b.ss, b.te, b.ge, b.ye
    XIN = b.xin()
    XOUT = b.out("XOUT", [D, TLOC], F32)
    DEV = bool(os.environ.get("KDEV"))
    ATTH = b.out("ATTH", [D, TLOC], F32) if DEV else b.dram("l0att", [D, TLOC], F32)
    KD = 256
    KC = 2
    CH = 128
    NCH = TLOC // CH
    CAP = 2048
    FH = 4096
    ones = C["ones128"]
    ident = C["ident"]
    identb = C["identb"]

    Wq = b.weight("L0_attn_Wq", [D, KD])
    Wk = b.weight("L0_attn_Wk", [D, KD])
    Wg = b.weight("L0_attn_Wg", [D, KD])
    Wv = b.weight("L0_attn_Wv", [D, KD])
    Wo = b.weight("L0_attn_Wo", [KD, D])
    bqc = b.weight("L0_attn_bq", [128, 2], transform=lambda w: w.reshape(2, 128).T)
    bkc = b.weight("L0_attn_bk", [128, 2], transform=lambda w: w.reshape(2, 128).T)
    bgc = b.weight("L0_attn_bg", [128, 2], transform=lambda w: w.reshape(2, 128).T)
    bvc = b.weight("L0_attn_bv", [1, KD], transform=lambda w: w.reshape(1, KD))
    boc = b.weight("L0_attn_bo", [128, DCH], transform=lambda w: w.reshape(DCH, 128).T)
    WgR = b.weight("L0_ffn_Wg", [D, 8])

    def w1prov(c, inputs, state):
        w = np.asarray(inputs["L0_ffn_W1"])[c]
        w4 = w.reshape(8, 128, 32, 128).transpose(2, 0, 1, 3)
        return np.ascontiguousarray(w4.reshape(32 * 8 * 128, 128), dtype=np.float32)
    W1E = b.inp("w_L0W1e", [32 * 8 * 128, 128], F32, w1prov)
    W2E = b.inp("w_L0W2e", [FH, D], F32,
                lambda c, inputs, state: np.ascontiguousarray(
                    np.asarray(inputs["L0_ffn_W2"])[c], dtype=np.float32))
    B1E = b.inp("w_L0b1e", [128, FH // 128], F32,
                lambda c, inputs, state: np.ascontiguousarray(
                    np.asarray(inputs["L0_ffn_b1"])[c].reshape(FH // 128, 128).T, dtype=np.float32))
    B2E = b.inp("w_L0b2e", [128, DCH], F32,
                lambda c, inputs, state: np.ascontiguousarray(
                    np.asarray(inputs["L0_ffn_b2"])[c].reshape(DCH, 128).T, dtype=np.float32))

    TRIU = b.const("l0triu", lambda c: np.triu(np.ones((128, 128), np.float32)))
    TRIL_B = b.const("l0tril", lambda c: np.tril(np.ones((128, 128), np.float32)))
    IOT8 = b.const("l0iot8", lambda c: np.ascontiguousarray(
        np.broadcast_to(np.arange(8, dtype=np.float32), (128, 8))))
    SELPAIR = b.const("l0selpair", lambda c: np.concatenate([
        np.kron(np.ones((8, 1), np.float32), np.eye(8, dtype=np.float32)),
        np.kron((np.arange(8) < c).astype(np.float32).reshape(8, 1), np.eye(8, dtype=np.float32)),
    ], axis=1))
    MYROWS = b.const("l0myrows", lambda c: np.ascontiguousarray(
        (c * TLOC + np.arange(TLOC, dtype=np.uint32)).reshape(TLOC // 128, 128).T))
    TIDS = b.const("l0tids", lambda c: np.ascontiguousarray(
        (c * TLOC + np.arange(TLOC, dtype=np.float32)).reshape(TLOC // 128, 128).T))
    SELC = b.const("l0selc", lambda c: np.ascontiguousarray(
        np.broadcast_to((np.arange(8) < c).astype(np.float32), (128, 8))))

    emit_ln(b, XIN, b.XN, C)

    QT = b.P1[:, 0:2 * TLOC]
    KT = b.P1[:, 2 * TLOC:4 * TLOC]
    QL = b.P1[:, 4 * TLOC:6 * TLOC]
    VT = b.P1[:, 6 * TLOC:6 * TLOC + 16 * KD]
    OUTP = b.P1[:, 8 * TLOC:12 * TLOC].bitcast(F32)
    CUM = b.P1[:, 12 * TLOC:16 * TLOC].bitcast(F32)
    LGB = b.P1[:, 16 * TLOC:20 * TLOC].bitcast(F32)
    AS1 = b.P1[:, 20 * TLOC:20 * TLOC + 128]
    AT1 = b.P1[:, 20 * TLOC + 128:20 * TLOC + 256]
    KHT = b.P1[:, 20 * TLOC + 256:20 * TLOC + 512]
    QHA = b.P1[:, 20 * TLOC + 512:20 * TLOC + 768]
    KHA = b.P1[:, 20 * TLOC + 768:20 * TLOC + 800]
    OUTB = b.P1[:, 21 * TLOC:23 * TLOC]

    ST = b.sb("l0state", [128, 2 * 256], F32)
    STB = b.sb("l0stateb", [128, 2 * 256], BF16)
    CARRY = b.sb("l0carry", [128, 2 * 256], F32)
    CARRYB = b.sb("l0carryb", [128, 2 * 256], BF16)
    DCOL = b.sb("l0dcol", [128, 2 * NCH], F32)
    LC = b.sb("l0lc", [128, 8], F32)
    zrow = b.sb("l0zrow", [128, 128], F32)
    trisb = b.sb("l0trisb", [128, 128], F32)
    triu_sb = b.sb("l0triusb", [128, 128], F32)
    bqs = b.sb("l0bqs", [128, 16], F32)
    sc32 = b.sb("l0sc32", [128, 1024], F32)

    g.memset(zrow[:, :], 0.0)
    g.memset(ST[:, :], 0.0)
    g.memset(STB[:, :], 0.0)
    g.memset(CARRY[:, :], 0.0)
    i = g.memset(LC[:, :], 0.0)
    b.sG.inc(i)
    i = g.dma_start(out=trisb[:, :], in_=TRIL_B[:, :]); b.sGD.inc(i, 16)
    i = g.dma_start(out=triu_sb[:, :], in_=TRIU[:, :]); b.sGD.inc(i, 16)
    i = g.dma_start(out=bqs[:, 0:2], in_=bqc[:, :]); b.sGD.inc(i, 16)
    i = g.dma_start(out=bqs[:, 2:4], in_=bkc[:, :]); b.sGD.inc(i, 16)
    i = g.dma_start(out=bqs[:, 4:6], in_=bgc[:, :]); b.sGD.inc(i, 16)
    i = g.dma_start(out=bqs[:, 6:6 + DCH], in_=boc[:, :]); b.sGD.inc(i, 16)
    bvr = b.SMALL[0:1, 0:KD]
    i = g.dma_start(out=bvr, in_=bvc[:, :]); b.sGD.inc(i, 16)

    v.wait_ge(b.sGD.h, b.sGD.n)
    b.sV.wait(t); b.sV.wait(s); b.sV.wait(g)
    load_weight_slab(b, b.WW, Wq, DCH, KD)
    b.sGD.wait(t)
    proj_fm(b, QT, 0, b.XN, b.WW, KD, bias_col=bqs[:, 0:2], wwidth=KD)
    b.sT.wait(g); b.sS.wait(g)
    load_weight_slab(b, b.WW, Wk, DCH, KD)
    b.sGD.wait(t)
    proj_fm(b, KT, 0, b.XN, b.WW, KD, bias_col=bqs[:, 2:4], act=AF.Sigmoid, wwidth=KD)
    b.sT.wait(g); b.sS.wait(g)
    load_weight_slab(b, b.WW, Wg, DCH, KD)
    b.sGD.wait(t)
    for f in range(KC):
        for si in range(NSL):
            ps = b.PS[6 if (f * NSL + si) % 2 == 0 else 7][:, 0:TS]
            t.wait_ge(b.sS.h, b.sS.n)
            t.wait_ge(b.sV.h, b.sV.n)
            for k in range(DCH):
                i = t.matmul(ps, b.WW[:, k * KD + f * 128:k * KD + f * 128 + 128],
                             b.XN[:, k * TLOC + si * TS:k * TLOC + (si + 1) * TS],
                             start=(k == 0), stop=(k == DCH - 1))
            b.sT.inc(i)
            s.wait_ge(b.sT.h, b.sT.n)
            s.wait_ge(b.sV.h, b.sV.n)
            zg = sc32[:, 0:TS]
            aa = sc32[:, TS:2 * TS]
            i = s.activation(zg, ps, AF.Identity, bias=bqs[:, 4 + f:5 + f])
            i = s.activation(aa, zg, AF.Abs)
            i = s.activation(aa, aa, AF.Exp, scale=-1.0)
            i = s.activation(aa, aa, AF.Ln, bias=b.ccol(1.0))
            i = s.activation(zg, zg, AF.Relu, scale=-1.0)
            dst = LGB[:, f * TLOC + si * TS:f * TLOC + (si + 1) * TS]
            v.wait_ge(b.sS.h, b.sS.n)
            i = v.tensor_add(dst, zg, aa)
            i = v.tensor_scalar_mul(dst, dst, -1.0)
    b.sT.wait(g); b.sS.wait(g); b.sV.wait(g)
    load_weight_slab(b, b.WW, Wv, DCH, KD)
    b.sGD.wait(t)
    for tt in range(NTT):
        ps = b.PS[6 if tt % 2 == 0 else 7][:, 0:KD]
        t.wait_ge(b.sS.h, b.sS.n)
        for k in range(DCH):
            t.matmul(ps, b.XN[:, k * TLOC + tt * 128:k * TLOC + tt * 128 + 128],
                     b.WW[:, k * KD:(k + 1) * KD], start=(k == 0), stop=False)
        i = t.matmul(ps, ones[0:1, 0:128], bvr, start=False, stop=True)
        b.sT.inc(i)
        s.wait_ge(b.sT.h, b.sT.n)
        i = s.activation(VT[:, tt * KD:(tt + 1) * KD], ps, AF.Copy)

    if DEV:
        L0QT = b.out("L0QT", [128, 2 * TLOC], BF16)
        L0KT = b.out("L0KT", [128, 2 * TLOC], BF16)
        L0LG = b.out("L0LG", [128, 2 * TLOC], F32)
        L0VT = b.out("L0VT", [128, 16 * KD], BF16)
        b.ye.wait_ge(b.sS.h, b.sS.n)
        b.ye.wait_ge(b.sV.h, b.sV.n)
        i = b.ye.dma_start(out=L0QT[:, :], in_=QT); b.sLD.inc(i, 16)
        i = b.ye.dma_start(out=L0KT[:, :], in_=KT); b.sLD.inc(i, 16)
        i = b.ye.dma_start(out=L0LG[:, :], in_=LGB); b.sLD.inc(i, 16)
        i = b.ye.dma_start(out=L0VT[:, :], in_=VT); b.sLD.inc(i, 16)
        b.sLD.wait(b.ye)

    RP = b.sb("l0rp", [128, 8], F32)
    RC = b.sb("l0rc", [128, 16 * 16], F32)
    RW = b.sb("l0rw", [128, 16 * 8], F32)
    RS = b.sb("l0rs", [128, 32], F32)
    iot8 = b.sb("l0iot8s", [128, 8], F32)
    selpair = b.sb("l0selpairs", [64, 16], F32)
    tids = b.sb("l0tidss", [128, 16], F32)
    myrows = b.sb("l0myrowss", [128, 16], U32)
    selcs = b.sb("l0selcs", [128, 8], F32)
    i = g.dma_start(out=iot8[:, :], in_=IOT8[:, :]); b.sGD.inc(i, 16)
    i = g.dma_start(out=selpair[:, :], in_=SELPAIR[:, :]); b.sGD.inc(i, 16)
    i = g.dma_start(out=tids[:, :], in_=TIDS[:, :]); b.sGD.inc(i, 16)
    i = g.dma_start(out=myrows[:, :], in_=MYROWS[:, :]); b.sGD.inc(i, 16)
    i = g.dma_start(out=selcs[:, :], in_=SELC[:, :]); b.sGD.inc(i, 16)

    b.sT.wait(g); b.sS.wait(g)
    load_weight_slab(b, b.WW, WgR, DCH, 8)
    b.sGD.wait(t)
    v.wait_ge(b.sGD.h, b.sGD.n)
    for tt in range(NTT):
        ps = b.PS[4][:, 0:8]
        t.wait_ge(b.sV.h, b.sV.n)
        t.wait_ge(b.sS.h, b.sS.n)
        for k in range(DCH):
            i = t.matmul(ps, b.XN[:, k * TLOC + tt * 128:k * TLOC + tt * 128 + 128],
                         b.WW[:, k * 8:(k + 1) * 8], start=(k == 0), stop=(k == DCH - 1))
        b.sT.inc(i)
        v.wait_ge(b.sT.h, b.sT.n)
        pr = RP[:, 0:8]
        m = RS[:, 0:1]
        i = v.tensor_reduce(m, ps, axis=AX.X, op=ALU.max)
        i = v.tensor_scalar_mul(m, m, -1.0)
        s.wait_ge(b.sV.h, b.sV.n)
        b.act_accum(pr, ps, AF.Exp, bias=m, accum_out=RS[:, 1:2])
        v.wait_ge(b.sS.h, b.sS.n)
        i = v.reciprocal(RS[:, 1:2], RS[:, 1:2])
        i = v.tensor_scalar_mul(pr, pr, RS[:, 1:2])
        oh1 = RC[:, tt * 16:tt * 16 + 8]
        oh2 = RC[:, tt * 16 + 8:tt * 16 + 16]
        w = RW[:, tt * 8:(tt + 1) * 8]
        i = v.tensor_reduce(w[:, 0:1], pr, axis=AX.X, op=ALU.max)
        i = v.tensor_tensor(oh1, pr, w[:, 0:1].broadcast_to([128, 8]), ALU.is_ge)
        i = v.tensor_tensor_scan(oh2, oh1, zrow[:, 0:8], 0.0, ALU.add, ALU.add)
        i = v.tensor_scalar(oh2, oh2, 1.5, None, ALU.is_lt)
        i = v.tensor_mul(oh1, oh1, oh2)
        i = v.scalar_tensor_tensor(oh2, oh1, -2.0, pr, ALU.mult, ALU.add)
        i = v.tensor_reduce(w[:, 1:2], oh2, axis=AX.X, op=ALU.max)
        i = v.tensor_tensor(oh2, oh2, w[:, 1:2].broadcast_to([128, 8]), ALU.is_ge)
        i = v.tensor_tensor_scan(RS[:, 2:10], oh2, zrow[:, 0:8], 0.0, ALU.add, ALU.add)
        i = v.tensor_scalar(RS[:, 2:10], RS[:, 2:10], 1.5, None, ALU.is_lt)
        i = v.tensor_mul(oh2, oh2, RS[:, 2:10])
        i = v.tensor_add(RS[:, 0:1], w[:, 0:1], w[:, 1:2])
        i = v.tensor_scalar_add(RS[:, 0:1], RS[:, 0:1], 1e-9)
        i = v.reciprocal(RS[:, 0:1], RS[:, 0:1])
        i = v.tensor_mul(w[:, 2:3], w[:, 0:1], RS[:, 0:1])
        i = v.tensor_mul(w[:, 3:4], w[:, 1:2], RS[:, 0:1])
        i = v.tensor_mul(RS[:, 2:10], oh1, iot8[:, 0:8])
        i = v.tensor_reduce(w[:, 4:5], RS[:, 2:10], axis=AX.X, op=ALU.add)
        i = v.tensor_mul(RS[:, 2:10], oh2, iot8[:, 0:8])
        i = v.tensor_reduce(w[:, 5:6], RS[:, 2:10], axis=AX.X, op=ALU.add)

    t.wait_ge(b.sV.h, b.sV.n)
    ph = b.PS[5][0:16, 0:1]
    for tt in range(NTT):
        i = t.matmul(ph, RC[:, tt * 16:tt * 16 + 16], ones[:, 0:1],
                     start=(tt == 0), stop=(tt == NTT - 1))
    b.sT.inc(i)
    v.wait_ge(b.sT.h, b.sT.n)
    i = v.tensor_copy(RS[0:16, 16:17], ph)
    PUBH = b.dram("l0pubh", [16, 1], F32)
    AGH = b.dram("l0agh", [128, 1], F32, shared=True)
    g.wait_ge(b.sV.h, b.sV.n)
    i = g.dma_start(out=PUBH[:, :], in_=RS[0:16, 16:17]); b.sGD.inc(i, 16)
    g.wait_ge(b.sGD.h, b.sGD.n)
    b.allgather(PUBH[:, :], AGH[:, :])
    y.wait_ge(b.sG.h, b.sG.n)
    HB = b.sb("l0hb", [64, 2], F32)
    agh3 = AGH[:, 0:1].rearrange("(r two j) one -> r two (j one)", two=2, j=8)
    i = y.dma_start(out=HB[:, 0:1], in_=agh3[:, 0, :])
    b.sLD.inc(i, 16)
    i = y.dma_start(out=HB[:, 1:2], in_=agh3[:, 1, :])
    b.sLD.inc(i, 16)
    t.wait_ge(b.sLD.h, b.sLD.n)
    ptot = b.PS[5][0:8, 1:3]
    ppre = b.PS[6][0:8, 0:2]
    t.matmul(ptot, selpair[:, 0:8], HB[:, 0:2], start=True, stop=True)
    i = t.matmul(ppre, selpair[:, 8:16], HB[:, 0:2], start=True, stop=True)
    b.sT.inc(i)
    v.wait_ge(b.sT.h, b.sT.n)
    cnt = RS[0:8, 18:19]
    i = v.tensor_scalar_min(cnt, ptot[:, 0:1], float(CAP))
    i = v.tensor_copy(RS[0:8, 19:20], ppre[:, 0:1])
    i = v.tensor_add(RS[0:8, 20:21], cnt, ppre[:, 1:2])
    t.wait_ge(b.sV.h, b.sV.n)
    prow0 = b.PS[5][0:1, 8:16]
    prow1 = b.PS[6][0:1, 0:8]
    t.matmul(prow0, RS[0:8, 19:20], ident[0:8, 0:8], start=True, stop=True)
    i = t.matmul(prow1, RS[0:8, 20:21], ident[0:8, 0:8], start=True, stop=True)
    b.sT.inc(i)
    v.wait_ge(b.sT.h, b.sT.n)
    runrows = b.SMALL[0:1, 256:272]
    i = v.tensor_copy(runrows[:, 0:8], prow0)
    i = v.tensor_copy(runrows[:, 8:16], prow1)
    SLOTTAB = b.dram("l0slots", [T, 1], F32)
    SLOTSUM = b.dram("l0slotsum", [T, 1], F32)
    zt = b.SMALL[:, 272:400]
    i = v.memset(zt, 0.0)
    g.wait_ge(b.sV.h, b.sV.n)
    i = g.dma_start(out=SLOTTAB[:, :], in_=zt[:, 0:128])
    b.sGD.inc(i, 16)
    fi_u = b.sb("l0fiu", [128, 2], U32)
    val2 = b.sb("l0val2", [128, 1], F32)
    FISLAB = b.sb("l0fislab", [128, 64], F32)
    for tt in range(NTT):
        for kk in range(2):
            oh = RC[:, tt * 16 + kk * 8:tt * 16 + (kk + 1) * 8]
            wsl = RW[:, tt * 8:(tt + 1) * 8]
            pp = b.PS[4][:, 0:8]
            t.wait_ge(b.sV.h, b.sV.n)
            t.matmul(pp, triu_sb[:, :], oh, start=True, stop=False)
            i = t.matmul(pp, ones[0:1, 0:128], runrows[:, kk * 8:(kk + 1) * 8],
                         start=False, stop=True)
            b.sT.inc(i)
            v.wait_ge(b.sT.h, b.sT.n)
            keep = RS[:, 2:10]
            i = v.tensor_scalar(keep, pp, float(CAP) + 0.5, None, ALU.is_lt)
            i = v.tensor_mul(keep, keep, oh)
            slotc = RS[:, 10:11]
            i = v.tensor_mul(RS[:, 12:20], pp, oh)
            i = v.tensor_reduce(slotc, RS[:, 12:20], axis=AX.X, op=ALU.add)
            kfl = RS[:, 11:12]
            i = v.tensor_reduce(kfl, keep, axis=AX.X, op=ALU.add)
            i = v.tensor_copy(wsl[:, 6 + kk:7 + kk], kfl)
            idxc = wsl[:, 4 + kk:5 + kk]
            fic = RS[:, 20:21]
            i = v.tensor_scalar(fic, idxc, float(CAP), -1.0, ALU.mult, ALU.add)
            i = v.tensor_add(fic, fic, slotc)
            i = v.tensor_mul(RS[:, 21:22], fic, kfl)
            i = v.scalar_tensor_tensor(RS[:, 22:23], kfl, -1e7, RS[:, 21:22], ALU.mult, ALU.add)
            i = v.tensor_scalar_add(RS[:, 22:23], RS[:, 22:23], 1e7)
            fim = RS[:, 22:23]
            v.wait_ge(b.sGD.h, b.sGD.n)
            i = v.tensor_copy(fi_u[:, kk:kk + 1], fim)
            i = v.tensor_copy(FISLAB[:, tt * 4 + kk:tt * 4 + kk + 1], fim)
            i = v.tensor_copy(val2[:, 0:1], tids[:, tt:tt + 1])
            g.wait_ge(b.sV.h, b.sV.n)
            i = g.indirect_dma_start(
                SLOTTAB[:, :],
                IndirectOffsetOnAxis(ap=fi_u[:, kk:kk + 1], axis=0),
                val2[:, 0:1], None,
                bounds_check=T - 1, oob_is_err=False)
            b.sGD.inc(i, 16)
            pt2 = b.PS[5][0:1, 16 + kk * 8:24 + kk * 8]
            i = t.matmul(pt2, ones[:, 0:1], oh, start=True, stop=True)
            b.sT.inc(i)
            v.wait_ge(b.sT.h, b.sT.n)
            i = v.tensor_tensor(runrows[:, kk * 8:(kk + 1) * 8],
                                runrows[:, kk * 8:(kk + 1) * 8], pt2, ALU.add)
    g.wait_ge(b.sGD.h, b.sGD.n)
    b.allreduce(SLOTTAB[:, :], SLOTSUM[:, :])
    if DEV:
        RWO = b.out("RWO", [128, 16 * 8], F32)
        FIO = b.out("FIO", [128, 64], F32)
        b.ye.wait_ge(b.sV.h, b.sV.n)
        i = b.ye.dma_start(out=RWO[:, :], in_=RW[:, :]); b.sLD.inc(i, 16)
        i = b.ye.dma_start(out=FIO[:, :], in_=FISLAB[:, :]); b.sLD.inc(i, 16)
        b.sLD.wait(b.ye)

    XNF = b.dram("l0xnf", [D, TLOC], BF16)
    XNPUB = b.dram("l0xnpub", [TLOC, D], BF16)
    XNALL = b.dram("l0xnall", [T, D], BF16, shared=True)
    b.sV.wait(y)
    for k in range(DCH):
        i = y.dma_start(out=XNF[k * 128:(k + 1) * 128, :], in_=b.XN[:, k * TLOC:(k + 1) * TLOC])
        b.sLD.inc(i, 16)
    XT1 = b.sb("l0xt1", [128, D], BF16)
    y.wait_ge(b.sLD.h, b.sLD.n)
    for tt in range(NTT):
        i = y.dma_start(out=XT1[:, :], in_=XNF[:, tt * 128:(tt + 1) * 128], transpose=True)
        b.sLD.inc(i, 16)
        y.wait_ge(b.sLD.h, b.sLD.n)
        i = y.dma_start(out=XNPUB[tt * 128:(tt + 1) * 128, :], in_=XT1[:, :])
        b.sLD.inc(i, 16)
        y.wait_ge(b.sLD.h, b.sLD.n)
    b.sLD.wait(g)
    b.allgather(XNPUB[:, :], XNALL[:, :])

    for c in range(NCH):
        co = c * CH
        for kc in range(KC):
            lgsl = LGB[:, kc * TLOC + co:kc * TLOC + co + CH]
            cumsl = CUM[:, kc * TLOC + co:kc * TLOC + co + CH]
            i = v.tensor_tensor_scan(cumsl, lgsl, zrow[:, :], 0.0, ALU.add, ALU.add)
            esl = sc32[:, 0:CH]
            s.wait_ge(b.sV.h, b.sV.n)
            i = s.activation(esl, cumsl, AF.Exp)
            v.wait_ge(b.sS.h, b.sS.n)
            qsl = QT[:, kc * TLOC + co:kc * TLOC + co + CH]
            i = v.tensor_mul(QL[:, kc * TLOC + co:kc * TLOC + co + CH], qsl, esl)
        pA = b.PS[0][:, 0:128]
        t.wait_ge(b.sV.h, b.sV.n)
        for a in range(8):
            sa0 = a * 16
            for kc in range(KC):
                cumsl = CUM[:, kc * TLOC + co:kc * TLOC + co + CH]
                acol = cumsl[:, sa0:sa0 + 1]
                i = v.tensor_scalar(sc32[:, 0:CH], cumsl, acol, 60.0, ALU.subtract, ALU.min)
                s.wait_ge(b.sV.h, b.sV.n)
                i = s.activation(sc32[:, 0:CH], sc32[:, 0:CH], AF.Exp)
                v.wait_ge(b.sS.h, b.sS.n)
                qsl = QT[:, kc * TLOC + co:kc * TLOC + co + CH]
                i = v.tensor_mul(QHA[:, kc * 128:kc * 128 + CH], qsl, sc32[:, 0:CH])
                i = v.tensor_scalar(sc32[:, CH:CH + 16], cumsl[:, sa0:sa0 + 16], acol,
                                    -80.0, ALU.subtract, ALU.max)
                s.wait_ge(b.sV.h, b.sV.n)
                i = s.activation(sc32[:, CH:CH + 16], sc32[:, CH:CH + 16], AF.Exp, scale=-1.0)
                v.wait_ge(b.sS.h, b.sS.n)
                ksl = KT[:, kc * TLOC + co + sa0:kc * TLOC + co + sa0 + 16]
                i = v.tensor_mul(KHA[:, kc * 16:kc * 16 + 16], ksl, sc32[:, CH:CH + 16])
            t.wait_ge(b.sV.h, b.sV.n)
            for kc in range(KC):
                i = t.matmul(pA[:, sa0:sa0 + 16], QHA[:, kc * 128:kc * 128 + CH],
                             KHA[:, kc * 16:kc * 16 + 16],
                             start=(kc == 0), stop=(kc == KC - 1))
            b.sT.inc(i)
        v.wait_ge(b.sT.h, b.sT.n)
        i = v.tensor_tensor(AS1[:, 0:128], pA, trisb[:, :], ALU.mult)
        t.wait_ge(b.sV.h, b.sV.n)
        pT = b.PS[1].bitcast(BF16)[:, 0:128]
        i = t.transpose(pT, AS1[:, 0:128], identb[:, :])
        b.sT.inc(i)
        v.wait_ge(b.sT.h, b.sT.n)
        i = v.tensor_copy(AT1[:, 0:128], pT)
        for vc in range(KC):
            po = b.PS[2 + vc][:, 0:128]
            t.wait_ge(b.sV.h, b.sV.n)
            t.wait_ge(b.sS.h, b.sS.n)
            t.matmul(po, VT[:, c * KD + vc * 128:c * KD + (vc + 1) * 128], AT1[:, 0:128],
                     start=True, stop=False)
            for kc in range(KC):
                i = t.matmul(po, STB[:, kc * 256 + vc * 128:kc * 256 + (vc + 1) * 128],
                             QL[:, kc * TLOC + co:kc * TLOC + co + CH],
                             start=False, stop=(kc == KC - 1))
            b.sT.inc(i)
            s.wait_ge(b.sT.h, b.sT.n)
            i = s.activation(OUTP[:, vc * TLOC + co:vc * TLOC + co + CH], po, AF.Copy)
        for kc in range(KC):
            cumsl = CUM[:, kc * TLOC + co:kc * TLOC + co + CH]
            s.wait_ge(b.sV.h, b.sV.n)
            i = s.activation(DCOL[:, kc * NCH + c:kc * NCH + c + 1], LC[:, kc:kc + 1], AF.Exp)
            v.wait_ge(b.sS.h, b.sS.n)
            i = v.tensor_tensor(LC[:, kc:kc + 1], LC[:, kc:kc + 1], cumsl[:, CH - 1:CH], ALU.add)
        for kc in range(KC):
            cumsl = CUM[:, kc * TLOC + co:kc * TLOC + co + CH]
            ccol = cumsl[:, CH - 1:CH]
            i = v.tensor_scalar(sc32[:, 0:CH], cumsl, ccol, None, ALU.subtract)
            s.wait_ge(b.sV.h, b.sV.n)
            i = s.activation(sc32[:, 0:CH], sc32[:, 0:CH], AF.Exp, scale=-1.0)
            v.wait_ge(b.sS.h, b.sS.n)
            ksl = KT[:, kc * TLOC + co:kc * TLOC + co + CH]
            i = v.tensor_mul(QHA[:, 0:CH], ksl, sc32[:, 0:CH])
            t.wait_ge(b.sV.h, b.sV.n)
            pT2 = b.PS[1].bitcast(BF16)[:, 0:128]
            i = t.transpose(pT2, QHA[:, 0:CH], identb[:, :])
            b.sT.inc(i)
            v.wait_ge(b.sT.h, b.sT.n)
            i = v.tensor_copy(KHT[:, kc * 128:(kc + 1) * 128], pT2)
            pS2 = b.PS[4][:, 0:256]
            t.wait_ge(b.sV.h, b.sV.n)
            i = t.matmul(pS2, KHT[:, kc * 128:(kc + 1) * 128], VT[:, c * KD:(c + 1) * KD],
                         start=True, stop=True)
            b.sT.inc(i)
            pcc = sc32[:, TS:TS + 1]
            s.wait_ge(b.sV.h, b.sV.n)
            i = s.activation(pcc, cumsl[:, CH - 1:CH], AF.Exp)
            v.wait_ge(b.sS.h, b.sS.n)
            v.wait_ge(b.sT.h, b.sT.n)
            i = v.tensor_scalar_mul(ST[:, kc * 256:(kc + 1) * 256],
                                    ST[:, kc * 256:(kc + 1) * 256], pcc)
            i = v.tensor_tensor(ST[:, kc * 256:(kc + 1) * 256],
                                ST[:, kc * 256:(kc + 1) * 256], pS2, ALU.add)
            i = v.tensor_copy(STB[:, kc * 256:(kc + 1) * 256], ST[:, kc * 256:(kc + 1) * 256])

    if DEV:
        L0OP = b.out("L0OP", [128, 2 * TLOC], F32)
        b.ye.wait_ge(b.sS.h, b.sS.n)
        b.ye.wait_ge(b.sV.h, b.sV.n)
        i = b.ye.dma_start(out=L0OP[:, :], in_=OUTP); b.sLD.inc(i, 16)
        b.sLD.wait(b.ye)

    PUBG = b.dram("l0pubg", [128, 2 * 256 + 2], F32)
    AGG = b.dram("l0agg", [NCORE * 128, 2 * 256 + 2], F32, shared=True)
    g.wait_ge(b.sV.h, b.sV.n)
    i = g.dma_start(out=PUBG[:, 0:512], in_=ST[:, :]); b.sGD.inc(i, 16)
    i = g.dma_start(out=PUBG[:, 512:514], in_=LC[:, 0:2]); b.sGD.inc(i, 16)
    g.wait_ge(b.sGD.h, b.sGD.n)
    b.allgather(PUBG[:, :], AGG[:, :])
    y.wait_ge(b.sG.h, b.sG.n)
    SI = b.sb("l0si", [128, 512], F32)
    LPi = b.sb("l0lpi", [128, 2], F32)
    acc_l = b.sb("l0accl", [128, 4], F32)
    v.wait_ge(b.sGD.h, b.sGD.n)
    i = v.memset(acc_l[:, 0:4], 0.0)
    for r in range(NCORE):
        y.wait_ge(b.sV.h, b.sV.n)
        i = y.dma_start(out=LPi[:, 0:2], in_=AGG[r * 128:(r + 1) * 128, 512:514])
        b.sLD.inc(i, 16)
        v.wait_ge(b.sLD.h, b.sLD.n)
        i = v.scalar_tensor_tensor(acc_l[:, 2:4], LPi[:, 0:2], selcs[:, r:r + 1],
                                   acc_l[:, 2:4], ALU.mult, ALU.add)
    for r in range(NCORE):
        y.wait_ge(b.sV.h, b.sV.n)
        i = y.dma_start(out=SI[:, :], in_=AGG[r * 128:(r + 1) * 128, 0:512])
        b.sLD.inc(i, 16)
        i = y.dma_start(out=LPi[:, 0:2], in_=AGG[r * 128:(r + 1) * 128, 512:514])
        b.sLD.inc(i, 16)
        v.wait_ge(b.sLD.h, b.sLD.n)
        i = v.tensor_tensor(acc_l[:, 0:2], acc_l[:, 0:2], LPi[:, 0:2], ALU.add)
        i = v.tensor_sub(LPi[:, 0:2], acc_l[:, 2:4], acc_l[:, 0:2])
        i = v.tensor_scalar_min(LPi[:, 0:2], LPi[:, 0:2], 0.0)
        s.wait_ge(b.sV.h, b.sV.n)
        i = s.activation(LPi[:, 0:2], LPi[:, 0:2], AF.Exp)
        v.wait_ge(b.sS.h, b.sS.n)
        i = v.tensor_mul(LPi[:, 0:2], LPi[:, 0:2], selcs[:, r:r + 1].broadcast_to([128, 2]))
        for kc in range(KC):
            i = v.scalar_tensor_tensor(CARRY[:, kc * 256:(kc + 1) * 256],
                                       SI[:, kc * 256:(kc + 1) * 256], LPi[:, kc:kc + 1],
                                       CARRY[:, kc * 256:(kc + 1) * 256], ALU.mult, ALU.add)
    for kc in range(KC):
        i = v.tensor_copy(CARRYB[:, kc * 256:(kc + 1) * 256], CARRY[:, kc * 256:(kc + 1) * 256])
    for c in range(NCH):
        co = c * CH
        for kc in range(KC):
            i = v.tensor_scalar_mul(QHA[:, kc * 128:kc * 128 + CH],
                                    QL[:, kc * TLOC + co:kc * TLOC + co + CH],
                                    DCOL[:, kc * NCH + c:kc * NCH + c + 1])
        t.wait_ge(b.sV.h, b.sV.n)
        for vc in range(KC):
            pc2 = b.PS[2 + vc][:, 0:128]
            t.wait_ge(b.sV.h, b.sV.n)
            for kc in range(KC):
                i = t.matmul(pc2, CARRYB[:, kc * 256 + vc * 128:kc * 256 + (vc + 1) * 128],
                             QHA[:, kc * 128:kc * 128 + CH],
                             start=(kc == 0), stop=(kc == KC - 1))
            b.sT.inc(i)
            v.wait_ge(b.sT.h, b.sT.n)
            osl = OUTP[:, vc * TLOC + co:vc * TLOC + co + CH]
            i = v.tensor_tensor(osl, osl, pc2, ALU.add)
    for vc in range(KC):
        i = v.tensor_copy(OUTB[:, vc * TLOC:(vc + 1) * TLOC], OUTP[:, vc * TLOC:(vc + 1) * TLOC])
    b.sT.wait(g); b.sS.wait(g); b.sV.wait(g)
    load_weight_slab(b, b.WW, Wo, KC, D)
    b.sGD.wait(t)
    oevac = b.SMALL[:, 1024:1024 + TS]
    for f in range(DCH):
        for si in range(NSL):
            ps = b.PS[6 if (f * NSL + si) % 2 == 0 else 7][:, 0:TS]
            t.wait_ge(b.sS.h, b.sS.n)
            t.wait_ge(b.sV.h, b.sV.n)
            for k in range(KC):
                i = t.matmul(ps, b.WW[:, k * D + f * 128:k * D + f * 128 + 128],
                             OUTB[:, k * TLOC + si * TS:k * TLOC + (si + 1) * TS],
                             start=(k == 0), stop=(k == KC - 1))
            b.sT.inc(i)
            s.wait_ge(b.sT.h, b.sT.n)
            s.wait_ge(b.sLD.h, b.sLD.n)
            i = s.activation(oevac, ps, AF.Identity, bias=bqs[:, 6 + f:7 + f])
            y.wait_ge(b.sS.h, b.sS.n)
            i = y.dma_start(out=ATTH[f * 128:(f + 1) * 128, si * TS:(si + 1) * TS], in_=oevac)
            b.sLD.inc(i, 16)

    b.phase_barrier()
    XGT = b.P1[:, 0:8 * TLOC]
    HTQ = b.P1[:, 8 * TLOC:16 * TLOC]
    W1S = b.P1[:, 16 * TLOC:16 * TLOC + 2 * 1024]
    W2S = b.P1[:, 16 * TLOC + 2 * 1024:16 * TLOC + 4 * 1024]
    OEV16 = b.SMALL[:, 1024:1280].bitcast(BF16)
    OEF = b.dram("l0oef", [D, TLOC], BF16)
    OEPUB = b.dram("l0oepub", [TLOC, D], BF16)
    OALL = b.dram("l0oall", [T, D], BF16, shared=True)
    XGrow = b.sb("l0xgrow", [128, D], BF16)
    myslot = b.sb("l0myslot", [128, 16], F32)
    myslot_u = b.sb("l0myslotu", [128, 16], U32)
    b1sb = b.sb("l0b1sb", [128, FH // 128], F32)
    b2sb = b.sb("l0b2sb", [128, DCH], F32)
    i = g.dma_start(out=b1sb[:, :], in_=B1E[:, :]); b.sGD.inc(i, 16)
    i = g.dma_start(out=b2sb[:, :], in_=B2E[:, :]); b.sGD.inc(i, 16)
    g.wait_ge(b.sG.h, b.sG.n)
    for tt in range(NTT):
        i = g.indirect_dma_start(
            myslot[:, tt:tt + 1], None, SLOTSUM[:, :],
            IndirectOffsetOnAxis(ap=myrows[:, tt:tt + 1], axis=0))
        b.sGD.inc(i, 16)
    v.wait_ge(b.sGD.h, b.sGD.n)
    i = v.tensor_copy(myslot_u[:, :], myslot[:, :])
    g.wait_ge(b.sV.h, b.sV.n)
    for tt in range(NTT):
        i = g.indirect_dma_start(
            XGrow[:, :], None, XNALL[:, :],
            IndirectOffsetOnAxis(ap=myslot_u[:, tt:tt + 1], axis=0))
        b.sGD.inc(i, 16)
        t.wait_ge(b.sGD.h, b.sGD.n)
        t.wait_ge(b.sV.h, b.sV.n)
        for kchunk in range(DCH):
            pX = b.PS[0].bitcast(BF16)[:, 0:128]
            t.wait_ge(b.sV.h, b.sV.n)
            i = t.transpose(pX, XGrow[:, kchunk * 128:(kchunk + 1) * 128], identb[:, :])
            b.sT.inc(i)
            v.wait_ge(b.sT.h, b.sT.n)
            i = v.tensor_copy(XGT[:, kchunk * TLOC + tt * 128:kchunk * TLOC + (tt + 1) * 128], pX)
        g.wait_ge(b.sV.h, b.sV.n)
    for q in range(NSL):
        for f in range(FH // 128):
            wbuf = W1S[:, (f % 2) * 1024:(f % 2) * 1024 + 1024]
            if not (q == 0 and f < 2):
                g.wait_ge(b.sT.h, b.sT.n)
            i = g.dma_start(out=wbuf.rearrange("p (k c) -> p k c", c=128),
                            in_=W1E[(f * DCH) * 128:(f * DCH + DCH) * 128, :].rearrange(
                                "(k p) c -> p k c", p=128))
            b.sGD.inc(i, 16)
            ps = b.PS[0 if f % 2 == 0 else 1][:, 0:TS]
            t.wait_ge(b.sGD.h, b.sGD.n)
            t.wait_ge(b.sS.h, b.sS.n)
            t.wait_ge(b.sV.h, b.sV.n)
            for k in range(DCH):
                i = t.matmul(ps, wbuf[:, k * 128:(k + 1) * 128],
                             XGT[:, k * TLOC + q * TS:k * TLOC + (q + 1) * TS],
                             start=(k == 0), stop=(k == DCH - 1))
            b.sT.inc(i)
            s.wait_ge(b.sT.h, b.sT.n)
            x0 = sc32[:, 0:TS]
            t1 = sc32[:, TS:2 * TS]
            i = s.activation(x0, ps, AF.Identity, bias=b1sb[:, f:f + 1])
            i = s.activation(t1, x0, AF.Square)
            v.wait_ge(b.sS.h, b.sS.n)
            cgel = 2.0 * math.sqrt(2.0 / math.pi)
            i = v.tensor_scalar(t1, t1, 0.044715 * cgel, cgel, ALU.mult, ALU.add)
            i = v.tensor_mul(t1, t1, x0)
            s.wait_ge(b.sV.h, b.sV.n)
            i = s.activation(t1, t1, AF.Sigmoid)
            v.wait_ge(b.sS.h, b.sS.n)
            i = v.tensor_mul(HTQ[:, f * TS:(f + 1) * TS], t1, x0)
        t.wait_ge(b.sV.h, b.sV.n)
        for f in range(FH // 128):
            wbuf = W2S[:, (f % 2) * 1024:(f % 2) * 1024 + 1024]
            if f >= 2:
                g.wait_ge(b.sT.h, b.sT.n)
            i = g.dma_start(out=wbuf, in_=W2E[f * 128:(f + 1) * 128, :])
            b.sGD.inc(i, 16)
            t.wait_ge(b.sGD.h, b.sGD.n)
            for dm in range(DCH):
                i = t.matmul(b.PS[dm][:, 0:TS], wbuf[:, dm * 128:(dm + 1) * 128],
                             HTQ[:, f * TS:(f + 1) * TS],
                             start=(f == 0), stop=(f == FH // 128 - 1))
            b.sT.inc(i)
        s.wait_ge(b.sT.h, b.sT.n)
        for dm in range(DCH):
            s.wait_ge(b.sLD.h, b.sLD.n)
            i = s.activation(OEV16, b.PS[dm][:, 0:TS], AF.Identity, bias=b2sb[:, dm:dm + 1])
            y.wait_ge(b.sS.h, b.sS.n)
            i = y.dma_start(out=OEF[dm * 128:(dm + 1) * 128, q * TS:(q + 1) * TS], in_=OEV16)
            b.sLD.inc(i, 16)
        t.wait_ge(b.sS.h, b.sS.n)
        t.wait_ge(b.sLD.h, b.sLD.n)
    y.wait_ge(b.sLD.h, b.sLD.n)
    for tt in range(NTT):
        i = y.dma_start(out=XT1[:, :], in_=OEF[:, tt * 128:(tt + 1) * 128], transpose=True)
        b.sLD.inc(i, 16)
        y.wait_ge(b.sLD.h, b.sLD.n)
        i = y.dma_start(out=OEPUB[tt * 128:(tt + 1) * 128, :], in_=XT1[:, :])
        b.sLD.inc(i, 16)
        y.wait_ge(b.sLD.h, b.sLD.n)
    b.sLD.wait(g)
    b.allgather(OEPUB[:, :], OALL[:, :])

    g.wait_ge(b.sG.h, b.sG.n)
    FFNTM = b.dram("l0ffntm", [TLOC, D], BF16)
    fi_all = b.sb("l0fiall", [128, 2], U32)
    gbuf = b.sb("l0gbuf", [128, 2 * D], BF16)
    accum = b.sb("l0accum", [128, D], BF16)
    for tt in range(NTT):
        v.wait_ge(b.sGD.h, b.sGD.n)
        i = v.tensor_scalar_min(RS[:, 24:25], FISLAB[:, tt * 4:tt * 4 + 1], float(T - 1))
        i = v.tensor_copy(fi_all[:, 0:1], RS[:, 24:25])
        i = v.tensor_scalar_min(RS[:, 24:25], FISLAB[:, tt * 4 + 1:tt * 4 + 2], float(T - 1))
        i = v.tensor_copy(fi_all[:, 1:2], RS[:, 24:25])
        g.wait_ge(b.sV.h, b.sV.n)
        g.wait_ge(b.sLD.h, b.sLD.n)
        for kk in range(2):
            i = g.indirect_dma_start(
                gbuf[:, kk * D:(kk + 1) * D], None, OALL[:, :],
                IndirectOffsetOnAxis(ap=fi_all[:, kk:kk + 1], axis=0))
            b.sGD.inc(i, 16)
        v.wait_ge(b.sGD.h, b.sGD.n)
        wsl = RW[:, tt * 8:(tt + 1) * 8]
        i = v.tensor_mul(RS[:, 0:1], wsl[:, 2:3], wsl[:, 6:7])
        i = v.tensor_mul(RS[:, 1:2], wsl[:, 3:4], wsl[:, 7:8])
        i = v.tensor_scalar_mul(accum[:, :], gbuf[:, 0:D], RS[:, 0:1])
        i = v.scalar_tensor_tensor(accum[:, :], gbuf[:, D:2 * D], RS[:, 1:2],
                                   accum[:, :], ALU.mult, ALU.add)
        y.wait_ge(b.sV.h, b.sV.n)
        i = y.dma_start(out=FFNTM[tt * 128:(tt + 1) * 128, :], in_=accum[:, :])
        b.sLD.inc(i, 16)
    b.sLD.wait(b.ye)
    b.phase_barrier()
    aa_ = b.GF[:, 0:2048]
    bb_ = b.GF[:, 2048:4096]
    ff_ = b.SMALL[:, 0:1024].bitcast(BF16)
    for k in range(DCH):
        if k >= 1:
            y.wait_ge(b.sV.h, b.sV.n)
        i = y.dma_start(out=aa_[:, :], in_=XIN[k * 128:(k + 1) * 128, :])
        b.sLD.inc(i, 16)
        i = y.dma_start(out=bb_[:, :], in_=ATTH[k * 128:(k + 1) * 128, :])
        b.sLD.inc(i, 16)
        i = y.dma_start(out=ff_[:, :], in_=FFNTM[:, k * 128:(k + 1) * 128], transpose=True)
        b.sLD.inc(i, 16)
        v.wait_ge(b.sLD.h, b.sLD.n)
        i = v.tensor_add(aa_[:, :], aa_[:, :], bb_[:, :])
        i = v.tensor_tensor(aa_[:, :], aa_[:, :], ff_[:, :], ALU.add)
        y.wait_ge(b.sV.h, b.sV.n)
        i = y.dma_start(out=XOUT[k * 128:(k + 1) * 128, :], in_=aa_[:, :])
        b.sLD.inc(i, 16)
    b.sLD.wait(b.ye)




def emit_l2(b, C):
    v, s, t, g, y = b.vs, b.ss, b.te, b.ge, b.ye
    XIN = b.xin()
    XOUT = b.out("XOUT", [D, TLOC], F32)
    DEV = bool(os.environ.get("KDEV"))
    ATTH = b.out("ATTH", [D, TLOC], F32) if DEV else b.dram("l2att", [D, TLOC], F32)
    FFNH = b.out("FFNH", [D, TLOC], F32) if DEV else b.dram("l2ffn", [D, TLOC], F32)
    ones = C["ones128"]
    ident = C["ident"]
    identb = C["identb"]
    NH = 32
    CONVD = 2304
    NCC = CONVD // 128
    NCHK = T // 128

    WTH = b.weight("L2_attn_in_proj_w", [1024, 4384], transform=lambda w: w.T)
    def cwprov(c, inputs, state):
        w = np.asarray(inputs["L2_attn_conv_w"]).reshape(CONVD, 4)[:, ::-1]
        return np.ascontiguousarray(w.reshape(NCC, 128, 4).transpose(1, 0, 2).reshape(128, NCC * 4), np.float32)
    CONVW2 = b.inp("w_l2convw", [128, NCC * 4], F32, cwprov)
    CONVB2 = b.inp("w_l2convb", [128, NCC], F32, lambda c, inputs, state:
                   np.ascontiguousarray(np.asarray(inputs["L2_attn_conv_b"]).reshape(NCC, 128).T, np.float32))
    DTB4 = b.inp("w_l2dtb4", [128, 4], F32, lambda c, inputs, state:
                 np.ascontiguousarray(np.broadcast_to(np.asarray(inputs["L2_attn_dt_bias"])[4 * c:4 * c + 4], (128, 4)), np.float32))
    ALOG4 = b.inp("w_l2alog4", [128, 4], F32, lambda c, inputs, state:
                  np.ascontiguousarray(np.broadcast_to(np.asarray(inputs["L2_attn_A_log"])[4 * c:4 * c + 4], (128, 4)), np.float32))
    DREP4 = b.inp("w_l2drep4", [128, 4], F32, lambda c, inputs, state:
                  np.ascontiguousarray(np.broadcast_to(np.asarray(inputs["L2_attn_D"])[4 * c:4 * c + 4], (128, 4)), np.float32))
    NRMW = b.inp("w_l2nrmw", [128, 2048], F32, lambda c, inputs, state:
                 np.ascontiguousarray(np.broadcast_to(np.asarray(inputs["L2_attn_norm_w"]), (128, 2048)), np.float32))
    WOTH = b.weight("L2_attn_out_proj_w", [2048, 1024], transform=lambda w: w.T)
    def selh4(c):
        m = np.zeros((32, 4), np.float32)
        for j in range(4):
            m[4 * c + j, j] = 1.0
        return m
    SELH4 = b.const("l2selh4", selh4)
    MASKUT = b.const("l2maskut", lambda c: np.triu(np.ones((128, 128), np.float32)))
    def mbdf(c):
        m = np.zeros((128, 128), np.float32)
        m[0:64, 0:64] = np.triu(np.ones((64, 64)))
        m[64:128, 64:128] = np.triu(np.ones((64, 64)))
        return m
    MBD = b.const("l2mbd", mbdf)
    def mxf(c):
        m = np.zeros((128, 128), np.float32)
        m[0:64, 64:128] = 1.0
        return m
    MX = b.const("l2mx", mxf)
    def ones2f(c):
        m = np.zeros((128, 2), np.float32)
        m[0:64, 0] = 1.0
        m[64:128, 1] = 1.0
        return m
    ONES2 = b.const("l2ones2", ones2f)
    BSEL = b.const("l2bsel", lambda c: np.ascontiguousarray(
        np.stack([np.where(np.arange(128) < 64, 1.0, 0.0),
                  np.where(np.arange(128) >= 64, 1.0, 0.0)]).astype(np.float32)))
    BROW1 = b.const("l2brow1", lambda c: np.ascontiguousarray(
        np.stack([np.zeros(128), np.ones(128)]).astype(np.float32)))
    MMIDT = b.const("l2mmidt", lambda c: np.array([[0.5, 1.0], [0.0, 0.5]], np.float32))
    H16IDX = b.const("l2h16idx", lambda c: (
        (max(c - 1, 0) * 16 + np.arange(16, dtype=np.uint32)).reshape(16, 1)))
    HMASK = b.const("l2hmask", lambda c: np.full((16, 1), 0.0 if c == 0 else 1.0, np.float32))
    XGIDX = b.const("l2xgidx", lambda c: np.ascontiguousarray(
        ((np.arange(NCHK, dtype=np.uint32)[None, :, None] * 128
          + np.arange(128, dtype=np.uint32)[:, None, None]) * 16
         + 2 * np.uint32(c) + np.arange(2, dtype=np.uint32)[None, None, :]).reshape(128, NCHK * 2)))
    EPIDX = b.const("l2epidx", lambda c: np.ascontiguousarray(
        (np.arange(8, dtype=np.uint32)[None, :, None] * np.uint32(T)
         + np.uint32(c) * np.uint32(TLOC)
         + np.arange(16, dtype=np.uint32)[None, None, :] * 128
         + np.arange(128, dtype=np.uint32)[:, None, None]).reshape(128, 128)))

    emit_ln(b, XIN, b.XN, C)
    b.phase_barrier()

    HALOPUB = b.dram("l2halopub", [16, 1024], BF16)
    HALOALL = b.dram("l2haloall", [128, 1024], BF16, shared=True)
    XH16 = b.dram("l2xh16", [16, 1024], BF16)
    MYLS = b.sb("l2myls", [16, 1024], BF16)
    HTM = b.sb("l2htm", [16, 1024], BF16)
    XHF = b.sb("l2xhf", [128, 128], BF16)
    hms = b.sb("l2hms", [16, 1], F32)
    h16s = b.sb("l2h16s", [16, 1], U32)
    i = g.dma_start(out=hms[:, :], in_=HMASK[:, :]); b.sGD.inc(i, 16)
    i = g.dma_start(out=h16s[:, :], in_=H16IDX[:, :]); b.sGD.inc(i, 16)
    for k in range(DCH):
        ps = b.PS[5][0:16, 0:128]
        t.wait_ge(b.sS.h, b.sS.n)
        i = t.matmul(ps, b.XN[:, (k + 1) * TLOC - 16:(k + 1) * TLOC], identb[:, :],
                     start=True, stop=True)
        b.sT.inc(i)
        s.wait_ge(b.sT.h, b.sT.n)
        s.wait_ge(b.sLD.h, b.sLD.n)
        i = s.activation(MYLS[:, k * 128:(k + 1) * 128], ps, AF.Copy)
    y.wait_ge(b.sS.h, b.sS.n)
    i = y.dma_start(out=HALOPUB[:, :], in_=MYLS[:, :]); b.sLD.inc(i, 16)
    b.sLD.wait(g)
    b.allgather(HALOPUB[:, :], HALOALL[:, :])
    g.wait_ge(b.sGD.h, b.sGD.n)
    g.wait_ge(b.sG.h, b.sG.n)
    i = g.indirect_dma_start(HTM[:, :], None, HALOALL[:, :],
                             IndirectOffsetOnAxis(ap=h16s[:, 0:1], axis=0))
    b.sGD.inc(i, 16)
    v.wait_ge(b.sGD.h, b.sGD.n)
    i = v.tensor_scalar_mul(HTM[:, :], HTM[:, :], hms[:, 0:1])
    y.wait_ge(b.sV.h, b.sV.n)
    i = y.dma_start(out=XH16[:, :], in_=HTM[:, :]); b.sLD.inc(i, 16)
    y.wait_ge(b.sLD.h, b.sLD.n)
    for k in range(DCH):
        i = y.dma_start(out=XHF[:, k * 16:(k + 1) * 16],
                        in_=XH16[:, k * 128:(k + 1) * 128], transpose=True)
        b.sLD.inc(i, 16)

    ZTH = b.dram("l2zth", [TLOC, 2048], BF16)
    zst = b.sb("l2zst", [128, 512], BF16)
    for s4 in range(4):
        b.sT.wait(g); b.sS.wait(g)
        load_weight_slab(b, b.WW, WTH, DCH, 512, col0=s4 * 512)
        b.sGD.wait(t)
        for tb in range(NTT):
            ps = b.PS[tb % 2][:, 0:512]
            t.wait_ge(b.sS.h, b.sS.n)
            for k in range(DCH):
                i = t.matmul(ps, b.XN[:, k * TLOC + tb * 128:k * TLOC + (tb + 1) * 128],
                             b.WW[:, k * 512:(k + 1) * 512], start=(k == 0), stop=(k == DCH - 1))
            b.sT.inc(i)
            s.wait_ge(b.sT.h, b.sT.n)
            s.wait_ge(b.sLD.h, b.sLD.n)
            i = s.activation(zst[:, :], ps, AF.Copy)
            y.wait_ge(b.sS.h, b.sS.n)
            i = y.dma_start(out=ZTH[tb * 128:(tb + 1) * 128, s4 * 512:(s4 + 1) * 512],
                            in_=zst[:, :])
            b.sLD.inc(i, 16)

    XBCF = b.dram("l2xbcf", [CONVD, TLOC], BF16)
    XP = b.P1[:, 0:2064]
    XCC = b.P1[:, 4096:8192].bitcast(F32)
    TMPC = b.P1[:, 8192:12288].bitcast(F32)
    cwb2 = b.sb("l2cwb2", [128, NCC * 4], F32)
    cbb2 = b.sb("l2cbb2", [128, NCC], F32)
    i = g.dma_start(out=cwb2[:, :], in_=CONVW2[:, :]); b.sGD.inc(i, 16)
    i = g.dma_start(out=cbb2[:, :], in_=CONVB2[:, :]); b.sGD.inc(i, 16)
    for f in range(NCC):
        b.sT.wait(g); b.sS.wait(g)
        load_weight_slab(b, b.WW, WTH, DCH, 128, col0=2048 + f * 128)
        b.sGD.wait(t)
        ph = b.PS[2][:, 0:16]
        t.wait_ge(b.sV.h, b.sV.n)
        t.wait_ge(b.sLD.h, b.sLD.n)
        for k in range(DCH):
            i = t.matmul(ph, b.WW[:, k * 128:(k + 1) * 128], XHF[:, k * 16:(k + 1) * 16],
                         start=(k == 0), stop=(k == DCH - 1))
        b.sT.inc(i)
        s.wait_ge(b.sT.h, b.sT.n)
        s.wait_ge(b.sV.h, b.sV.n)
        s.wait_ge(b.sLD.h, b.sLD.n)
        i = s.activation(XP[:, 0:16], ph, AF.Copy)
        for si in range(NSL):
            ps = b.PS[si % 2][:, 0:512]
            t.wait_ge(b.sS.h, b.sS.n)
            for k in range(DCH):
                i = t.matmul(ps, b.WW[:, k * 128:(k + 1) * 128],
                             b.XN[:, k * TLOC + si * TS:k * TLOC + (si + 1) * TS],
                             start=(k == 0), stop=(k == DCH - 1))
            b.sT.inc(i)
            s.wait_ge(b.sT.h, b.sT.n)
            i = s.activation(XP[:, 16 + si * TS:16 + (si + 1) * TS], ps, AF.Copy)
        v.wait_ge(b.sS.h, b.sS.n)
        v.wait_ge(b.sGD.h, b.sGD.n)
        i = v.tensor_scalar_mul(XCC[:, :], XP[:, 16:16 + TLOC], cwb2[:, 4 * f:4 * f + 1])
        for j in range(1, 4):
            i = v.tensor_scalar_mul(TMPC[:, :], XP[:, 16 - j:16 - j + TLOC],
                                    cwb2[:, 4 * f + j:4 * f + j + 1])
            i = v.tensor_add(XCC[:, :], XCC[:, :], TMPC[:, :])
        s.wait_ge(b.sV.h, b.sV.n)
        s.wait_ge(b.sLD.h, b.sLD.n)
        i = s.activation(XP[:, 16:16 + TLOC], XCC[:, :], AF.Silu, bias=cbb2[:, f:f + 1])
        y.wait_ge(b.sS.h, b.sS.n)
        i = y.dma_start(out=XBCF[f * 128:(f + 1) * 128, :], in_=XP[:, 16:16 + TLOC])
        b.sLD.inc(i, 16)

    XPUBX = b.dram("l2xpubx", [TLOC, 2048], BF16)
    XPUBD = b.dram("l2xpubd", [TLOC, 288], BF16)
    XALLX = b.dram("l2xallx", [T, 2048], BF16, shared=True)
    XALLD = b.dram("l2xalld", [T, 288], BF16, shared=True)
    PUBT = b.P1[:, 12288:14336]
    PUBD = b.P1[:, 14336:14624]
    dts = b.sb("l2dts", [128, 8 * 32], BF16)
    load_weight_slab(b, dts, WTH, DCH, 32, col0=4352)
    b.sGD.wait(t)
    b.sLD.wait(y)
    for tb in range(NTT):
        y.wait_ge(b.sLD.h, b.sLD.n)
        for f in range(16):
            i = y.dma_start(out=PUBT[:, f * 128:(f + 1) * 128],
                            in_=XBCF[f * 128:(f + 1) * 128, tb * 128:(tb + 1) * 128],
                            transpose=True)
            b.sLD.inc(i, 16)
        for f in range(2):
            i = y.dma_start(out=PUBD[:, f * 128:(f + 1) * 128],
                            in_=XBCF[(16 + f) * 128:(17 + f) * 128, tb * 128:(tb + 1) * 128],
                            transpose=True)
            b.sLD.inc(i, 16)
        ps = b.PS[4][:, 0:32]
        t.wait_ge(b.sS.h, b.sS.n)
        for k in range(DCH):
            i = t.matmul(ps, b.XN[:, k * TLOC + tb * 128:k * TLOC + (tb + 1) * 128],
                         dts[:, k * 32:(k + 1) * 32], start=(k == 0), stop=(k == DCH - 1))
        b.sT.inc(i)
        s.wait_ge(b.sT.h, b.sT.n)
        s.wait_ge(b.sLD.h, b.sLD.n)
        i = s.activation(PUBD[:, 256:288], ps, AF.Copy)
        y.wait_ge(b.sS.h, b.sS.n)
        i = y.dma_start(out=XPUBX[tb * 128:(tb + 1) * 128, :], in_=PUBT[:, 0:2048])
        b.sLD.inc(i, 16)
        i = y.dma_start(out=XPUBD[tb * 128:(tb + 1) * 128, :], in_=PUBD[:, 0:288])
        b.sLD.inc(i, 16)
    b.sLD.wait(g)
    b.allgather(XPUBX[:, :], XALLX[:, :])
    b.allgather(XPUBD[:, :], XALLD[:, :])

    YPUBH = b.dram("l2ypubh", [T, 256], BF16)
    YALLH = b.dram("l2yallh", [8 * T, 256], BF16, shared=True)
    XALLXv = XALLX[:, :].rearrange("a (b c) -> (a b) c", c=128)
    xgix = b.sb("l2xgix", [128, NCHK * 2], U32)
    i = g.dma_start(out=xgix[:, :], in_=XGIDX[:, :]); b.sGD.inc(i, 16)
    selh4s = b.sb("l2selh4s", [32, 4], BF16)
    i = g.dma_start(out=selh4s[:, :], in_=SELH4[:, :]); b.sGD.inc(i, 16)
    mutf = b.sb("l2mutf", [128, 128], F32)
    i = g.dma_start(out=mutf[:, :], in_=MASKUT[:, :]); b.sGD.inc(i, 16)
    mbdf_ = b.sb("l2mbdf", [128, 128], F32)
    i = g.dma_start(out=mbdf_[:, :], in_=MBD[:, :]); b.sGD.inc(i, 16)
    mxf_ = b.sb("l2mxf", [128, 128], F32)
    i = g.dma_start(out=mxf_[:, :], in_=MX[:, :]); b.sGD.inc(i, 16)
    ones2s = b.sb("l2ones2s", [128, 2], F32)
    i = g.dma_start(out=ones2s[:, :], in_=ONES2[:, :]); b.sGD.inc(i, 16)
    bsels = b.sb("l2bsels", [2, 128], F32)
    i = g.dma_start(out=bsels[:, :], in_=BSEL[:, :]); b.sGD.inc(i, 16)
    brow1s = b.sb("l2brow1s", [2, 128], F32)
    i = g.dma_start(out=brow1s[:, :], in_=BROW1[:, :]); b.sGD.inc(i, 16)
    mmidts = b.sb("l2mmidts", [2, 2], F32)
    i = g.dma_start(out=mmidts[:, :], in_=MMIDT[:, :]); b.sGD.inc(i, 16)
    dtb4s = b.sb("l2dtb4s", [128, 4], F32)
    i = g.dma_start(out=dtb4s[:, :], in_=DTB4[:, :]); b.sGD.inc(i, 16)
    a4s = b.sb("l2a4s", [128, 4], F32)
    i = g.dma_start(out=a4s[:, :], in_=ALOG4[:, :]); b.sGD.inc(i, 16)
    dr4s = b.sb("l2dr4s", [128, 4], F32)
    i = g.dma_start(out=dr4s[:, :], in_=DREP4[:, :]); b.sGD.inc(i, 16)
    s.wait_ge(b.sGD.h, b.sGD.n)
    i = s.activation(a4s[:, :], a4s[:, :], AF.Exp)
    v.wait_ge(b.sS.h, b.sS.n)
    i = v.tensor_scalar_mul(a4s[:, :], a4s[:, :], -1.0)

    XGX = b.sb("l2xgx", [128, 256], BF16)
    BCD = b.sb("l2bcd", [128, 288], BF16)
    BTF = b.sb("l2btf", [128, 128], BF16)
    CTF = b.sb("l2ctf", [128, 128], BF16)
    Gm = b.sb("l2gm", [128, 128], BF16)
    DT32 = b.sb("l2dt32", [32, 128], BF16)
    D4T = b.sb("l2d4t", [4, 128], BF16)
    DTW = b.sb("l2dtw", [128, 64], F32)
    ce2sb = b.sb("l2ce2sb", [2, 4], F32)
    mid2sb = b.sb("l2mid2sb", [2, 4], F32)
    GmX = b.sb("l2gmx", [128, 128], BF16)
    XWX = b.sb("l2xwx", [128, 64], BF16)
    SH = b.sb("l2sh", [128, 4 * 64], BF16)
    XW = b.sb("l2xw", [128, 64], BF16)
    XW2 = b.sb("l2xw2", [128, 64], BF16)
    YCB = b.sb("l2ycb", [128, 256], BF16)
    TMP64 = b.sb("l2tmp64", [128, 64], F32)
    ce1 = b.sb("l2ce1", [1, 4], F32)
    i = v.memset(SH[:, :], 0.0)
    g.wait_ge(b.sGD.h, b.sGD.n)
    g.wait_ge(b.sG.h, b.sG.n)
    y.wait_ge(b.sG.h, b.sG.n)
    for c in range(NCHK):
        r0 = c * 128
        g.wait_ge(b.sT.h, b.sT.n); g.wait_ge(b.sS.h, b.sS.n)
        g.wait_ge(b.sV.h, b.sV.n)
        for hf in range(2):
            i = g.indirect_dma_start(XGX[:, hf * 128:(hf + 1) * 128], None, XALLXv,
                                     IndirectOffsetOnAxis(ap=xgix[:, 2 * c + hf:2 * c + hf + 1], axis=0))
            b.sGD.inc(i, 16)
        y.wait_ge(b.sT.h, b.sT.n); y.wait_ge(b.sV.h, b.sV.n)
        i = y.dma_start(out=BCD[:, :], in_=XALLD[r0:r0 + 128, :]); b.sLD.inc(i, 16)
        i = y.dma_start(out=BTF[:, :], in_=XALLD[r0:r0 + 128, 0:128], transpose=True)
        b.sLD.inc(i, 16)
        i = y.dma_start(out=CTF[:, :], in_=XALLD[r0:r0 + 128, 128:256], transpose=True)
        b.sLD.inc(i, 16)
        pd = b.PS[4][0:32, 0:128]
        t.wait_ge(b.sLD.h, b.sLD.n)
        t.wait_ge(b.sV.h, b.sV.n)
        t.wait_ge(b.sS.h, b.sS.n)
        i = t.matmul(pd, BCD[:, 256:288], identb[:, :], start=True, stop=True)
        b.sT.inc(i)
        s.wait_ge(b.sT.h, b.sT.n)
        i = s.activation(DT32[:, :], pd, AF.Copy)
        pd2 = b.PS[4][0:4, 0:128]
        t.wait_ge(b.sS.h, b.sS.n)
        i = t.matmul(pd2, selh4s[:, :], DT32[:, :], start=True, stop=True)
        b.sT.inc(i)
        s.wait_ge(b.sT.h, b.sT.n)
        i = s.activation(D4T[:, :], pd2, AF.Copy)
        pd3 = b.PS[4][:, 0:4]
        t.wait_ge(b.sS.h, b.sS.n)
        i = t.matmul(pd3, D4T[:, :], identb[0:4, 0:4], start=True, stop=True)
        b.sT.inc(i)
        v.wait_ge(b.sT.h, b.sT.n)
        dmy = DTW[:, 0:4]; dt4 = DTW[:, 4:8]; aa4 = DTW[:, 8:12]
        cum4 = DTW[:, 12:16]; ceb4 = DTW[:, 16:20]; midt = DTW[:, 20:24]
        w1t = DTW[:, 24:28]; w2t = DTW[:, 28:32]; ece4 = DTW[:, 32:36]
        epd = DTW[:, 36:40]; epo = DTW[:, 40:44]; scr = DTW[:, 44:48]
        m1t = DTW[:, 48:52]; wxt = DTW[:, 52:56]
        i = v.tensor_tensor(dmy, pd3, dtb4s[:, :], ALU.add)
        s.wait_ge(b.sV.h, b.sV.n)
        i = s.activation(scr, dmy, AF.Abs)
        i = s.activation(scr, scr, AF.Exp, scale=-1.0)
        i = s.activation(scr, scr, AF.Ln, bias=b.ccol(1.0))
        i = s.activation(dt4, dmy, AF.Relu)
        v.wait_ge(b.sS.h, b.sS.n)
        i = v.tensor_add(dt4, dt4, scr)
        i = v.tensor_tensor(aa4, dt4, a4s[:, :], ALU.mult)
        pc = b.PS[5][:, 0:4]
        t.wait_ge(b.sV.h, b.sV.n)
        i = t.matmul(pc, mutf[:, :], aa4, start=True, stop=True)
        b.sT.inc(i)
        pe = b.PS[6][0:1, 0:4]
        i = t.matmul(pe, ones[:, 0:1], aa4, start=True, stop=True)
        b.sT.inc(i)
        pc2 = b.PS[7][0:2, 0:4]
        i = t.matmul(pc2, ones2s[:, :], aa4, start=True, stop=True)
        b.sT.inc(i)
        v.wait_ge(b.sT.h, b.sT.n)
        i = v.tensor_copy(cum4, pc)
        i = v.tensor_copy(ce1[:, :], pe)
        i = v.tensor_copy(ce2sb[:, :], pc2)
        pb = b.PS[5][:, 0:4]
        t.wait_ge(b.sV.h, b.sV.n)
        i = t.matmul(pb, ones[0:1, 0:128], ce1[:, :], start=True, stop=True)
        b.sT.inc(i)
        pm2 = b.PS[6][0:2, 0:4]
        i = t.matmul(pm2, mmidts[:, :], ce2sb[:, :], start=True, stop=True)
        b.sT.inc(i)
        v.wait_ge(b.sT.h, b.sT.n)
        i = v.tensor_copy(ceb4, pb)
        i = v.tensor_copy(mid2sb[:, :], pm2)
        pmt = b.PS[5][:, 0:4]
        t.wait_ge(b.sV.h, b.sV.n)
        i = t.matmul(pmt, bsels[:, :], mid2sb[:, :], start=True, stop=True)
        b.sT.inc(i)
        pm1 = b.PS[6][:, 0:4]
        i = t.matmul(pm1, brow1s[:, :], mid2sb[:, :], start=True, stop=True)
        b.sT.inc(i)
        v.wait_ge(b.sT.h, b.sT.n)
        i = v.tensor_copy(midt, pmt)
        i = v.tensor_copy(m1t, pm1)
        i = v.tensor_sub(scr, midt, cum4)
        s.wait_ge(b.sV.h, b.sV.n)
        i = s.activation(scr, scr, AF.Exp)
        v.wait_ge(b.sS.h, b.sS.n)
        i = v.tensor_tensor(w1t, dt4, scr, ALU.mult)
        i = v.tensor_sub(scr, m1t, cum4)
        s.wait_ge(b.sV.h, b.sV.n)
        i = s.activation(scr, scr, AF.Exp)
        v.wait_ge(b.sS.h, b.sS.n)
        i = v.tensor_tensor(wxt, dt4, scr, ALU.mult)
        i = v.tensor_sub(scr, ceb4, cum4)
        s.wait_ge(b.sV.h, b.sV.n)
        i = s.activation(scr, scr, AF.Exp)
        v.wait_ge(b.sS.h, b.sS.n)
        i = v.tensor_tensor(w2t, dt4, scr, ALU.mult)
        s.wait_ge(b.sV.h, b.sV.n)
        i = s.activation(ece4, ceb4, AF.Exp)
        i = s.activation(epo, cum4, AF.Exp)
        v.wait_ge(b.sS.h, b.sS.n)
        i = v.tensor_sub(scr, cum4, midt)
        s.wait_ge(b.sV.h, b.sV.n)
        i = s.activation(epd, scr, AF.Exp)
        pg = b.PS[0][:, 0:128]
        t.wait_ge(b.sV.h, b.sV.n)
        i = t.matmul(pg, BTF[:, :], CTF[:, :], start=True, stop=True)
        b.sT.inc(i)
        v.wait_ge(b.sT.h, b.sT.n)
        v.wait_ge(b.sS.h, b.sS.n)
        i = v.tensor_tensor(Gm[:, :], pg, mbdf_[:, :], ALU.mult)
        i = v.tensor_tensor(GmX[:, :], pg, mxf_[:, :], ALU.mult)
        for j in range(4):
            xs = XGX[:, j * 64:(j + 1) * 64]
            s.wait_ge(b.sT.h, b.sT.n)
            s.wait_ge(b.sV.h, b.sV.n)
            s.wait_ge(b.sLD.h, b.sLD.n)
            i = s.activation(XW[:, :], xs, AF.Identity, scale=w1t[:, j:j + 1])
            i = s.activation(XWX[:, :], xs, AF.Identity, scale=wxt[:, j:j + 1])
            i = s.activation(XW2[:, :], xs, AF.Identity, scale=w2t[:, j:j + 1])
            pyd = b.PS[1][:, 0:64]
            pyo = b.PS[2][:, 0:64]
            pst = b.PS[3][:, 0:64]
            t.wait_ge(b.sS.h, b.sS.n)
            t.wait_ge(b.sV.h, b.sV.n)
            i = t.matmul(pyd, Gm[:, :], XW[:, :], start=True, stop=False)
            i = t.matmul(pyd, GmX[:, :], XWX[:, :], start=False, stop=True)
            b.sT.inc(i)
            i = t.matmul(pyo, CTF[:, :], SH[:, j * 64:(j + 1) * 64], start=True, stop=True)
            b.sT.inc(i)
            i = t.matmul(pst, BCD[:, 0:128], XW2[:, :], start=True, stop=True)
            b.sT.inc(i)
            s.wait_ge(b.sT.h, b.sT.n)
            i = s.activation(YCB[:, j * 64:(j + 1) * 64], pyd, AF.Identity,
                             scale=epd[:, j:j + 1])
            i = s.activation(TMP64[:, :], pyo, AF.Identity, scale=epo[:, j:j + 1])
            v.wait_ge(b.sS.h, b.sS.n)
            i = v.tensor_tensor(YCB[:, j * 64:(j + 1) * 64],
                                YCB[:, j * 64:(j + 1) * 64], TMP64[:, :], ALU.add)
            i = v.tensor_scalar_mul(SH[:, j * 64:(j + 1) * 64],
                                    SH[:, j * 64:(j + 1) * 64], ece4[0:128, j:j + 1])
            i = v.tensor_tensor(SH[:, j * 64:(j + 1) * 64],
                                SH[:, j * 64:(j + 1) * 64], pst, ALU.add)
            i = v.tensor_scalar_mul(TMP64[:, :], xs, dr4s[:, j:j + 1])
            i = v.tensor_tensor(YCB[:, j * 64:(j + 1) * 64],
                                YCB[:, j * 64:(j + 1) * 64], TMP64[:, :], ALU.add)
        y.wait_ge(b.sV.h, b.sV.n)
        i = y.dma_start(out=YPUBH[r0:r0 + 128, :], in_=YCB[:, :]); b.sLD.inc(i, 16)
    b.sLD.wait(g)
    b.allgather(YPUBH[:, :], YALLH[:, :])
    if DEV:
        YP2 = b.out("YP2", [2048, 256], BF16)
        i = g.dma_start(out=YP2[:, :], in_=YPUBH[0:2048, :]); b.sGD.inc(i, 16)

    b.phase_barrier()
    TH = b.dram("l2th", [TLOC, 2048], BF16)
    epix = b.sb("l2epix", [128, 128], U32)
    i = g.dma_start(out=epix[:, :], in_=EPIDX[:, :]); b.sGD.inc(i, 16)
    g.wait_ge(b.sGD.h, b.sGD.n)
    nwt = b.GF[:, 0:2048]
    i = g.dma_start(out=nwt[:, :], in_=NRMW[:, :]); b.sGD.inc(i, 16)
    YF = b.P1[:, 16384:18432]
    ZSL = b.P1[:, 18432:20480]
    TT = b.P1[:, 20480:24576].bitcast(F32)
    TMPQ = b.P1[:, 24576:28672].bitcast(F32)
    rcol = b.sb("l2rcol", [128, 2], F32)
    tb16 = b.sb("l2tb16", [128, 2048], BF16)
    g.wait_ge(b.sG.h, b.sG.n)
    for tb in range(NTT):
        g.wait_ge(b.sV.h, b.sV.n)
        for j in range(8):
            i = g.indirect_dma_start(YF[:, j * 256:(j + 1) * 256], None, YALLH[:, :],
                                     IndirectOffsetOnAxis(ap=epix[:, j * 16 + tb:j * 16 + tb + 1], axis=0))
            b.sGD.inc(i, 16)
        y.wait_ge(b.sV.h, b.sV.n)
        i = y.dma_start(out=ZSL[:, :], in_=ZTH[tb * 128:(tb + 1) * 128, :])
        b.sLD.inc(i, 16)
        s.wait_ge(b.sLD.h, b.sLD.n)
        s.wait_ge(b.sV.h, b.sV.n)
        i = s.activation(TT[:, :], ZSL[:, :], AF.Silu)
        v.wait_ge(b.sS.h, b.sS.n)
        v.wait_ge(b.sGD.h, b.sGD.n)
        i = v.tensor_tensor(TT[:, :], TT[:, :], YF[:, :], ALU.mult)
        s.wait_ge(b.sV.h, b.sV.n)
        i = s.activation(TMPQ[:, :], TT[:, :], AF.Square)
        v.wait_ge(b.sS.h, b.sS.n)
        i = v.tensor_reduce(rcol[:, 0:1], TMPQ[:, :], axis=AX.X, op=ALU.add)
        i = v.tensor_scalar_mul(rcol[:, 0:1], rcol[:, 0:1], 1.0 / 2048.0)
        s.wait_ge(b.sV.h, b.sV.n)
        i = s.activation(rcol[:, 0:1], rcol[:, 0:1], AF.Sqrt, bias=b.ccol(1e-5))
        v.wait_ge(b.sS.h, b.sS.n)
        i = v.reciprocal(rcol[:, 0:1], rcol[:, 0:1])
        i = v.tensor_scalar_mul(TT[:, :], TT[:, :], rcol[:, 0:1])
        i = v.tensor_tensor(tb16[:, :], TT[:, :], nwt[:, :], ALU.mult)
        y.wait_ge(b.sV.h, b.sV.n)
        i = y.dma_start(out=TH[tb * 128:(tb + 1) * 128, :], in_=tb16[:, :])
        b.sLD.inc(i, 16)

    b.sT.wait(g); b.sS.wait(g); b.sV.wait(g)
    WOTS = b.P1[:, 0:16384]
    load_weight_slab(b, WOTS, WOTH, 16, 1024)
    b.sGD.wait(t)
    TL = b.sb("l2tl", [128, 2 * 512], BF16)
    oevac = b.SMALL[:, 1024:1536]
    b.sLD.wait(y)
    for si in range(NSL):
        for k in range(16):
            y.wait_ge(b.sT.h, b.sT.n)
            i = y.dma_start(out=TL[:, (k % 2) * 512:(k % 2) * 512 + 512],
                            in_=TH[si * 512:(si + 1) * 512, k * 128:(k + 1) * 128],
                            transpose=True)
            b.sLD.inc(i, 16)
            t.wait_ge(b.sLD.h, b.sLD.n)
            if k == 0:
                t.wait_ge(b.sS.h, b.sS.n)
            for f in range(8):
                i = t.matmul(b.PS[f][:, 0:512],
                             WOTS[:, k * 1024 + f * 128:k * 1024 + f * 128 + 128],
                             TL[:, (k % 2) * 512:(k % 2) * 512 + 512],
                             start=(k == 0), stop=(k == 15))
            b.sT.inc(i)
        for f in range(8):
            s.wait_ge(b.sT.h, b.sT.n)
            s.wait_ge(b.sLD.h, b.sLD.n)
            i = s.activation(oevac, b.PS[f][:, 0:512], AF.Copy)
            y.wait_ge(b.sS.h, b.sS.n)
            i = y.dma_start(out=ATTH[f * 128:(f + 1) * 128, si * TS:(si + 1) * TS],
                            in_=oevac)
            b.sLD.inc(i, 16)

    emit_l2_ffn(b, C, FFNH)
    emit_final_combine(b, XIN, XOUT, [ATTH, FFNH])



def emit_ln_sb(b, SRC, DST, C, lg, lb):
    v, s, t = b.vs, b.ss, b.te
    ones = C["ones128"]
    onesb = C["onesb"]
    wf = b.WW[:, 0:8192].bitcast(F32)
    sumx = wf[0:1, 0:TLOC]
    sumsq = wf[0:1, TLOC:2 * TLOC]
    tmpr = b.GF[0:1, 0:TLOC]
    rstdb = b.GF[:, 0:TLOC]
    mrb = b.GF[:, TLOC:2 * TLOC]
    sq = b.SMALL[:, 0:2048]
    t.wait_ge(b.sV.h, b.sV.n)
    t.wait_ge(b.sS.h, b.sS.n)
    for k in range(DCH):
        for si in range(NSL):
            i = t.matmul(b.PS[si][0:1, 0:TS], onesb[:, 0:1],
                         SRC[:, k * TLOC + si * TS:k * TLOC + (si + 1) * TS],
                         start=(k == 0), stop=(k == DCH - 1))
        if k >= 1:
            s.wait_ge(b.sT.h, b.sT.n)
        s.wait_ge(b.sV.h, b.sV.n)
        i = s.activation(sq[:, :], SRC[:, k * TLOC:(k + 1) * TLOC], AF.Square)
        t.wait_ge(b.sS.h, b.sS.n)
        for si in range(NSL):
            i = t.matmul(b.PS[4 + si][0:1, 0:TS], ones[:, 0:1],
                         sq[:, si * TS:(si + 1) * TS],
                         start=(k == 0), stop=(k == DCH - 1))
        b.sT.inc(i)
    v.wait_ge(b.sT.h, b.sT.n)
    for si in range(NSL):
        v.tensor_copy(sumx[:, si * TS:(si + 1) * TS], b.PS[si][0:1, 0:TS])
        i = v.tensor_copy(sumsq[:, si * TS:(si + 1) * TS], b.PS[4 + si][0:1, 0:TS])
    v.tensor_scalar_mul(sumx, sumx, 1.0 / D)
    v.tensor_scalar_mul(sumsq, sumsq, 1.0 / D)
    v.tensor_tensor(tmpr, sumx, sumx, ALU.mult)
    i = v.tensor_sub(sumsq, sumsq, tmpr)
    s.wait_ge(b.sV.h, b.sV.n)
    i = s.activation(sumsq, sumsq, AF.Sqrt, bias=b.ccol(1e-5)[0:1, 0:1])
    v.wait_ge(b.sS.h, b.sS.n)
    i = v.reciprocal(sumsq, sumsq)
    i = v.tensor_mul(sumx, sumx, sumsq)
    t.wait_ge(b.sV.h, b.sV.n)
    for si in range(NSL):
        t.matmul(b.PS[4][:, 0:TS], ones[0:1, 0:128], sumsq[:, si * TS:(si + 1) * TS],
                 start=True, stop=True)
        i = t.matmul(b.PS[5][:, 0:TS], ones[0:1, 0:128], sumx[:, si * TS:(si + 1) * TS],
                     start=True, stop=True)
        b.sT.inc(i)
        v.wait_ge(b.sT.h, b.sT.n)
        v.tensor_copy(rstdb[:, si * TS:(si + 1) * TS], b.PS[4][:, 0:TS])
        i = v.tensor_copy(mrb[:, si * TS:(si + 1) * TS], b.PS[5][:, 0:TS])
        t.wait_ge(b.sV.h, b.sV.n)
    for k in range(DCH):
        dst = DST[:, k * TLOC:(k + 1) * TLOC]
        i = v.scalar_tensor_tensor(dst, SRC[:, k * TLOC:(k + 1) * TLOC], 1.0,
                                   rstdb, ALU.mult, ALU.mult)
        i = v.tensor_sub(dst, dst, mrb)
        i = v.tensor_scalar_mul(dst, dst, lg[:, k:k + 1])
        i = v.tensor_scalar(dst, dst, lb[:, k:k + 1], None, ALU.add)


def emit_l2_ffn(b, C, FFNH):
    v, s, t, g, y = b.vs, b.ss, b.te, b.ge, b.ye
    DEV = bool(os.environ.get("KDEV"))
    ones = C["ones128"]
    ident = C["ident"]
    identb = C["identb"]
    HID2 = 2816

    WQKV = [b.inp(f"w_l2qkv{l}", [1024, 1536], F32, lambda c, inputs, state, l=l:
                  np.ascontiguousarray(np.asarray(inputs["L2_ffn_wqkv"])[l], np.float32))
            for l in range(2)]
    WO2 = [b.inp(f"w_l2wo{l}", [512, 1024], F32, lambda c, inputs, state, l=l:
                 np.ascontiguousarray(np.asarray(inputs["L2_ffn_wo"])[l], np.float32))
           for l in range(2)]
    def w1prov(c, inputs, state, l):
        w = np.asarray(inputs["L2_ffn_w1"])[l]
        out = np.zeros((1024, 2 * HID2), np.float32)
        out[:, 0:2730] = w[:, 0:2730]
        out[:, HID2:HID2 + 2730] = w[:, 2730:5460]
        return out
    W1P = [b.inp(f"w_l2w1p{l}", [1024, 2 * HID2], F32,
                 lambda c, inputs, state, l=l: w1prov(c, inputs, state, l))
           for l in range(2)]
    def b1prov(c, inputs, state, l):
        bb = np.asarray(inputs["L2_ffn_b1"])[l]
        out = np.zeros(2 * HID2, np.float32)
        out[0:2730] = bb[0:2730]
        out[HID2:HID2 + 2730] = bb[2730:5460]
        return np.ascontiguousarray(out.reshape(44, 128).T)
    B1C = [b.inp(f"w_l2b1c{l}", [128, 44], F32,
                 lambda c, inputs, state, l=l: b1prov(c, inputs, state, l))
           for l in range(2)]
    def w2prov(c, inputs, state, l):
        w = np.asarray(inputs["L2_ffn_w2"])[l]
        out = np.zeros((HID2, 1024), np.float32)
        out[0:2730] = w
        return out
    W2P = [b.inp(f"w_l2w2p{l}", [HID2, 1024], F32,
                 lambda c, inputs, state, l=l: w2prov(c, inputs, state, l))
           for l in range(2)]
    B2R2 = [b.inp(f"w_l2b2r{l}", [1, 1024], F32, lambda c, inputs, state, l=l:
                  np.ascontiguousarray(np.asarray(inputs["L2_ffn_b2"])[l].reshape(1, 1024), np.float32))
            for l in range(2)]
    LG2 = [b.inp(f"w_l2lg{l}", [128, 8], F32, lambda c, inputs, state, l=l:
                 np.ascontiguousarray(np.asarray(inputs["L2_ffn_ln_g"])[l].reshape(8, 128).T, np.float32))
           for l in range(2)]
    LB2 = [b.inp(f"w_l2lb{l}", [128, 8], F32, lambda c, inputs, state, l=l:
                 np.ascontiguousarray(np.asarray(inputs["L2_ffn_ln_b"])[l].reshape(8, 128).T, np.float32))
           for l in range(2)]
    QSR = b.inp("w_l2qsr", [128, 4], F32, lambda c, inputs, state:
                np.ascontiguousarray(np.stack(
                    [np.tile(np.asarray(inputs["L2_ffn_q_scale"])[0], 2),
                     np.tile(np.asarray(inputs["L2_ffn_q_scale"])[1], 2),
                     np.tile(np.asarray(inputs["L2_ffn_k_scale"])[0], 2),
                     np.tile(np.asarray(inputs["L2_ffn_k_scale"])[1], 2)], axis=1), np.float32))

    def mqprov(c):
        out = np.full((2, 4, 128, 1024), -1e30, np.float32)
        ck = np.arange(512)
        for var in range(2):
            for qs in range(4):
                tq = qs * 128 + np.arange(128)[:, None]
                prev = np.where(ck[None, :] >= tq, 0.0, -1e30)
                if var == 1:
                    prev = np.full((128, 512), -1e30, np.float32)
                own = np.where(ck[None, :] <= tq, 0.0, -1e30)
                out[var, qs] = np.concatenate([prev, own], axis=1)
        return np.ascontiguousarray(out.transpose(2, 0, 1, 3).reshape(128, 8192))
    MSKQI = b.const("l2mskq", mqprov)
    def mtprov(c):
        out = np.zeros((2, 8, 128, 512), np.float32)
        tq = np.arange(512)
        r = np.arange(128)[:, None]
        for kb in range(8):
            if kb < 4:
                m0 = np.where(tq[None, :] <= kb * 128 + r, 0.0, -1e30)
                out[0, kb] = m0
                out[1, kb] = -1e30
            else:
                m0 = np.where(tq[None, :] >= (kb - 4) * 128 + r, 0.0, -1e30)
                out[0, kb] = m0
                out[1, kb] = m0
        return np.ascontiguousarray(out.transpose(2, 0, 1, 3).reshape(128, 8192))
    MSKTI = b.const("l2mskt", mtprov)

    def tblprov(c, kind):
        pos = (np.arange(2560, dtype=np.float64) + c * 2048 - 512)
        mj = np.arange(64) % 32
        invf = 10000.0 ** (-(2 * mj) / 64.0)
        fr = pos[None, :] * invf[:, None]
        base = (2 * mj + 0.4 * 64) / (1.4 * 64)
        power = (pos - 8192.0) / 256.0
        sc = base[:, None] ** power[None, :]
        if kind == 0:
            tb = np.cos(fr) * sc
        elif kind == 1:
            tb = np.sin(fr) * sc
        elif kind == 2:
            tb = np.cos(fr) / sc
        else:
            tb = np.sin(fr) / sc
        return np.ascontiguousarray(np.tile(tb, (2, 1)).astype(np.float32))
    QCT = b.const("l2qct", lambda c: tblprov(c, 0))
    QST = b.const("l2qst", lambda c: tblprov(c, 1))
    KCT = b.const("l2kct", lambda c: tblprov(c, 2))
    KST = b.const("l2kst", lambda c: tblprov(c, 3))
    def rtprov(c):
        R = np.zeros((128, 128), np.float32)
        for bb in (0, 64):
            for i2 in range(32):
                R[bb + i2 + 32, bb + i2] = -1.0
                R[bb + i2, bb + i2 + 32] = 1.0
        return R
    RTC = b.const("l2rtc", rtprov)
    XLIDXA = b.const("l2xlidxa", lambda c: np.maximum(
        np.arange(2560, dtype=np.int64) + c * 2048 - 512, 0).astype(np.uint32).reshape(20, 128).T.copy())
    XLIDXB = b.const("l2xlidxb", lambda c: np.maximum(
        np.arange(2560, dtype=np.int64) + c * 2048 - 513, 0).astype(np.uint32).reshape(20, 128).T.copy())
    def zm1prov(c):
        out = np.ones((128, 2), np.float32)
        if c == 0:
            out[127, 0] = 0.0
            out[:, 1] = 0.0
        return out
    ZM1 = b.const("l2zm1", zm1prov)
    L1IDX = b.const("l2l1idx", lambda c: np.full((2, 1), max(c - 1, 0), np.uint32))

    XF = b.P1[:, 0:16384]
    XFH = b.dram("l2xfh", [1024, TLOC], BF16)
    XPUB2 = b.dram("l2xpub2", [TLOC, 1024], BF16)
    XALL2 = b.dram("l2xall2", [T, 1024], BF16, shared=True)
    XLA = b.dram("l2xla", [2560, 1024], BF16)
    XLB = b.dram("l2xlb", [2560, 1024], BF16)
    QFH = b.dram("l2qfh", [512, 2048], BF16)
    KFH = b.dram("l2kfh", [512, 2560], BF16)
    VTH = b.dram("l2vth", [2560, 512], BF16)
    PRH = b.dram("l2prh", [HID2, TLOC], BF16)
    L1PUB = b.dram("l2l1pub", [1, 1024], BF16)
    L1ALL = b.dram("l2l1all", [8, 1024], BF16, shared=True)

    xlixa = b.sb("l2xlixa", [128, 20], U32)
    xlixb = b.sb("l2xlixb", [128, 20], U32)
    zm1s = b.sb("l2zm1s", [128, 2], F32)
    l1ixs = b.sb("l2l1ixs", [2, 1], U32)
    qsrs = b.sb("l2qsrs", [128, 4], F32)
    rtf = b.sb("l2rtf", [128, 128], F32)
    GB = b.sb("l2gb", [128, 1024], BF16)
    NMCH = b.sb("l2nmch", [128, 32], F32)
    G1 = b.sb("l2g1", [2, 1024], BF16)
    HCOL = b.sb("l2hcol", [128, 8], F32)
    lgc = [b.sb(f"l2lgc{l}", [128, 8], F32) for l in range(2)]
    lbc = [b.sb(f"l2lbc{l}", [128, 8], F32) for l in range(2)]
    b1cs = b.sb("l2b1cs", [128, 44], F32)

    b.phase_barrier()
    i = g.dma_start(out=xlixa[:, :], in_=XLIDXA[:, :]); b.sGD.inc(i, 16)
    i = g.dma_start(out=xlixb[:, :], in_=XLIDXB[:, :]); b.sGD.inc(i, 16)
    i = g.dma_start(out=zm1s[:, :], in_=ZM1[:, :]); b.sGD.inc(i, 16)
    i = g.dma_start(out=l1ixs[:, :], in_=L1IDX[:, :]); b.sGD.inc(i, 16)
    i = g.dma_start(out=qsrs[:, :], in_=QSR[:, :]); b.sGD.inc(i, 16)
    i = g.dma_start(out=rtf[:, :], in_=RTC[:, :]); b.sGD.inc(i, 16)
    for l in range(2):
        i = g.dma_start(out=lgc[l][:, :], in_=LG2[l][:, :]); b.sGD.inc(i, 16)
        i = g.dma_start(out=lbc[l][:, :], in_=LB2[l][:, :]); b.sGD.inc(i, 16)
    g.wait_ge(b.sGD.h, b.sGD.n)
    v.wait_ge(b.sGD.h, b.sGD.n)
    for k in range(DCH):
        i = v.tensor_copy(XF[:, k * TLOC:(k + 1) * TLOC], b.XN[:, k * TLOC:(k + 1) * TLOC])

    for li in range(2):
        b.phase_barrier()
        for k in range(DCH):
            i = y.dma_start(out=XFH[k * 128:(k + 1) * 128, :], in_=XF[:, k * TLOC:(k + 1) * TLOC])
            b.sLD.inc(i, 16)
        PUB = b.P1[:, 16384:17408]
        y.wait_ge(b.sLD.h, b.sLD.n)
        for tb in range(NTT):
            for k2 in range(8):
                i = y.dma_start(out=PUB[:, k2 * 128:(k2 + 1) * 128],
                                in_=XFH[k2 * 128:(k2 + 1) * 128, tb * 128:(tb + 1) * 128],
                                transpose=True)
                b.sLD.inc(i, 16)
            i = y.dma_start(out=XPUB2[tb * 128:(tb + 1) * 128, :], in_=PUB[:, :])
            b.sLD.inc(i, 16)
        b.sLD.wait(g)
        b.allgather(XPUB2[:, :], XALL2[:, :])
        g.wait_ge(b.sG.h, b.sG.n)
        for (XL, xlix, msk) in ((XLA, xlixa, False), (XLB, xlixb, True)):
            for blk in range(20):
                g.wait_ge(b.sLD.h, b.sLD.n)
                i = g.indirect_dma_start(GB[:, :], None, XALL2[:, :],
                                         IndirectOffsetOnAxis(ap=xlix[:, blk:blk + 1], axis=0))
                b.sGD.inc(i, 16)
                if msk and blk == 3:
                    v.wait_ge(b.sGD.h, b.sGD.n)
                    i = v.tensor_scalar_mul(GB[:, :], GB[:, :], zm1s[:, 0:1])
                    y.wait_ge(b.sV.h, b.sV.n)
                else:
                    y.wait_ge(b.sGD.h, b.sGD.n)
                i = y.dma_start(out=XL[blk * 128:(blk + 1) * 128, :], in_=GB[:, :])
                b.sLD.inc(i, 16)
        HSF = b.P1[:, 16384:36864]
        y.wait_ge(b.sLD.h, b.sLD.n)
        for k in range(8):
            XL = XLA if k < 4 else XLB
            for blk in range(20):
                i = y.dma_start(out=HSF[:, k * 2560 + blk * 128:k * 2560 + (blk + 1) * 128],
                                in_=XL[blk * 128:(blk + 1) * 128, k * 128:(k + 1) * 128],
                                transpose=True)
                b.sLD.inc(i, 16)

        QP = b.SMALL[:, 0:512]
        SQ = b.SMALL[:, 512:1024]
        NR = b.SMALL[0:1, 1024:1536]
        EB = b.SMALL[:, 1536:1792].bitcast(BF16)
        TB = b.GF[:, 0:1024]
        b.sT.wait(g); b.sS.wait(g)
        load_weight_slab(b, b.WW, WQKV[li], DCH, 1024, col0=0)
        b.sGD.wait(t)
        for ec in range(8):
            isq = ec < 4
            for sl in range(1 if isq else 0, 5):
                CTB = QCT if isq else KCT
                STB = QST if isq else KST
                y.wait_ge(b.sV.h, b.sV.n)
                i = y.dma_start(out=TB[:, 0:512], in_=CTB[:, sl * 512:(sl + 1) * 512])
                b.sLD.inc(i, 16)
                i = y.dma_start(out=TB[:, 512:1024], in_=STB[:, sl * 512:(sl + 1) * 512])
                b.sLD.inc(i, 16)
                ps = b.PS[0][:, 0:512]
                t.wait_ge(b.sS.h, b.sS.n)
                t.wait_ge(b.sLD.h, b.sLD.n)
                for kd in range(DCH):
                    i = t.matmul(ps, b.WW[:, kd * 1024 + ec * 128:kd * 1024 + ec * 128 + 128],
                                 HSF[:, kd * 2560 + sl * 512:kd * 2560 + (sl + 1) * 512],
                                 start=(kd == 0), stop=(kd == DCH - 1))
                b.sT.inc(i)
                s.wait_ge(b.sT.h, b.sT.n)
                s.wait_ge(b.sV.h, b.sV.n)
                s.wait_ge(b.sLD.h, b.sLD.n)
                i = s.activation(QP[:, :], ps, AF.Copy)
                v.wait_ge(b.sS.h, b.sS.n)
                i = v.tensor_tensor(SQ[:, :], QP[:, :], QP[:, :], ALU.mult)
                for bb in (0, 64):
                    pn = b.PS[1][0:1, 0:512]
                    t.wait_ge(b.sV.h, b.sV.n)
                    i = t.matmul(pn, ones[bb:bb + 64, 0:1], SQ[bb:bb + 64, :],
                                 start=True, stop=True)
                    b.sT.inc(i)
                    s.wait_ge(b.sT.h, b.sT.n)
                    i = s.activation(NR[:, :], pn, AF.Sqrt)
                    v.wait_ge(b.sS.h, b.sS.n)
                    i = v.tensor_scalar_max(NR[:, :], NR[:, :], 1e-12)
                    i = v.reciprocal(NR[:, :], NR[:, :])
                    pbc = b.PS[2][0:64, 0:512]
                    t.wait_ge(b.sV.h, b.sV.n)
                    i = t.matmul(pbc, ones[0:1, 0:64], NR[:, :], start=True, stop=True)
                    b.sT.inc(i)
                    v.wait_ge(b.sT.h, b.sT.n)
                    i = v.tensor_tensor(QP[bb:bb + 64, :], QP[bb:bb + 64, :],
                                        pbc[0:64, 0:512], ALU.mult)
                i = v.tensor_scalar_mul(QP[:, :], QP[:, :],
                                        qsrs[:, (li if isq else 2 + li):(li if isq else 2 + li) + 1])
                if isq:
                    i = v.tensor_scalar_mul(QP[:, :], QP[:, :], 8.0)
                pr = b.PS[3][:, 0:512]
                t.wait_ge(b.sV.h, b.sV.n)
                i = t.matmul(pr, rtf[:, :], QP[:, :], start=True, stop=True)
                b.sT.inc(i)
                v.wait_ge(b.sT.h, b.sT.n)
                v.wait_ge(b.sLD.h, b.sLD.n)
                i = v.tensor_tensor(QP[:, :], QP[:, :], TB[:, 0:512], ALU.mult)
                i = v.tensor_tensor(SQ[:, :], pr, TB[:, 512:1024], ALU.mult)
                i = v.tensor_add(QP[:, :], QP[:, :], SQ[:, :])
                i = v.tensor_copy(EB[:, :], QP[:, :])
                y.wait_ge(b.sV.h, b.sV.n)
                if isq:
                    i = y.dma_start(out=QFH[ec * 128:(ec + 1) * 128,
                                            (sl - 1) * 512:sl * 512], in_=EB[:, :])
                else:
                    i = y.dma_start(out=KFH[(ec - 4) * 128:(ec - 3) * 128,
                                            sl * 512:(sl + 1) * 512], in_=EB[:, :])
                b.sLD.inc(i, 16)
        b.sT.wait(g); b.sS.wait(g)
        load_weight_slab(b, b.WW, WQKV[li], DCH, 512, col0=1024)
        b.sGD.wait(t)
        for blk in range(20):
            ps = b.PS[blk % 2][:, 0:512]
            t.wait_ge(b.sS.h, b.sS.n)
            for kd in range(DCH):
                i = t.matmul(ps, HSF[:, kd * 2560 + blk * 128:kd * 2560 + (blk + 1) * 128],
                             b.WW[:, kd * 512:(kd + 1) * 512],
                             start=(kd == 0), stop=(kd == DCH - 1))
            b.sT.inc(i)
            s.wait_ge(b.sT.h, b.sT.n)
            s.wait_ge(b.sLD.h, b.sLD.n)
            i = s.activation(EB[:, :], ps, AF.Copy)
            y.wait_ge(b.sS.h, b.sS.n)
            i = y.dma_start(out=VTH[blk * 128:(blk + 1) * 128, :], in_=EB[:, :])
            b.sLD.inc(i, 16)

        b.phase_barrier()
        OF = b.P1[:, 16384:24576]
        QW = b.P1[:, 24576:26624]
        KW = b.P1[:, 26624:30720]
        VW = b.P1[:, 30720:34816]
        PT = b.P1[:, 34816:38912]
        MQ = b.P1[:, 38912:43008]
        MT = b.P1[:, 43008:47104]
        MS = b.SMALL[:, 0:1024]
        PEX = b.SMALL[:, 1024:2048]
        NMROW = b.GF[0:1, 0:512]
        STT = b.GF[:, 512:520]
        for w in range(4):
            var = 1 if w == 0 else 0
            if w <= 1:
                g.wait_ge(b.sV.h, b.sV.n); g.wait_ge(b.sS.h, b.sS.n)
                i = g.dma_start(out=MQ[:, :], in_=MSKQI[:, var * 4096:(var + 1) * 4096])
                b.sGD.inc(i, 16)
                i = g.dma_start(out=MT[:, :], in_=MSKTI[:, var * 4096:(var + 1) * 4096])
                b.sGD.inc(i, 16)
            y.wait_ge(b.sT.h, b.sT.n)
            for qc in range(4):
                i = y.dma_start(out=QW[:, qc * 512:(qc + 1) * 512],
                                in_=QFH[qc * 128:(qc + 1) * 128, w * 512:(w + 1) * 512])
                b.sLD.inc(i, 16)
            for kc in range(4):
                i = y.dma_start(out=KW[:, kc * 1024:(kc + 1) * 1024],
                                in_=KFH[kc * 128:(kc + 1) * 128, w * 512:w * 512 + 1024])
                b.sLD.inc(i, 16)
            for kb in range(8):
                i = y.dma_start(out=VW[:, kb * 512:(kb + 1) * 512],
                                in_=VTH[w * 512 + kb * 128:w * 512 + (kb + 1) * 128, :])
                b.sLD.inc(i, 16)
            for h in range(8):
                ch = h // 2
                bb = (h % 2) * 64
                for qs in range(4):
                    psA = b.PS[0][:, 0:512]
                    psB2 = b.PS[1][:, 0:512]
                    t.wait_ge(b.sV.h, b.sV.n)
                    t.wait_ge(b.sLD.h, b.sLD.n)
                    t.wait_ge(b.sGD.h, b.sGD.n)
                    lhq = QW[bb:bb + 64, ch * 512 + qs * 128:ch * 512 + (qs + 1) * 128]
                    i = t.matmul(psA, lhq, KW[bb:bb + 64, ch * 1024:ch * 1024 + 512],
                                 start=True, stop=True)
                    b.sT.inc(i)
                    i = t.matmul(psB2, lhq, KW[bb:bb + 64, ch * 1024 + 512:ch * 1024 + 1024],
                                 start=True, stop=True)
                    b.sT.inc(i)
                    v.wait_ge(b.sT.h, b.sT.n)
                    v.wait_ge(b.sS.h, b.sS.n)
                    v.wait_ge(b.sGD.h, b.sGD.n)
                    i = v.tensor_tensor(MS[:, 0:512], psA, MQ[:, qs * 1024:qs * 1024 + 512], ALU.add)
                    i = v.tensor_tensor(MS[:, 512:1024], psB2, MQ[:, qs * 1024 + 512:(qs + 1) * 1024], ALU.add)
                    i = v.tensor_reduce(STT[:, 0:1], MS[:, 0:512], axis=AX.X, op=ALU.max)
                    i = v.tensor_reduce(STT[:, 1:2], MS[:, 512:1024], axis=AX.X, op=ALU.max)
                    i = v.tensor_tensor(STT[:, 2:3], STT[:, 0:1], STT[:, 1:2], ALU.max)
                    i = v.tensor_scalar_mul(STT[:, 3:4], STT[:, 2:3], -1.0)
                    s.wait_ge(b.sV.h, b.sV.n)
                    b.act_accum(PEX[:, 0:512], MS[:, 0:512], AF.Exp,
                                bias=STT[:, 3:4], accum_out=STT[:, 4:5])
                    b.act_accum(PEX[:, 512:1024], MS[:, 512:1024], AF.Exp,
                                bias=STT[:, 3:4], accum_out=STT[:, 5:6])
                    v.wait_ge(b.sS.h, b.sS.n)
                    i = v.tensor_add(STT[:, 6:7], STT[:, 4:5], STT[:, 5:6])
                    s.wait_ge(b.sV.h, b.sV.n)
                    i = s.activation(STT[:, 7:8], STT[:, 6:7], AF.Ln)
                    v.wait_ge(b.sS.h, b.sS.n)
                    i = v.tensor_sub(NMCH[:, h * 4 + qs:h * 4 + qs + 1],
                                     STT[:, 3:4], STT[:, 7:8])
                for qs in range(4):
                    pn = b.PS[2][0:1, 0:128]
                    t.wait_ge(b.sV.h, b.sV.n)
                    i = t.matmul(pn, NMCH[:, h * 4 + qs:h * 4 + qs + 1], ident[:, :],
                                 start=True, stop=True)
                    b.sT.inc(i)
                    v.wait_ge(b.sT.h, b.sT.n)
                    i = v.tensor_copy(NMROW[0:1, qs * 128:(qs + 1) * 128], pn)
                for kb in range(8):
                    psT = b.PS[3][:, 0:512]
                    t.wait_ge(b.sV.h, b.sV.n)
                    t.wait_ge(b.sS.h, b.sS.n)
                    i = t.matmul(psT, KW[bb:bb + 64, ch * 1024 + kb * 128:ch * 1024 + (kb + 1) * 128],
                                 QW[bb:bb + 64, ch * 512:(ch + 1) * 512], start=True, stop=False)
                    i = t.matmul(psT, ones[0:1, 0:128], NMROW[0:1, 0:512],
                                 start=False, stop=False)
                    i = t.matmul(psT, identb[:, :], MT[:, kb * 512:(kb + 1) * 512],
                                 start=False, stop=True)
                    b.sT.inc(i)
                    s.wait_ge(b.sT.h, b.sT.n)
                    i = s.activation(PT[:, kb * 512:(kb + 1) * 512], psT, AF.Exp)
                psO = b.PS[4][0:64, 0:512]
                t.wait_ge(b.sS.h, b.sS.n)
                for kb in range(8):
                    i = t.matmul(psO, VW[:, kb * 512 + h * 64:kb * 512 + (h + 1) * 64],
                                 PT[:, kb * 512:(kb + 1) * 512],
                                 start=(kb == 0), stop=(kb == 7))
                b.sT.inc(i)
                s.wait_ge(b.sT.h, b.sT.n)
                i = s.activation(OF[bb:bb + 64, ch * 2048 + w * 512:ch * 2048 + (w + 1) * 512],
                                 psO, AF.Copy)

        b.sT.wait(g); b.sS.wait(g); b.sV.wait(g)
        load_weight_slab(b, b.WW, WO2[li], 4, 1024)
        b.sGD.wait(t)
        for fe in range(8):
            for si4 in range(NSL):
                ps = b.PS[(fe * NSL + si4) % 2][:, 0:512]
                t.wait_ge(b.sV.h, b.sV.n)
                t.wait_ge(b.sS.h, b.sS.n)
                for kc in range(4):
                    i = t.matmul(ps, b.WW[:, kc * 1024 + fe * 128:kc * 1024 + fe * 128 + 128],
                                 OF[:, kc * 2048 + si4 * 512:kc * 2048 + (si4 + 1) * 512],
                                 start=(kc == 0), stop=(kc == 3))
                b.sT.inc(i)
                v.wait_ge(b.sT.h, b.sT.n)
                i = v.tensor_tensor(XF[:, fe * TLOC + si4 * 512:fe * TLOC + (si4 + 1) * 512],
                                    XF[:, fe * TLOC + si4 * 512:fe * TLOC + (si4 + 1) * 512],
                                    ps, ALU.add)
        if DEV:
            XAO = b.out(f"XA{li}", [1024, TLOC], BF16)
            b.sV.wait(y)
            for k in range(DCH):
                i = y.dma_start(out=XAO[k * 128:(k + 1) * 128, :], in_=XF[:, k * TLOC:(k + 1) * TLOC])
                b.sLD.inc(i, 16)

        L1R = b.SMALL[0:1, 0:1024].bitcast(BF16)[0:1, 0:1024]
        for k in range(DCH):
            pl = b.PS[5][0:1, 0:128]
            t.wait_ge(b.sS.h, b.sS.n)
            t.wait_ge(b.sV.h, b.sV.n)
            i = t.matmul(pl, XF[:, k * TLOC + TLOC - 1:k * TLOC + TLOC], identb[:, :],
                         start=True, stop=True)
            b.sT.inc(i)
            s.wait_ge(b.sT.h, b.sT.n)
            i = s.activation(L1R[0:1, k * 128:(k + 1) * 128], pl, AF.Copy)
        y.wait_ge(b.sS.h, b.sS.n)
        i = y.dma_start(out=L1PUB[:, :], in_=L1R[:, :]); b.sLD.inc(i, 16)
        b.sLD.wait(g)
        b.allgather(L1PUB[:, :], L1ALL[:, :])
        g.wait_ge(b.sG.h, b.sG.n)
        g.wait_ge(b.sV.h, b.sV.n)
        i = g.indirect_dma_start(G1[:, :], None, L1ALL[:, :],
                                 IndirectOffsetOnAxis(ap=l1ixs[0:2, 0:1], axis=0))
        b.sGD.inc(i, 16)
        for k in range(DCH):
            pl = b.PS[5][:, 0:1]
            t.wait_ge(b.sGD.h, b.sGD.n)
            t.wait_ge(b.sV.h, b.sV.n)
            i = t.matmul(pl, G1[0:1, k * 128:(k + 1) * 128], identb[0:1, 0:1],
                         start=True, stop=True)
            b.sT.inc(i)
            v.wait_ge(b.sT.h, b.sT.n)
            i = v.tensor_copy(HCOL[:, k:k + 1], pl)
        i = v.tensor_scalar_mul(HCOL[:, :], HCOL[:, :], zm1s[:, 1:2])
        HS2F = b.P1[:, 16384:32768]
        for k in range(DCH):
            if k < 4:
                i = v.tensor_copy(HS2F[:, k * TLOC:(k + 1) * TLOC], XF[:, k * TLOC:(k + 1) * TLOC])
            else:
                i = v.tensor_copy(HS2F[:, k * TLOC + 1:(k + 1) * TLOC],
                                  XF[:, k * TLOC:(k + 1) * TLOC - 1])
                i = v.tensor_copy(HS2F[:, k * TLOC:k * TLOC + 1], HCOL[:, k:k + 1])
        XN2 = b.P1[:, 32768:49152]
        emit_ln_sb(b, HS2F, XN2, C, lgc[li], lbc[li])

        i = g.dma_start(out=b1cs[:, :], in_=B1C[li][:, :]); b.sGD.inc(i, 16)
        o5 = b.SMALL[0:1, 0:512]
        b2r2 = b.SMALL[0:1, 512:1536]
        prb = b.SMALL[:, 1536:1792].bitcast(BF16)
        tga = b.GF[:, 2048:2560]
        tgg = b.GF[:, 2560:3072]
        v.wait_ge(b.sS.h, b.sS.n)
        i = v.memset(o5, 1.0)
        y.wait_ge(b.sV.h, b.sV.n)
        i = y.dma_start(out=b2r2[:, :], in_=B2R2[li][:, :]); b.sLD.inc(i, 16)
        for kf in range(22):
            b.sT.wait(g); b.sS.wait(g)
            load_weight_slab(b, b.WW, W1P[li], DCH, 128, col0=kf * 128)
            load_weight_slab(b, b.WW[:, 1024:2048], W1P[li], DCH, 128, col0=HID2 + kf * 128)
            b.sGD.wait(t)
            for si in range(NSL):
                pa = b.PS[6][:, 0:512]
                pgg = b.PS[7][:, 0:512]
                t.wait_ge(b.sS.h, b.sS.n)
                t.wait_ge(b.sV.h, b.sV.n)
                for kd in range(DCH):
                    i = t.matmul(pa, b.WW[:, kd * 128:(kd + 1) * 128],
                                 XN2[:, kd * TLOC + si * 512:kd * TLOC + (si + 1) * 512],
                                 start=(kd == 0), stop=(kd == DCH - 1))
                for kd in range(DCH):
                    i = t.matmul(pgg, b.WW[:, 1024 + kd * 128:1024 + (kd + 1) * 128],
                                 XN2[:, kd * TLOC + si * 512:kd * TLOC + (si + 1) * 512],
                                 start=(kd == 0), stop=(kd == DCH - 1))
                b.sT.inc(i)
                s.wait_ge(b.sT.h, b.sT.n)
                s.wait_ge(b.sV.h, b.sV.n)
                s.wait_ge(b.sGD.h, b.sGD.n)
                i = s.activation(tga[:, :], pa, AF.Identity, bias=b1cs[:, kf:kf + 1])
                i = s.activation(tgg[:, :], pgg, AF.Gelu, bias=b1cs[:, 22 + kf:23 + kf])
                v.wait_ge(b.sS.h, b.sS.n)
                v.wait_ge(b.sLD.h, b.sLD.n)
                i = v.tensor_tensor(prb[:, :], tga[:, :], tgg[:, :], ALU.mult)
                y.wait_ge(b.sV.h, b.sV.n)
                i = y.dma_start(out=PRH[kf * 128:(kf + 1) * 128, si * 512:(si + 1) * 512],
                                in_=prb[:, :])
                b.sLD.inc(i, 16)
        W2S = b.P1[:, 16384:27648]
        prw = b.sb("l2prw", [128, 2 * 512], BF16)
        for eh in range(2):
            b.sT.wait(g); b.sS.wait(g); b.sV.wait(g)
            load_weight_slab(b, W2S, W2P[li], 22, 512, col0=eh * 512)
            b.sGD.wait(t)
            for sp in range(2):
                for kf in range(22):
                    y.wait_ge(b.sT.h, b.sT.n)
                    i = y.dma_start(out=prw[:, 0:512],
                                    in_=PRH[kf * 128:(kf + 1) * 128, sp * 1024:sp * 1024 + 512])
                    b.sLD.inc(i, 16)
                    i = y.dma_start(out=prw[:, 512:1024],
                                    in_=PRH[kf * 128:(kf + 1) * 128, sp * 1024 + 512:(sp + 1) * 1024])
                    b.sLD.inc(i, 16)
                    t.wait_ge(b.sLD.h, b.sLD.n)
                    if kf == 0:
                        t.wait_ge(b.sV.h, b.sV.n)
                    for ec in range(4):
                        i = t.matmul(b.PS[ec][:, 0:512],
                                     W2S[:, kf * 512 + ec * 128:kf * 512 + ec * 128 + 128],
                                     prw[:, 0:512], start=(kf == 0), stop=False)
                        i = t.matmul(b.PS[4 + ec][:, 0:512],
                                     W2S[:, kf * 512 + ec * 128:kf * 512 + ec * 128 + 128],
                                     prw[:, 512:1024], start=(kf == 0), stop=False)
                    b.sT.inc(i)
                for ec in range(4):
                    i = t.matmul(b.PS[ec][:, 0:512], b2r2[0:1, eh * 512 + ec * 128:eh * 512 + ec * 128 + 128],
                                 o5, start=False, stop=True)
                    i = t.matmul(b.PS[4 + ec][:, 0:512], b2r2[0:1, eh * 512 + ec * 128:eh * 512 + ec * 128 + 128],
                                 o5, start=False, stop=True)
                b.sT.inc(i)
                v.wait_ge(b.sT.h, b.sT.n)
                for ec in range(4):
                    eg = eh * 4 + ec
                    c0 = eg * TLOC + sp * 1024
                    i = v.tensor_tensor(XF[:, c0:c0 + 512], XF[:, c0:c0 + 512],
                                        b.PS[ec][:, 0:512], ALU.add)
                    i = v.tensor_tensor(XF[:, c0 + 512:c0 + 1024], XF[:, c0 + 512:c0 + 1024],
                                        b.PS[4 + ec][:, 0:512], ALU.add)
        if DEV:
            XLO = b.out(f"XL{li}", [1024, TLOC], BF16)
            b.sV.wait(y)
            for k in range(DCH):
                i = y.dma_start(out=XLO[k * 128:(k + 1) * 128, :], in_=XF[:, k * TLOC:(k + 1) * TLOC])
                b.sLD.inc(i, 16)

    b.phase_barrier()
    stg2 = b.GF[:, 0:2048]
    for k in range(DCH):
        v.wait_ge(b.sLD.h, b.sLD.n)
        i = v.tensor_copy(stg2[:, :], XF[:, k * TLOC:(k + 1) * TLOC])
        y.wait_ge(b.sV.h, b.sV.n)
        i = y.dma_start(out=FFNH[k * 128:(k + 1) * 128, :], in_=stg2[:, :])
        b.sLD.inc(i, 16)
    b.sLD.wait(b.ye)




def _l3cut(b, XIN, XOUT, n):
    if os.environ.get("L3CUT") == str(n):
        g = b.ge
        b.phase_barrier()
        i = g.dma_start(out=XOUT[:, :], in_=XIN[:, :])
        b.sGD.inc(i, 16)
        b.sGD.wait(g)
        return True
    return False


def emit_l3(b, C):
    v, s, t, g, y = b.vs, b.ss, b.te, b.ge, b.ye
    XIN = b.xin()
    XOUT = b.out("XOUT", [D, TLOC], F32)
    DEV = bool(os.environ.get("KDEV"))
    ATTH = b.out("ATTH", [D, TLOC], F32) if DEV else b.dram("l3att", [D, TLOC], F32)
    FFNH = b.out("FFNH", [D, TLOC], F32) if DEV else b.dram("l3ffn", [D, TLOC], F32)
    CHL = 256
    SEG = TLOC
    NSEG = T // SEG
    NWIN = T // TS
    ones = C["ones128"]
    ident = C["ident"]

    WinX = b.inp("w_l3winx", [D, CHL], F32, lambda c, inputs, state:
                 np.ascontiguousarray(np.asarray(inputs["L3_attn_W_in"])[:, 256 * c:256 * (c + 1)], np.float32))
    WinZ = b.inp("w_l3winz", [D, 2048], F32, lambda c, inputs, state:
                 np.ascontiguousarray(np.asarray(inputs["L3_attn_W_in"])[:, 2048:], np.float32))
    CONVW = b.inp("w_l3convw", [1, 4 * CHL], F32, lambda c, inputs, state:
                  np.ascontiguousarray(np.asarray(inputs["L3_attn_conv_w"])[256 * c:256 * (c + 1), ::-1].T.reshape(1, -1), np.float32))
    CONVB = b.inp("w_l3convb", [128, 2], F32, lambda c, inputs, state:
                  np.ascontiguousarray(np.asarray(inputs["L3_attn_conv_b"])[256 * c:256 * (c + 1)].reshape(2, 128).T, np.float32))
    WXP = b.inp("w_l3wxp", [CHL, 96], F32, lambda c, inputs, state:
                np.ascontiguousarray(np.asarray(inputs["L3_attn_W_xproj"])[256 * c:256 * (c + 1), :], np.float32))
    WDT = b.inp("w_l3wdt", [64, CHL], F32, lambda c, inputs, state:
                np.ascontiguousarray(np.asarray(inputs["L3_attn_W_dt"])[:, 256 * c:256 * (c + 1)], np.float32))
    BDT = b.inp("w_l3bdt", [128, 2], F32, lambda c, inputs, state:
                np.ascontiguousarray(np.asarray(inputs["L3_attn_b_dt"])[256 * c:256 * (c + 1)].reshape(2, 128).T, np.float32))
    def aprov(c, inputs, state):
        sl = np.asarray(inputs["L3_attn_A_log"])[256 * c:256 * (c + 1), :]
        a4 = sl.reshape(2, 16, 8, 16)
        return np.ascontiguousarray(a4.transpose(2, 3, 0, 1).reshape(128, 32), np.float32)
    ALOG = b.inp("w_l3alog", [128, 32], F32, aprov)
    DP = b.inp("w_l3dp", [128, 2], F32, lambda c, inputs, state:
               np.ascontiguousarray(np.asarray(inputs["L3_attn_Dp"])[256 * c:256 * (c + 1)].reshape(2, 128).T, np.float32))
    WOUT = b.weight("L3_attn_W_out", [2048, D])

    def selrep(cc):
        m = np.zeros((128, 16 * 128), np.float32)
        for gg in range(16):
            for r in range(128):
                m[8 * gg + r // 16, gg * 128 + r] = 1.0
        return m
    SELR = b.const("l3selr", selrep)
    def selwf(cc):
        m = np.zeros((128, 16 * 128), np.float32)
        for gg in range(16):
            for r in range(128):
                m[r, gg * 128 + 8 * gg + r // 16] = 1.0
        return m
    SELW = b.const("l3selw", selwf)
    def selnf(cc):
        m = np.zeros((16, 128), np.float32)
        for r in range(128):
            m[r % 16, r] = 1.0
        return m
    SELN = b.const("l3seln2", selnf)
    def gidxf(c):
        gch = (np.arange(16, dtype=np.uint32)[None, :] * 128
               + np.arange(128, dtype=np.uint32)[:, None])
        out = np.zeros((128, 32), np.uint32)
        out[:, 0::2] = gch * 16 + 2 * np.uint32(c)
        out[:, 1::2] = gch * 16 + 2 * np.uint32(c) + 1
        return out
    GIDX = b.const("l3gidx", gidxf)

    emit_ln(b, XIN, b.XN, C)
    b.phase_barrier()

    XNF = b.dram("l3xnf", [D, TLOC], BF16)
    XNPUB = b.dram("l3xnpub", [TLOC, D], BF16)
    XNALL = b.dram("l3xnall", [T, D], BF16, shared=True)
    for k in range(DCH):
        i = y.dma_start(out=XNF[k * 128:(k + 1) * 128, :], in_=b.XN[:, k * TLOC:(k + 1) * TLOC])
        b.sLD.inc(i, 16)
    XT1 = b.WW[:, 0:1024]
    y.wait_ge(b.sLD.h, b.sLD.n)
    for tt in range(NTT):
        i = y.dma_start(out=XT1[:, :], in_=XNF[:, tt * 128:(tt + 1) * 128], transpose=True)
        b.sLD.inc(i, 16)
        y.wait_ge(b.sLD.h, b.sLD.n)
        i = y.dma_start(out=XNPUB[tt * 128:(tt + 1) * 128, :], in_=XT1[:, :])
        b.sLD.inc(i, 16)
        y.wait_ge(b.sLD.h, b.sLD.n)
    b.sLD.wait(g)
    b.allgather(XNPUB[:, :], XNALL[:, :])
    if _l3cut(b, XIN, XOUT, 1):
        return

    WXS = b.P1[:, 0:2048]
    WJS = b.P1[:, 2048:10240]
    cwb = b.SMALL[:, 0:1024]
    cwrow = b.SMALL[0:1, 1024:2048]
    i = g.dma_start(out=cwrow[:, :], in_=CONVW[:, :]); b.sGD.inc(i, 16)
    load_weight_slab(b, WXS, WinX, DCH, CHL)
    wxps = b.sb("l3wxps", [128, 192], BF16)
    load_weight_slab(b, wxps, WXP, 2, 96)
    convbs = b.sb("l3convbs", [128, 2], F32)
    i = g.dma_start(out=convbs[:, :], in_=CONVB[:, :]); b.sGD.inc(i, 16)
    t.wait_ge(b.sGD.h, b.sGD.n)
    for j in range(4):
        pb = b.PS[0][:, 0:CHL]
        t.wait_ge(b.sV.h, b.sV.n)
        i = t.matmul(pb, ones[0:1, 0:128], cwrow[0:1, j * CHL:(j + 1) * CHL],
                     start=True, stop=True)
        b.sT.inc(i)
        v.wait_ge(b.sT.h, b.sT.n)
        i = v.tensor_copy(cwb[:, j * CHL:(j + 1) * CHL], pb)
    v.wait_ge(b.sGD.h, b.sGD.n)
    for j in range(4):
        for k in range(DCH):
            i = v.tensor_tensor(WJS[:, (j * DCH + k) * CHL:(j * DCH + k + 1) * CHL],
                                WXS[:, k * CHL:(k + 1) * CHL],
                                cwb[:, j * CHL:(j + 1) * CHL], ALU.mult)
    z512 = b.SMALL[0:1, 0:512]
    i = v.memset(z512, 0.0)

    XCH = b.dram("l3xch", [2 * 128, T], BF16)
    XDBLP = b.dram("l3xdblp", [96, T], F32)
    XDBLS = b.dram("l3xdbls", [96, T], F32)
    XHW = b.P1[:, 10240:14464]
    XCW = b.P1[:, 14464:16512]
    xdp = b.P1[:, 16512:17536].bitcast(F32)[0:96, :]
    b.phase_barrier()
    y.wait_ge(b.sG.h, b.sG.n)
    t.wait_ge(b.sV.h, b.sV.n)
    for w in range(NWIN):
        t0 = w * TS
        lo = max(t0 - 16, 0)
        hoff = 16 if w > 0 else 0
        y.wait_ge(b.sT.h, b.sT.n)
        for k in range(DCH):
            i = y.dma_start(out=XHW[:, k * 528:k * 528 + (t0 + TS - lo)],
                            in_=XNALL[lo:t0 + TS, k * 128:(k + 1) * 128], transpose=True)
            b.sLD.inc(i, 16)
        t.wait_ge(b.sLD.h, b.sLD.n)
        for chc in range(2):
            ps = b.PS[chc][:, 0:TS]
            t.wait_ge(b.sS.h, b.sS.n)
            first = True
            for j in range(4):
                for k in range(DCH):
                    lhs = WJS[:, (j * DCH + k) * CHL + chc * 128:(j * DCH + k) * CHL + chc * 128 + 128]
                    if w == 0 and j > 0:
                        i = t.matmul(ps[:, j:TS], lhs, XHW[:, k * 528:k * 528 + TS - j],
                                     start=False, stop=False)
                    else:
                        i = t.matmul(ps, lhs,
                                     XHW[:, k * 528 + hoff - j:k * 528 + hoff - j + TS],
                                     start=first, stop=False)
                    first = False
            i = t.matmul(ps, ones[0:1, 0:128], z512, start=False, stop=True)
            b.sT.inc(i)
            s.wait_ge(b.sT.h, b.sT.n)
            s.wait_ge(b.sLD.h, b.sLD.n)
            i = s.activation(XCW[:, chc * TS:(chc + 1) * TS], ps, AF.Silu,
                             bias=convbs[:, chc:chc + 1])
            y.wait_ge(b.sS.h, b.sS.n)
            i = y.dma_start(out=XCH[chc * 128:(chc + 1) * 128, t0:t0 + TS],
                            in_=XCW[:, chc * TS:(chc + 1) * TS])
            b.sLD.inc(i, 16)
        pd = b.PS[4][0:96, 0:TS]
        t.wait_ge(b.sS.h, b.sS.n)
        t.wait_ge(b.sV.h, b.sV.n)
        for chc in range(2):
            i = t.matmul(pd, wxps[:, chc * 96:(chc + 1) * 96],
                         XCW[:, chc * TS:(chc + 1) * TS],
                         start=(chc == 0), stop=(chc == 1))
        b.sT.inc(i)
        v.wait_ge(b.sT.h, b.sT.n)
        v.wait_ge(b.sLD.h, b.sLD.n)
        i = v.tensor_copy(xdp[:, :], pd)
        y.wait_ge(b.sV.h, b.sV.n)
        i = y.dma_start(out=XDBLP[:, t0:t0 + TS], in_=xdp[:, :])
        b.sLD.inc(i, 16)
    b.phase_barrier()
    b.sLD.wait(g)
    b.allreduce(XDBLP[:, :], XDBLS[:, :])
    if _l3cut(b, XIN, XOUT, 2):
        return
    if DEV:
        XCHO = b.out("XCHO", [256, TLOC], BF16)
        i = g.dma_start(out=XCHO[:, :], in_=XCH[:, 0:TLOC]); b.sGD.inc(i, 16)
        XDBO = b.out("XDBO", [96, TLOC], F32)
        g.wait_ge(b.sG.h, b.sG.n)
        i = g.dma_start(out=XDBO[:, :], in_=XDBLS[:, 0:TLOC]); b.sGD.inc(i, 16)

    acol = b.sb("l3acol", [128, 32], F32)
    i = g.dma_start(out=acol[:, :], in_=ALOG[:, :]); b.sGD.inc(i, 16)
    s.wait_ge(b.sGD.h, b.sGD.n)
    i = s.activation(acol[:, :], acol[:, :], AF.Exp)
    v.wait_ge(b.sS.h, b.sS.n)
    i = v.tensor_scalar_mul(acol[:, :], acol[:, :], -1.0)
    selr = b.P1[:, 17536:19584]
    selwS = b.P1[:, 19584:21632]
    i = g.dma_start(out=selr[:, :], in_=SELR[:, :]); b.sGD.inc(i, 16)
    i = g.dma_start(out=selwS[:, :], in_=SELW[:, :]); b.sGD.inc(i, 16)
    seln_s = b.sb("l3selns", [16, 128], BF16)
    i = g.dma_start(out=seln_s[:, :], in_=SELN[:, :]); b.sGD.inc(i, 16)
    wdts = b.sb("l3wdts", [64, CHL], BF16)
    i = g.dma_start(out=wdts[:, :], in_=WDT[:, :]); b.sGD.inc(i, 16)
    bdts = b.sb("l3bdts", [128, 2], F32)
    i = g.dma_start(out=bdts[:, :], in_=BDT[:, :]); b.sGD.inc(i, 16)
    dps = b.sb("l3dps", [128, 2], F32)
    i = g.dma_start(out=dps[:, :], in_=DP[:, :]); b.sGD.inc(i, 16)

    DTSEG = b.P1[:, 0:4096]
    USEG = b.P1[:, 4096:8192]
    XCSEG = b.P1[:, 8192:12288]
    BREP = b.P1[:, 12288:14336]
    CREP = b.P1[:, 14336:16384]
    DTP = b.P1[0:64, 21632:23680]
    BSEG = b.P1[0:16, 23680:25728]
    CSEG = b.P1[0:16, 25728:27776]
    HC = b.sb("l3hc", [128, 32], F32)
    DA = b.sb("l3da", [128, TS], BF16)
    DBX = b.sb("l3dbx", [128, TS], BF16)
    PRD = b.sb("l3prd", [128, TS], BF16)
    HT3 = b.sb("l3ht3", [128, TS + 1], F32)
    SC3 = b.sb("l3sc3", [128, 1280], F32)
    YPUB = b.dram("l3ypub", [256, T], BF16)
    YALL = b.dram("l3yall", [2048, T], BF16, shared=True)
    i = g.memset(HC[:, :], 0.0)
    b.sG.inc(i)
    g.wait_ge(b.sG.h, b.sG.n)

    for sg in range(NSEG):
        so = sg * SEG
        y.wait_ge(b.sT.h, b.sT.n)
        y.wait_ge(b.sV.h, b.sV.n)
        for chc in range(2):
            i = y.dma_start(out=XCSEG[:, chc * SEG:(chc + 1) * SEG],
                            in_=XCH[chc * 128:(chc + 1) * 128, so:so + SEG])
            b.sLD.inc(i, 16)
        g.wait_ge(b.sT.h, b.sT.n)
        i = g.dma_start(out=BSEG[:, :], in_=XDBLS[64:80, so:so + SEG]); b.sGD.inc(i, 16)
        i = g.dma_start(out=CSEG[:, :], in_=XDBLS[80:96, so:so + SEG]); b.sGD.inc(i, 16)
        i = g.dma_start(out=DTP[:, :], in_=XDBLS[0:64, so:so + SEG]); b.sGD.inc(i, 16)
        for chc in range(2):
            for si4 in range(SEG // TS):
                ps = b.PS[2][:, 0:TS]
                t.wait_ge(b.sGD.h, b.sGD.n)
                t.wait_ge(b.sV.h, b.sV.n)
                t.wait_ge(b.sS.h, b.sS.n)
                i = t.matmul(ps, wdts[0:64, chc * 128:(chc + 1) * 128],
                             DTP[0:64, si4 * TS:(si4 + 1) * TS], start=True, stop=True)
                b.sT.inc(i)
                s.wait_ge(b.sT.h, b.sT.n)
                zz = SC3[:, 0:TS]
                aa = SC3[:, TS:1024]
                i = s.activation(zz, ps, AF.Identity, bias=bdts[:, chc:chc + 1])
                i = s.activation(aa, zz, AF.Abs)
                i = s.activation(aa, aa, AF.Exp, scale=-1.0)
                i = s.activation(aa, aa, AF.Ln, bias=b.ccol(1.0))
                i = s.activation(zz, zz, AF.Relu)
                dsl = DTSEG[:, chc * SEG + si4 * TS:chc * SEG + (si4 + 1) * TS]
                v.wait_ge(b.sS.h, b.sS.n)
                v.wait_ge(b.sT.h, b.sT.n)
                i = v.tensor_add(dsl, zz, aa)
        v.wait_ge(b.sLD.h, b.sLD.n)
        for chc in range(2):
            i = v.tensor_tensor(USEG[:, chc * SEG:(chc + 1) * SEG],
                                DTSEG[:, chc * SEG:(chc + 1) * SEG],
                                XCSEG[:, chc * SEG:(chc + 1) * SEG], ALU.mult)
        if DEV and sg == 0:
            b._dtout = b.out("DTO", [128, 4096], BF16)
            y.wait_ge(b.sV.h, b.sV.n)
            i = y.dma_start(out=b._dtout[:, :], in_=DTSEG[:, :]); b.sLD.inc(i, 16)
        for si4 in range(SEG // TS):
            pb_ = b.PS[3][:, 0:TS]
            t.wait_ge(b.sV.h, b.sV.n)
            t.wait_ge(b.sGD.h, b.sGD.n)
            i = t.matmul(pb_, seln_s[0:16, :], BSEG[0:16, si4 * TS:(si4 + 1) * TS],
                         start=True, stop=True)
            b.sT.inc(i)
            v.wait_ge(b.sT.h, b.sT.n)
            i = v.tensor_copy(BREP[:, si4 * TS:(si4 + 1) * TS], pb_)
            t.wait_ge(b.sV.h, b.sV.n)
            i = t.matmul(pb_, seln_s[0:16, :], CSEG[0:16, si4 * TS:(si4 + 1) * TS],
                         start=True, stop=True)
            b.sT.inc(i)
            v.wait_ge(b.sT.h, b.sT.n)
            i = v.tensor_copy(CREP[:, si4 * TS:(si4 + 1) * TS], pb_)
        for chc in range(2):
            for si4 in range(SEG // TS):
                pyb = b.PS[6 + (si4 % 2)][:, 0:TS]
                for gg in range(16):
                    gt = chc * 16 + gg
                    pr_ = b.PS[4][:, 0:TS]
                    t.wait_ge(b.sS.h, b.sS.n)
                    t.wait_ge(b.sV.h, b.sV.n)
                    i = t.matmul(pr_, selr[:, gg * 128:(gg + 1) * 128],
                                 DTSEG[:, chc * SEG + si4 * TS:chc * SEG + (si4 + 1) * TS],
                                 start=True, stop=True)
                    b.sT.inc(i)
                    s.wait_ge(b.sT.h, b.sT.n)
                    i = s.activation(DA[:, 0:TS], pr_, AF.Exp, scale=acol[:, gt:gt + 1])
                    pr2 = b.PS[5][:, 0:TS]
                    i = t.matmul(pr2, selr[:, gg * 128:(gg + 1) * 128],
                                 USEG[:, chc * SEG + si4 * TS:chc * SEG + (si4 + 1) * TS],
                                 start=True, stop=True)
                    b.sT.inc(i)
                    v.wait_ge(b.sT.h, b.sT.n)
                    v.wait_ge(b.sS.h, b.sS.n)
                    i = v.scalar_tensor_tensor(DBX[:, 0:TS], pr2, 1.0,
                                               BREP[:, si4 * TS:(si4 + 1) * TS],
                                               ALU.mult, ALU.mult)
                    i = v.tensor_tensor_scan(HT3[:, 1:TS + 1], DA[:, 0:TS], DBX[:, 0:TS],
                                             HC[:, gt:gt + 1], ALU.mult, ALU.add)
                    i = v.tensor_copy(HC[:, gt:gt + 1], HT3[:, TS:TS + 1])
                    i = v.tensor_tensor(PRD[:, 0:TS], HT3[:, 1:TS + 1],
                                        CREP[:, si4 * TS:(si4 + 1) * TS], ALU.mult)
                    t.wait_ge(b.sV.h, b.sV.n)
                    i = t.matmul(pyb, selwS[:, gg * 128:(gg + 1) * 128], PRD[:, 0:TS],
                                 start=(gg == 0), stop=(gg == 15))
                    b.sT.inc(i)
                v.wait_ge(b.sT.h, b.sT.n)
                v.wait_ge(b.sLD.h, b.sLD.n)
                xcd = SC3[:, 0:TS]
                i = v.tensor_scalar_mul(xcd,
                                        XCSEG[:, chc * SEG + si4 * TS:chc * SEG + (si4 + 1) * TS],
                                        dps[:, chc:chc + 1])
                i = v.tensor_tensor(xcd, xcd, pyb, ALU.add)
                yev = SC3[:, 512:768].bitcast(BF16)
                i = v.tensor_copy(yev, xcd)
                y.wait_ge(b.sV.h, b.sV.n)
                i = y.dma_start(out=YPUB[chc * 128:(chc + 1) * 128,
                                         so + si4 * TS:so + (si4 + 1) * TS], in_=yev)
                b.sLD.inc(i, 16)
    b.sLD.wait(g)
    b.allgather(YPUB[:, :], YALL[:, :])
    if _l3cut(b, XIN, XOUT, 3):
        return
    if DEV:
        YPO = b.out("YPO", [256, TLOC], BF16)
        i = g.dma_start(out=YPO[:, :], in_=YPUB[:, 0:TLOC]); b.sGD.inc(i, 16)

    b.phase_barrier()
    ZS = b.P1[:, 0:32768]
    for half in range(2):
        b.sT.wait(g); b.sS.wait(g)
        load_weight_slab(b, b.WW, WinZ, DCH, 1024, col0=half * 1024)
        b.sGD.wait(t)
        proj_fm(b, ZS, half * 8 * TLOC, b.XN, b.WW, 1024, act=AF.Silu, wwidth=1024)
    GIXs = b.sb("l3gixs", [128, 32], U32)
    i = g.dma_start(out=GIXs[:, :], in_=GIDX[:, :]); b.sGD.inc(i, 16)
    g.wait_ge(b.sGD.h, b.sGD.n)
    YTMP = b.sb("l3ytmp", [128, 2048], BF16)
    g.wait_ge(b.sG.h, b.sG.n)
    YALLv = YALL[:, :].rearrange("a (b c) -> (a b) c", c=1024)
    GMODE = os.environ.get("L3SKIPGATE", "")
    for k in range(0 if GMODE == "1" else 16):
        g.wait_ge(b.sV.h, b.sV.n)
        if GMODE != "3":
            for hf in range(2):
                i = g.indirect_dma_start(YTMP[:, hf * 1024:(hf + 1) * 1024], None, YALLv,
                                         IndirectOffsetOnAxis(ap=GIXs[:, 2 * k + hf:2 * k + hf + 1], axis=0))
                if k == 0 and os.environ.get("L3DBGG"):
                    print("KERNEL GATHER:", i)
                b.sGD.inc(i, 16)
        if GMODE != "2":
            v.wait_ge(b.sGD.h, b.sGD.n)
            v.wait_ge(b.sS.h, b.sS.n)
            i = v.tensor_tensor(ZS[:, k * TLOC:(k + 1) * TLOC],
                                ZS[:, k * TLOC:(k + 1) * TLOC], YTMP[:, :], ALU.mult)
    if _l3cut(b, XIN, XOUT, 4):
        return
    oevac = b.SMALL[:, 1024:1536]
    for half in range(2):
        b.sT.wait(g); b.sS.wait(g)
        load_weight_slab(b, b.WW, WOUT, 16, 512, col0=half * 512)
        b.sGD.wait(t)
        for f in range(4):
            for si in range(NSL):
                ps = b.PS[6 if (f * NSL + si) % 2 == 0 else 7][:, 0:TS]
                t.wait_ge(b.sS.h, b.sS.n)
                t.wait_ge(b.sV.h, b.sV.n)
                for k in range(16):
                    i = t.matmul(ps, b.WW[:, k * 512 + f * 128:k * 512 + f * 128 + 128],
                                 ZS[:, k * TLOC + si * TS:k * TLOC + (si + 1) * TS],
                                 start=(k == 0), stop=(k == 15))
                b.sT.inc(i)
                s.wait_ge(b.sT.h, b.sT.n)
                s.wait_ge(b.sLD.h, b.sLD.n)
                i = s.activation(oevac, ps, AF.Copy)
                y.wait_ge(b.sS.h, b.sS.n)
                i = y.dma_start(out=ATTH[(half * 4 + f) * 128:(half * 4 + f + 1) * 128,
                                         si * TS:(si + 1) * TS], in_=oevac)
                b.sLD.inc(i, 16)

    if _l3cut(b, XIN, XOUT, 5):
        return
    emit_l3_ffn(b, C, FFNH)
    emit_final_combine(b, XIN, XOUT, [ATTH, FFNH])


def emit_l3_ffn(b, C, FFNH):
    v, s, t, g, y = b.vs, b.ss, b.te, b.ge, b.ye
    DEV = bool(os.environ.get("KDEV"))
    FH3 = 2048
    ones = C["ones128"]
    ident = C["ident"]
    WRED = b.weight("L3_ffn_w_red", [D, 16])
    WGF = b.weight("L3_ffn_wg", [8, 16])
    W1F = b.inp("w_l3w1", [8 * D, FH3], F32, lambda c, inputs, state:
                np.ascontiguousarray(np.asarray(inputs["L3_ffn_W1"]).reshape(8 * D, FH3), np.float32))
    B1F = b.inp("w_l3b1", [128, 128], F32, lambda c, inputs, state:
                np.ascontiguousarray(np.asarray(inputs["L3_ffn_b1"]).reshape(8, 16, 128).transpose(2, 0, 1).reshape(128, 128), np.float32))
    W2F = b.inp("w_l3w2", [8 * FH3, D], F32, lambda c, inputs, state:
                np.ascontiguousarray(np.asarray(inputs["L3_ffn_W2"]).reshape(8 * FH3, D), np.float32))
    B2R = b.inp("w_l3b2", [1, 8 * D], F32, lambda c, inputs, state:
                np.ascontiguousarray(np.asarray(inputs["L3_ffn_b2"]).reshape(1, 8 * D), np.float32))
    b.phase_barrier()

    wgs = b.sb("l3wgs", [8, 16], F32)
    sq8 = b.sb("l3sq8", [8, 16], F32)
    nrm = b.sb("l3nrm", [8, 2], F32)
    zrow8 = b.sb("l3zrow8", [128, 8], F32)
    wgsb = b.sb("l3wgsb", [16, 8], BF16)
    i = v.memset(zrow8[:, :], 0.0)
    i = g.dma_start(out=wgs[:, :], in_=WGF[:, :]); b.sGD.inc(i, 16)
    v.wait_ge(b.sGD.h, b.sGD.n)
    i = v.tensor_tensor(sq8[:, :], wgs[:, :], wgs[:, :], ALU.mult)
    i = v.tensor_reduce(nrm[:, 0:1], sq8[:, :], axis=AX.X, op=ALU.add)
    s.wait_ge(b.sV.h, b.sV.n)
    i = s.activation(nrm[:, 0:1], nrm[:, 0:1], AF.Sqrt)
    v.wait_ge(b.sS.h, b.sS.n)
    i = v.tensor_scalar_max(nrm[:, 0:1], nrm[:, 0:1], 1e-4)
    i = v.reciprocal(nrm[:, 0:1], nrm[:, 0:1])
    i = v.tensor_scalar_mul(wgs[:, :], wgs[:, :], nrm[:, 0:1])
    pwt = b.PS[1][0:16, 0:8]
    t.wait_ge(b.sV.h, b.sV.n)
    i = t.matmul(pwt, wgs[:, :], ident[0:8, 0:8], start=True, stop=True)
    b.sT.inc(i)
    v.wait_ge(b.sT.h, b.sT.n)
    i = v.tensor_copy(wgsb[:, :], pwt)
    b.sT.wait(g); b.sS.wait(g)
    load_weight_slab(b, b.WW, WRED, DCH, 16)
    b.sGD.wait(t)
    XRB = b.GF[:, 2048:3072].bitcast(BF16)[0:16, :]
    for si in range(NSL):
        ps = b.PS[0][0:16, 0:TS]
        t.wait_ge(b.sV.h, b.sV.n)
        t.wait_ge(b.sS.h, b.sS.n)
        for k in range(DCH):
            i = t.matmul(ps, b.WW[:, k * 16:(k + 1) * 16],
                         b.XN[:, k * TLOC + si * TS:k * TLOC + (si + 1) * TS],
                         start=(k == 0), stop=(k == DCH - 1))
        b.sT.inc(i)
        v.wait_ge(b.sT.h, b.sT.n)
        i = v.tensor_copy(XRB[:, si * TS:(si + 1) * TS], ps)
    MCO = b.sb("l3mco", [128, 192], F32)
    cums = b.sb("l3cums", [128, 8], F32)
    for tt in range(NTT):
        ps = b.PS[2][:, 0:8]
        t.wait_ge(b.sV.h, b.sV.n)
        i = t.matmul(ps, XRB[:, tt * 128:(tt + 1) * 128], wgsb[:, :], start=True, stop=True)
        b.sT.inc(i)
        base = tt * 12
        mcol = MCO[:, base:base + 1]
        oh = MCO[:, base + 1:base + 9]
        neg = MCO[:, base + 9:base + 10]
        acc = MCO[:, base + 10:base + 11]
        v.wait_ge(b.sT.h, b.sT.n)
        i = v.tensor_reduce(mcol, ps, axis=AX.X, op=ALU.max)
        i = v.tensor_scalar_mul(neg, mcol, -1.0)
        s.wait_ge(b.sV.h, b.sV.n)
        s.wait_ge(b.sT.h, b.sT.n)
        b.act_accum(oh, ps, AF.Exp, bias=neg, accum_out=acc)
        v.wait_ge(b.sS.h, b.sS.n)
        i = v.reciprocal(mcol, acc)
        i = v.tensor_scalar(oh, oh, 0.999999, None, ALU.is_ge)
        i = v.tensor_tensor_scan(cums[:, :], oh, zrow8[:, :], 0.0, ALU.add, ALU.add)
        i = v.tensor_scalar(cums[:, :], cums[:, :], 1.5, None, ALU.is_lt)
        i = v.tensor_mul(oh, oh, cums[:, :])
    if DEV:
        RDBO = b.out("RDBO", [128, 192], F32)
        y.wait_ge(b.sV.h, b.sV.n)
        i = y.dma_start(out=RDBO[:, :], in_=MCO[:, :]); b.sLD.inc(i, 16)
    COTM = b.GF[:, 3072:3200]
    for tt in range(NTT):
        i = v.tensor_scalar_mul(COTM[:, tt * 8:(tt + 1) * 8],
                                MCO[:, tt * 12 + 1:tt * 12 + 9],
                                MCO[:, tt * 12:tt * 12 + 1])
    CBALL = b.P1[:, 32768:49152]
    CBROW = b.SMALL[0:1, 512:1024]
    for e in range(8):
        for q in range(NSL):
            t.wait_ge(b.sV.h, b.sV.n)
            for tq in range(4):
                tt = q * 4 + tq
                i = t.matmul(b.PS[2][0:1, tq * 128:(tq + 1) * 128],
                             COTM[:, tt * 8 + e:tt * 8 + e + 1], ident[:, :],
                             start=True, stop=True)
            b.sT.inc(i)
            v.wait_ge(b.sT.h, b.sT.n)
            i = v.tensor_copy(CBROW, b.PS[2][0:1, 0:TS])
            t.wait_ge(b.sV.h, b.sV.n)
            ps3 = b.PS[3][:, 0:TS]
            i = t.matmul(ps3, ones[0:1, 0:128], CBROW, start=True, stop=True)
            b.sT.inc(i)
            s.wait_ge(b.sT.h, b.sT.n)
            i = s.activation(CBALL[:, e * TLOC + q * TS:e * TLOC + (q + 1) * TS],
                             ps3, AF.Copy)

    W1SL = b.P1[:, 0:16384]
    OACC = b.P1[:, 16384:32768]
    HT = b.WW[:, 0:8192]
    tmp = b.SMALL[:, 1536:2048]
    o512 = b.SMALL[0:1, 0:512]
    b2row = b.SMALL[0:1, 512:1536]
    b1s = b.sb("l3b1s", [128, 128], F32)
    i = g.dma_start(out=b1s[:, :], in_=B1F[:, :]); b.sGD.inc(i, 16)
    v.wait_ge(b.sS.h, b.sS.n)
    i = v.memset(o512, 1.0)
    v.wait_ge(b.sT.h, b.sT.n)
    for k in range(DCH):
        i = v.memset(OACC[:, k * TLOC:(k + 1) * TLOC], 0.0)
    marks = []
    for e in range(8):
        b.sT.wait(g); b.sS.wait(g); b.sV.wait(g); b.sT.wait(y)
        load_weight_slab(b, W1SL, W1F, DCH, FH3, row0=e * D)
        i = y.dma_start(out=b2row, in_=B2R[0:1, e * D:(e + 1) * D])
        b.sLD.inc(i, 16)
        b.sGD.wait(t)
        for q in range(NSL):
            for f in range(16):
                ps = b.PS[f % 2][:, 0:TS]
                t.wait_ge(b.sS.h, b.sS.n)
                for k in range(DCH):
                    i = t.matmul(ps, W1SL[:, k * FH3 + f * 128:k * FH3 + f * 128 + 128],
                                 b.XN[:, k * TLOC + q * TS:k * TLOC + (q + 1) * TS],
                                 start=(k == 0), stop=(k == DCH - 1))
                b.sT.inc(i)
                s.wait_ge(b.sT.h, b.sT.n)
                s.wait_ge(b.sGD.h, b.sGD.n)
                i = s.activation(HT[:, f * TS:(f + 1) * TS], ps, AF.Gelu,
                                 bias=b1s[:, e * 16 + f:e * 16 + f + 1])
            t.wait_ge(b.sS.h, b.sS.n)
            t.wait_ge(b.sLD.h, b.sLD.n)
            for f in range(16):
                wbuf = b.GF[:, (f % 2) * 512:(f % 2) * 512 + 512].bitcast(BF16)
                if len(marks) >= 2:
                    g.wait_ge(b.sT.h, marks[-2])
                i = g.dma_start(out=wbuf[:, :],
                                in_=W2F[(e * 16 + f) * 128:(e * 16 + f + 1) * 128, :])
                b.sGD.inc(i, 16)
                t.wait_ge(b.sGD.h, b.sGD.n)
                for dm in range(DCH):
                    i = t.matmul(b.PS[dm][:, 0:TS], wbuf[:, dm * 128:(dm + 1) * 128],
                                 HT[:, f * TS:(f + 1) * TS], start=(f == 0), stop=False)
                b.sT.inc(i)
                marks.append(b.sT.n)
            for dm in range(DCH):
                i = t.matmul(b.PS[dm][:, 0:TS], b2row[0:1, dm * 128:(dm + 1) * 128],
                             o512, start=False, stop=True)
            b.sT.inc(i)
            v.wait_ge(b.sT.h, b.sT.n)
            for dm in range(DCH):
                osl = OACC[:, dm * TLOC + q * TS:dm * TLOC + (q + 1) * TS]
                i = v.tensor_tensor(tmp, b.PS[dm][:, 0:TS],
                                    CBALL[:, e * TLOC + q * TS:e * TLOC + (q + 1) * TS],
                                    ALU.mult)
                i = v.tensor_add(osl, osl, tmp)
            t.wait_ge(b.sV.h, b.sV.n)
    b.phase_barrier()
    stg = b.GF[:, 1024:3072]
    for k in range(DCH):
        v.wait_ge(b.sLD.h, b.sLD.n)
        i = v.tensor_copy(stg, OACC[:, k * TLOC:(k + 1) * TLOC])
        y.wait_ge(b.sV.h, b.sV.n)
        i = y.dma_start(out=FFNH[k * 128:(k + 1) * 128, :], in_=stg)
        b.sLD.inc(i, 16)
    b.sLD.wait(b.ye)


def emit_passthrough(b, C):
    XIN = b.xin()
    XOUT = b.out("XOUT", [D, TLOC], F32)
    g = b.ge
    i = g.dma_start(out=XOUT[:, :], in_=XIN[:, :])
    b.sGD.inc(i, 16)
    b.sGD.wait(g)



def build_layer(li):
    b = B(f"l{li}")
    C = std_consts(b)
    if li == 0:
        emit_l0(b, C)
    elif li == 1:
        emit_l1(b, C)
    elif li == 2:
        emit_l2(b, C)
    elif li == 3:
        emit_l3(b, C)
    else:
        emit_passthrough(b, C)
    import os
    if os.environ.get("KDEV"):
        print(f"[build l{li}] sbuf bytes remaining/partition: {b.nc.sbuf_bytes_remaining}; "
              f"instructions: {b.nc.n_instructions()}")
    return b


_BCACHE = {}


def run_layer(li, inputs, state):
    if li not in _BCACHE:
        _BCACHE[li] = build_layer(li)
    b = _BCACHE[li]
    in_maps = []
    for c in range(NCORE):
        m = {name: prov(c, inputs, state) for name, prov in b.providers.items()}
        in_maps.append(m)
    res = run_bass_kernel_spmd(b.nc, in_maps, core_ids=list(range(NCORE)))
    return [np.asarray(res.results[c]["XOUT"], np.float32) for c in range(NCORE)]


def kernel(**inputs):
    x = np.asarray(inputs["L0_attn_hidden_state"])[0]
    state = [np.ascontiguousarray(x[c * TLOC:(c + 1) * TLOC].T, dtype=np.float32)
             for c in range(NCORE)]
    for li in range(4):
        state = run_layer(li, inputs, state)
    out = np.concatenate([s.T for s in state], axis=0)
    return out.reshape(1, T, D)

